# version 01
# baseline (speedup 1.0000x reference)
import os
os.environ.setdefault("TRN_ALLOW_ICI_ALL_TO_ALL", "1")

import numpy as np
import ml_dtypes
from contextlib import ExitStack

import concourse.bass as bass
import concourse.mybir as mybir
from concourse.bass_utils import run_bass_kernel_spmd
from concourse.aot_env import allow_ici_all_to_all

F32 = mybir.dt.float32
BF16 = mybir.dt.bfloat16
I32 = mybir.dt.int32
U8 = mybir.dt.uint8
AF = mybir.ActivationFunctionType
ALU = mybir.AluOpType
AX = mybir.AxisListType

NCORE = 8
TCOR = 4096
NT = 32
D = 1024
H = 4096
E = 8
CAPL = 1280
SHARD = 1296
NST = 20
SEQ = 8192
TCH = 512
NTC = 16
BIGF = 1.0e9

_UPTO = 99
_DEBUG = False


class _SemScope:

    def __init__(self, nc, ctx):
        self.nc = nc
        self.handles = []
        ctx.callback(self._close)

    def _close(self):
        if self.handles:
            self.nc.all_engine_barrier()
            self.nc.clear_and_free_semaphores(self.handles)


class Sem:
    def __init__(self, nc, ctx, name):
        if not hasattr(ctx, "_semscope"):
            ctx._semscope = _SemScope(nc, ctx)
        self.h = nc.alloc_semaphore(name=name)
        ctx._semscope.handles.append(self.h)
        self.n = 0

    def inc(self, ins, k=1):
        ins.then_inc(self.h, k)
        self.n += k
        return self.n


def r2(t):
    return t.rearrange("p a b -> p (a b)")


def build_nc():
    nc = bass.Bass("TRN2", target_bir_lowering=False, debug=False,
                 detect_race_conditions=False)
    _th = {}

    class _ApDict:
        def __getitem__(self, name):
            return _th[name].ap()

    ap = _ApDict()

    def din(name, shape, dt):
        _th[name] = nc.dram_tensor(name, shape, dt, kind="ExternalInput")

    def dint(name, shape, dt):
        _th[name] = nc.dram_tensor(name, shape, dt)

    din("x_my", [TCOR, D], F32)
    din("fc1wT", [D, H], F32)
    din("fc2w", [H, D], F32)
    din("b1t", [128, 32], F32)
    din("b2", [1, D], F32)
    din("gwT", [D, E], F32)
    din("winc", [D, D], F32)
    din("woutr", [D, D], F32)
    din("wxp", [512, 96], F32)
    din("wdtc", [64, 512], F32)
    din("bdt", [128, 4], F32)
    din("alog", [128, 4, 16], F32)
    din("dvec", [128, 4], F32)
    din("ck", [128, 8, 4], F32)
    din("bout", [128, 8], F32)
    din("ident", [128, 128], BF16)
    din("utri", [128, 128], BF16)
    din("ltri", [128, 128], BF16)
    din("onesm", [128, 128], BF16)
    din("iotaE", [128, NT, E], F32)
    din("prefv", [8, 1], F32)
    din("ones8", [8, 1], F32)
    din("mam_rows", [128, 64], I32)
    din("xdb_rows", [128, 4], I32)
    din("mo_rows", [128, 16], I32)
    din("m01", [128, 2], F32)
    din("fconst", [128, 8], F32)

    _th["y_out"] = nc.dram_tensor("y_out", [TCOR, D], F32, kind="ExternalOutput")

    dint("lnx_my", [TCOR, D], BF16)
    dint("ldisp", [E * SHARD, D], BF16)
    dint("ddisp", [E * SHARD, D], BF16)
    dint("eo_my", [E * CAPL, D], BF16)
    dint("eo_ret", [E * CAPL, D], BF16)
    dint("totd", [1, 16], F32)
    dint("cnt_all", [8, 16], F32)
    dint("pq_d", [1, 32], F32)
    dint("x1_d", [TCOR, D], F32)
    dint("x2_d", [TCOR, D], F32)
    dint("ln2pair", [2 * TCOR, D], BF16)
    dint("xdbl_pair", [192, SEQ], F32)
    dint("xz_d", [D, SEQ], BF16)
    dint("xhc_d", [512, SEQ], BF16)
    dint("zc_d", [512, SEQ], BF16)
    dint("xdbl_my", [96, SEQ], F32)
    dint("bd_d", [16, SEQ], BF16)
    dint("cd_d", [16, SEQ], BF16)
    dint("op_send", [2 * D, TCOR], BF16)
    dint("op_pag", [4 * D, TCOR], BF16)
    dint("mo_d", [D, TCOR], BF16)
    dint("wbf1_d", [128, 8, H], BF16)
    dint("wbf2_d", [128, 32, D], BF16)
    if _DEBUG:
        for nm, shape, dt in [("dbg_lnx", [TCOR, D], BF16), ("dbg_tot", [1, 16], F32),
                              ("dbg_cnt", [8, 16], F32), ("dbg_pq", [1, 32], F32),
                              ("dbg_ldisp", [E * SHARD, D], BF16),
                              ("dbg_ddisp", [E * SHARD, D], BF16),
                              ("dbg_eo", [E * CAPL, D], BF16),
                              ("dbg_eor", [E * CAPL, D], BF16),
                              ("dbg_rt", [128, 64, 8], F32), ("dbg_lg", [128, NT, 8], F32),
                              ("dbg_oh", [128, NT, 16], F32), ("dbg_pp", [128, NT, 16], F32)]:
            _th[nm] = nc.dram_tensor(nm, shape, dt, kind="ExternalOutput")

    pctx = ExitStack()

    def sbuf(ctx, name, shape, dt):
        return ctx.enter_context(nc.sbuf_tensor(name, shape, dt))

    def pst(ctx, name, shape, dt):
        return ctx.enter_context(nc.psum_tensor(name, shape, dt))

    C = {}
    for nm, shape, dt in [("ident", [128, 128], BF16), ("utri", [128, 128], BF16),
                          ("ltri", [128, 128], BF16), ("onesm", [128, 128], BF16),
                          ("iotaE", [128, NT, E], F32),
                          ("prefv", [8, 1], F32), ("ones8", [8, 1], F32),
                          ("b1t", [128, 32], F32), ("m01", [128, 2], F32),
                          ("bout", [128, 8], F32), ("mam_rows", [128, 64], I32),
                          ("xdb_rows", [128, 4], I32), ("mo_rows", [128, 16], I32),
                          ("bdt", [128, 4], F32), ("alog", [128, 4, 16], F32),
                          ("dvec", [128, 4], F32), ("ck", [128, 8, 4], F32),
                          ("fconst", [128, 8], F32)]:
        C[nm] = sbuf(pctx, "c_" + nm, shape, dt)
    C["b2bc"] = sbuf(pctx, "c_b2bc", [128, D], F32)
    C["gwTb"] = sbuf(pctx, "c_gwTb", [128, 8, 8], BF16)
    C["A"] = sbuf(pctx, "c_A", [128, 4, 16], F32)
    C["bdt2"] = sbuf(pctx, "c_bdt2", [128, 4], F32)

    ld0 = Sem(nc, pctx, "ld0")
    ldg = Sem(nc, pctx, "ldg")
    lds = Sem(nc, pctx, "lds")
    ldz = Sem(nc, pctx, "ldz")
    cm0 = Sem(nc, pctx, "cm0")

    f1ctx = ExitStack()
    w1s = sbuf(f1ctx, "w1s", [128, 8, H], BF16)
    w2s = sbuf(f1ctx, "w2s", [128, 32, D], BF16)

    with ExitStack() as zctx:
        zt = sbuf(zctx, "zt", [128, 10240], BF16)
        with nc.Block() as blk:
            @blk.sync
            def _(s):
                for nm in ["ident", "utri", "ltri", "iotaE", "prefv", "ones8",
                           "b1t", "m01", "bout", "mam_rows", "xdb_rows",
                           "mo_rows", "bdt", "alog", "dvec", "ck", "fconst", "onesm"]:
                    ld0.inc(s.dma_start(out=C[nm][...], in_=ap[nm]), 16)

            @blk.scalar
            def _(sc):
                bap = bass.AP(ap["b2"].tensor, 0, [[0, 128], [1, D]])
                lds.inc(sc.dma_start(out=C["b2bc"][:, :], in_=bap), 16)

            @blk.gpsimd
            def _(g):
                ldg.inc(g.dma_start(out=w1s[:, :, :],
                        in_=ap["fc1wT"].rearrange("(k p) h -> p k h", p=128)), 16)
                ldg.inc(g.dma_start(out=w2s[:, :, :],
                        in_=ap["fc2w"].rearrange("(g p) d -> p g d", p=128)), 16)
                ldg.inc(g.dma_start(out=C["gwTb"][:, :, :],
                        in_=ap["gwT"].rearrange("(k p) e -> p k e", p=128)), 16)

            @blk.vector
            def _(v):
                cm0.inc(v.memset(zt[:, :], 0))

        with nc.Block() as blk:
            mk_ld = ld0.n

            @blk.scalar
            def _(sc):
                sc.wait_ge(ld0.h, mk_ld)
                sc.wait_ge(ldg.h, 48)
                cm0.inc(sc.activation(C["A"][...], C["alog"][...], AF.Exp))

            @blk.vector
            def _(v):
                v.wait_ge(cm0.h, 2)
                cm0.inc(v.tensor_scalar(C["A"][...], C["A"][...], -1.0, None, op0=ALU.mult))
                cm0.inc(v.tensor_scalar(C["bdt2"][...], C["bdt"][...], 2.0, None, op0=ALU.mult))

            @blk.sync
            def _(s):
                s.wait_ge(cm0.h, 1)
                rows = E * SHARD
                r0 = 0
                while r0 < rows:
                    n = min(1280, rows - r0)
                    nb = n // 128
                    ldz.inc(s.dma_start(
                        out=ap["ldisp"][r0:r0 + n, :].rearrange("(a p) d -> p a d", p=128),
                        in_=zt[:, 0:nb * 1024].rearrange("p (a d) -> p a d", d=1024)), 16)
                    r0 += n
                s.wait_ge(ld0.h, mk_ld)

            @blk.gpsimd
            def _(g):
                g.wait_ge(ldg.h, 48)
                ldg.inc(g.dma_start(out=ap["wbf1_d"], in_=w1s[:, :, :]), 16)
                ldg.inc(g.dma_start(out=ap["wbf2_d"], in_=w2s[:, :, :]), 16)

    P0_LD = ld0.n

    def emit_ffn(tag, src_x, dst_x, wa, wb, done, xdep=None):
        fctx = ExitStack()
        logits = sbuf(fctx, tag + "logits", [128, NT, 8], F32)
        if _DEBUG:
            dbg_oh_sb = sbuf(fctx, tag + "dbgoh", [128, NT, 16], F32)
            dbg_pp_sb = sbuf(fctx, tag + "dbgpp", [128, NT, 16], F32)
        geff = sbuf(fctx, tag + "geff", [128, NT, 2], F32)
        sidx = sbuf(fctx, tag + "sidx", [128, 64], I32)
        cidx = sbuf(fctx, tag + "cidx", [128, 64], I32)
        grdy = Sem(nc, fctx, tag + "grdy")
        cc2 = Sem(nc, fctx, tag + "cc2")
        cc3 = Sem(nc, fctx, tag + "cc3")

        with ExitStack() as c1:
            xin = [sbuf(c1, tag + "xin%d" % i, [128, D], F32) for i in range(2)]
            lnb = [sbuf(c1, tag + "lnb%d" % i, [128, D], BF16) for i in range(2)]
            sqs = sbuf(c1, tag + "sqs", [128, D], F32)
            st1 = sbuf(c1, tag + "st1", [128, 8], F32)
            lnxT = sbuf(c1, tag + "lnxT", [128, 8, 128], BF16)
            ptA = pst(c1, tag + "ptA", [128, 1024], BF16)
            ptB = pst(c1, tag + "ptB", [128, 1024], BF16)
            psL = pst(c1, tag + "psL", [128, 8], F32)
            LD = [Sem(nc, c1, tag + "p1ld%d" % i) for i in range(2)]
            ST = [Sem(nc, c1, tag + "p1st%d" % i) for i in range(2)]
            EV = Sem(nc, c1, tag + "p1ev")
            EA = Sem(nc, c1, tag + "p1ea")
            MM = Sem(nc, c1, tag + "p1mm")
            with nc.Block() as blk:
                @blk.sync
                def _(s):
                    s.wait_ge(ld0.h, P0_LD)
                    s.wait_ge(ldg.h, 80)
                    s.wait_ge(lds.h, 16)
                    s.wait_ge(ldz.h, 144)
                    if xdep is not None:
                        s.wait_ge(xdep[0].h, xdep[1])
                    for t in range(NT):
                        b = t % 2
                        if t >= 2:
                            s.wait_ge(EV.h, 4 * (t - 2) + 2)
                        LD[b].inc(s.dma_start(out=xin[b][:, :],
                                              in_=src_x[128 * t:128 * t + 128, :]), 16)
                        if t >= 1:
                            s.wait_ge(EV.h, 4 * (t - 1) + 2)
                            ST[(t - 1) % 2].inc(s.dma_start(
                                out=ap["lnx_my"][128 * (t - 1):128 * t, :],
                                in_=lnb[(t - 1) % 2][:, :]), 16)
                    s.wait_ge(EV.h, 4 * (NT - 1) + 2)
                    ST[(NT - 1) % 2].inc(s.dma_start(
                        out=ap["lnx_my"][128 * (NT - 1):128 * NT, :],
                        in_=lnb[(NT - 1) % 2][:, :]), 16)
                    s.wait_ge(ST[0].h, 16 * (NT // 2))
                    s.wait_ge(ST[1].h, 16 * (NT // 2))

                @blk.vector
                def _(v):
                    for t in range(NT):
                        b = t % 2
                        v.wait_ge(LD[b].h, 16 * (t // 2 + 1))
                        v.tensor_reduce(st1[:, 4 * b:4 * b + 1], xin[b][:, :], AX.X, ALU.add)
                        v.drain()
                        v.wait_ge(EA.h, 2 * t + 1)
                        v.tensor_scalar(st1[:, 4 * b:4 * b + 1], st1[:, 4 * b:4 * b + 1],
                                        1.0 / D, None, op0=ALU.mult)
                        v.drain()
                        v.tensor_tensor(st1[:, 4 * b + 2:4 * b + 3], st1[:, 4 * b:4 * b + 1],
                                        st1[:, 4 * b:4 * b + 1], ALU.mult)
                        v.drain()
                        EV.inc(v.scalar_tensor_tensor(st1[:, 4 * b + 1:4 * b + 2],
                                                      st1[:, 4 * b + 1:4 * b + 2], 1.0 / D,
                                                      st1[:, 4 * b + 2:4 * b + 3],
                                                      op0=ALU.mult, op1=ALU.subtract))
                        v.wait_ge(EA.h, 2 * t + 2)
                        if t >= 2:
                            v.wait_ge(MM.h, 2 * (t - 2) + 1)
                            v.wait_ge(ST[b].h, 16 * ((t - 2) // 2 + 1))
                        EV.inc(v.tensor_scalar(lnb[b][:, :], xin[b][:, :],
                                               st1[:, 4 * b:4 * b + 1],
                                               st1[:, 4 * b + 1:4 * b + 2],
                                               op0=ALU.subtract, op1=ALU.mult))
                        v.wait_ge(MM.h, 2 * t + 1)
                        EV.inc(v.tensor_copy(lnxT[:, :, :],
                                             (ptA if b == 0 else ptB)[:, :].rearrange(
                                                 "p (k q) -> p k q", q=128)))
                        v.wait_ge(MM.h, 2 * t + 2)
                        EV.inc(v.tensor_copy(logits[:, t, :], psL[:, :]))

                @blk.scalar
                def _(sc):
                    for t in range(NT):
                        b = t % 2
                        sc.wait_ge(LD[b].h, 16 * (t // 2 + 1))
                        if t >= 2:
                            sc.wait_ge(EV.h, 4 * (t - 2) + 2)
                        EA.inc(sc.activation(sqs[:, :], xin[b][:, :], AF.Square,
                                             accum_out=st1[:, 4 * b + 1:4 * b + 2]))
                        sc.wait_ge(EV.h, 4 * t + 1)
                        sc.activation(st1[:, 4 * b + 1:4 * b + 2],
                                      st1[:, 4 * b + 1:4 * b + 2], AF.Ln,
                                      bias=C["fconst"][:, 0:1])
                        sc.drain()
                        EA.inc(sc.activation(st1[:, 4 * b + 1:4 * b + 2],
                                             st1[:, 4 * b + 1:4 * b + 2], AF.Exp,
                                             scale=C["fconst"][:, 1:2]))

                @blk.tensor
                def _(t_):
                    for t in range(NT):
                        b = t % 2
                        pt = ptA if b == 0 else ptB
                        t_.wait_ge(EV.h, 4 * t + 2)
                        if t >= 2:
                            t_.wait_ge(EV.h, 4 * (t - 2) + 3)
                        for k in range(8):
                            ins = t_.transpose(pt[:, 128 * k:128 * k + 128],
                                               lnb[b][:, 128 * k:128 * k + 128],
                                               C["ident"][:, :])
                        MM.inc(ins)
                        t_.wait_ge(EV.h, 4 * t + 3)
                        for k in range(8):
                            ins = t_.matmul(psL[:, :], lnxT[:, k, :], C["gwTb"][:, k, :],
                                            start=(k == 0), stop=(k == 7))
                        MM.inc(ins)

        with ExitStack() as c2:
            ohb = sbuf(c2, tag + "ohb", [128, NT, 16], BF16)
            locP = sbuf(c2, tag + "locP", [128, NT, 2], F32)
            g01 = sbuf(c2, tag + "g01", [128, NT, 2], F32)
            scr = sbuf(c2, tag + "scr", [128, NT, 8], F32)
            scr2 = sbuf(c2, tag + "scr2", [128, NT, 8], F32)
            msk = sbuf(c2, tag + "msk", [128, NT, 8], U8)
            sm1 = sbuf(c2, tag + "sm1", [128, NT, 1], F32)
            sm2 = sbuf(c2, tag + "sm2", [128, NT, 1], F32)
            si1 = sbuf(c2, tag + "si1", [128, NT, 1], F32)
            si2 = sbuf(c2, tag + "si2", [128, NT, 1], F32)
            Pp = sbuf(c2, tag + "Pp", [128, NT, 16], F32)
            Pf = sbuf(c2, tag + "Pf", [128, NT, 16], F32)
            Tot2 = sbuf(c2, tag + "Tot2", [128, 16, NT], F32)
            Incl = sbuf(c2, tag + "Incl", [128, 16, NT], F32)
            onv = sbuf(c2, tag + "onv", [128, NT], F32)
            tot = sbuf(c2, tag + "tot", [1, 16], F32)
            totbc = sbuf(c2, tag + "totbc", [128, 16], F32)
            tmp1 = sbuf(c2, tag + "tmp1", [128, NT, 1], F32)
            cnt_sb = sbuf(c2, tag + "cnt", [8, 16], F32)
            prefbc = sbuf(c2, tag + "prefbc", [128, 16], F32)
            totA = sbuf(c2, tag + "totA", [128, 16], F32)
            pq = sbuf(c2, tag + "pq", [1, 32], F32)
            psP = pst(c2, tag + "psP", [128, 16], F32)
            psT = pst(c2, tag + "psT", [128, 16], F32)
            EVr = Sem(nc, c2, tag + "p2ev")
            MMr = Sem(nc, c2, tag + "p2mm")
            STr = Sem(nc, c2, tag + "p2st")
            LDr = Sem(nc, c2, tag + "p2ldr")
            EAr = Sem(nc, c2, tag + "p2ea")
            cc1 = Sem(nc, c2, tag + "cc1")
            GID = [Sem(nc, c2, tag + "gid%d" % i) for i in range(2)]
            LDD = [Sem(nc, c2, tag + "ldd%d" % i) for i in range(2)]
            LD4 = Sem(nc, c2, tag + "ld4")

            def D_(v):
                v.drain()

            with nc.Block() as blk:
                @blk.vector
                def _(v):
                    io = C["iotaE"]
                    L = logits
                    v.tensor_reduce(sm1[...], L[...], AX.X, ALU.max); D_(v)
                    v.tensor_tensor(msk[...], L[...], sm1[...].to_broadcast([128, NT, 8]), ALU.is_equal); D_(v)
                    v.memset(scr[...], BIGF); D_(v)
                    v.copy_predicated(scr[...], msk[...], io[...]); D_(v)
                    v.tensor_reduce(si1[...], scr[...], AX.X, ALU.min); D_(v)
                    v.tensor_tensor(msk[...], io[...], si1[...].to_broadcast([128, NT, 8]), ALU.is_equal); D_(v)
                    v.tensor_copy(ohb[:, :, 0:8], msk[...]); D_(v)
                    v.scalar_tensor_tensor(scr2[...], ohb[:, :, 0:8], -BIGF, L[...],
                                           op0=ALU.mult, op1=ALU.add); D_(v)
                    v.tensor_reduce(sm2[...], scr2[...], AX.X, ALU.max); D_(v)
                    v.tensor_tensor(msk[...], scr2[...], sm2[...].to_broadcast([128, NT, 8]), ALU.is_equal); D_(v)
                    v.memset(scr[...], BIGF); D_(v)
                    v.copy_predicated(scr[...], msk[...], io[...]); D_(v)
                    v.tensor_reduce(si2[...], scr[...], AX.X, ALU.min); D_(v)
                    v.tensor_tensor(msk[...], io[...], si2[...].to_broadcast([128, NT, 8]), ALU.is_equal); D_(v)
                    EV0 = EVr.inc(v.tensor_copy(ohb[:, :, 8:16], msk[...]))
                    v.drain()
                    v.tensor_tensor(sm2[...], sm2[...], sm1[...], ALU.subtract); D_(v)
                    EVr.inc(v.tensor_copy(sm2[...], sm2[...]))
                    v.memset(onv[:, :], 1.0)
                    v.drain()
                    for t in range(NT):
                        v.wait_ge(MMr.h, 2 * t + 1)
                        EVr.inc(v.tensor_copy(Pp[:, t, :], psP[:, :]))
                        v.wait_ge(MMr.h, 2 * t + 2)
                        EVr.inc(v.tensor_copy(Tot2[:, :, t], psT[:, :]))
                    v.drain()
                    for e8 in range(16):
                        v.tensor_tensor_scan(Incl[:, e8, :], onv[:, :], Tot2[:, e8, :],
                                             0.0, ALU.mult, ALU.add)
                    v.drain()
                    v.tensor_tensor(Incl[:, :, :], Incl[:, :, :], Tot2[:, :, :], ALU.subtract)
                    v.drain()
                    v.tensor_tensor(Pf[:, :, :], Pp[:, :, :],
                                    Incl[:, :, :].rearrange("p e t -> p t e"), ALU.add)
                    v.drain()
                    EVr.inc(v.tensor_tensor(tot[0:1, :], r2(Incl[0:1, :, NT - 1:NT]),
                                             r2(Tot2[0:1, :, NT - 1:NT]), ALU.add))
                    v.drain()
                    v.tensor_tensor(scr[...], ohb[:, :, 0:8], Pf[:, :, 0:8], ALU.mult); D_(v)
                    v.tensor_reduce(locP[:, :, 0:1], scr[...], AX.X, ALU.add); D_(v)
                    v.tensor_tensor(scr[...], ohb[:, :, 8:16], Pf[:, :, 8:16], ALU.mult); D_(v)
                    v.tensor_reduce(locP[:, :, 1:2], scr[...], AX.X, ALU.add); D_(v)
                    v.wait_ge(EAr.h, 1)
                    v.tensor_scalar(sm1[...], sm2[...], 1.0, None, op0=ALU.add); D_(v)
                    v.reciprocal(sm1[...], sm1[...]); D_(v)
                    v.tensor_scalar(g01[:, :, 0:1], sm1[...], 0.5, None, op0=ALU.mult); D_(v)
                    v.tensor_tensor(tmp1[...], sm2[...], sm1[...], ALU.mult); D_(v)
                    v.tensor_scalar(g01[:, :, 1:2], tmp1[...], 0.5, None, op0=ALU.mult); D_(v)
                    v.tensor_tensor(totbc[:, :], r2(Incl[:, :, NT - 1:NT]),
                                    r2(Tot2[:, :, NT - 1:NT]), ALU.add)
                    v.drain()
                    v.tensor_tensor(scr[...], ohb[:, :, 8:16],
                                    totbc[:, 0:8].rearrange("p (a e) -> p a e", a=1).to_broadcast([128, NT, 8]),
                                    ALU.mult); D_(v)
                    v.tensor_reduce(tmp1[...], scr[...], AX.X, ALU.add); D_(v)
                    v.tensor_tensor(locP[:, :, 1:2], locP[:, :, 1:2], tmp1[...], ALU.add); D_(v)
                    v.tensor_scalar(sm1[...], locP[:, :, 0:1], -1.0, None, op0=ALU.add); D_(v)
                    v.tensor_scalar(sm2[...], locP[:, :, 1:2], -1.0, None, op0=ALU.add); D_(v)
                    v.tensor_scalar(scr[:, :, 0:1], sm1[...], float(SHARD - 8), None, op0=ALU.min); D_(v)
                    v.tensor_scalar(scr[:, :, 1:2], sm2[...], float(SHARD - 8), None, op0=ALU.min); D_(v)
                    v.scalar_tensor_tensor(scr[:, :, 2:3], si1[...], float(SHARD),
                                           scr[:, :, 0:1], op0=ALU.mult, op1=ALU.add); D_(v)
                    v.scalar_tensor_tensor(scr[:, :, 3:4], si2[...], float(SHARD),
                                           scr[:, :, 1:2], op0=ALU.mult, op1=ALU.add); D_(v)
                    v.tensor_copy(sidx[:, 0:32], r2(scr[:, :, 2:3])); D_(v)
                    v.tensor_copy(sidx[:, 32:64], r2(scr[:, :, 3:4])); D_(v)
                    v.tensor_scalar(scr[:, :, 0:1], sm1[...], float(CAPL - 1), None, op0=ALU.min); D_(v)
                    v.tensor_scalar(scr[:, :, 1:2], sm2[...], float(CAPL - 1), None, op0=ALU.min); D_(v)
                    v.scalar_tensor_tensor(scr[:, :, 2:3], si1[...], float(CAPL),
                                           scr[:, :, 0:1], op0=ALU.mult, op1=ALU.add); D_(v)
                    v.scalar_tensor_tensor(scr[:, :, 3:4], si2[...], float(CAPL),
                                           scr[:, :, 1:2], op0=ALU.mult, op1=ALU.add); D_(v)
                    v.tensor_copy(cidx[:, 0:32], r2(scr[:, :, 2:3])); D_(v)
                    if _DEBUG:
                        v.tensor_copy(dbg_oh_sb[:, :, :], ohb[:, :, :]); D_(v)
                        v.tensor_copy(dbg_pp_sb[:, :, :], Pp[:, :, :]); D_(v)
                    EVr.inc(v.tensor_copy(cidx[:, 32:64], r2(scr[:, :, 3:4])))

                @blk.scalar
                def _(sc):
                    sc.wait_ge(EVr.h, 2)
                    EAr.inc(sc.activation(sm2[...], sm2[...], AF.Exp))

                @blk.tensor
                def _(t_):
                    t_.wait_ge(EVr.h, 1)
                    for t in range(NT):
                        if t > 0:
                            t_.wait_ge(EVr.h, 2 + 2 * (t - 1) + 1)
                        t_.matmul(psP[:, 0:8], C["utri"][:, :], ohb[:, t, 0:8],
                                  start=True, stop=True)
                        ins = t_.matmul(psP[:, 8:16], C["utri"][:, :], ohb[:, t, 8:16],
                                        start=True, stop=True)
                        MMr.inc(ins)
                        if t > 0:
                            t_.wait_ge(EVr.h, 2 + 2 * (t - 1) + 2)
                        t_.matmul(psT[:, 0:8], C["onesm"][:, :], ohb[:, t, 0:8],
                                  start=True, stop=True)
                        ins = t_.matmul(psT[:, 8:16], C["onesm"][:, :], ohb[:, t, 8:16],
                                        start=True, stop=True)
                        MMr.inc(ins)

                @blk.sync
                def _(s):
                    s.wait_ge(EVr.h, 2 + 2 * NT + 1)
                    STr.inc(s.dma_start(out=ap["totd"], in_=tot[:, :]), 16)

            EVr_done = EVr.n
            with ExitStack() as c3:
                lbuf = [sbuf(c3, tag + "lb%d" % i, [128, D], BF16) for i in range(2)]
                with nc.Block() as blk:
                    @blk.gpsimd
                    def _(g):
                        g.wait_ge(ld0.h, P0_LD)
                        g.wait_ge(ldg.h, 80)
                        g.wait_ge(lds.h, 16)
                        g.wait_ge(ldz.h, 144)
                        g.wait_ge(STr.h, 16)
                        cc1.inc(g.collective_compute(
                            "AllGather", ALU.bypass,
                            replica_groups=[list(range(NCORE))],
                            ins=[ap["totd"]], outs=[ap["cnt_all"]]))
                        g.wait_ge(EVr.h, EVr_done)
                        rb = g.to_reg(E * SHARD - 1)
                        for t in range(NT):
                            b = t % 2
                            g.wait_ge(LDD[b].h, 16 * (t // 2 + 1))
                            GID[b].inc(g.indirect_dma_start(
                                out=ap["ldisp"],
                                out_offset=bass.IndirectOffsetOnAxis(sidx[:, t:t + 1], 0),
                                in_=lbuf[b][:, :], in_offset=None,
                                bounds_check=rb, oob_is_err=False), 16)
                            GID[b].inc(g.indirect_dma_start(
                                out=ap["ldisp"],
                                out_offset=bass.IndirectOffsetOnAxis(sidx[:, 32 + t:33 + t], 0),
                                in_=lbuf[b][:, :], in_offset=None,
                                bounds_check=rb, oob_is_err=False), 16)

                    @blk.sync
                    def _(s):
                        for t in range(NT):
                            b = t % 2
                            if t >= 2:
                                s.wait_ge(GID[b].h, 32 * (t // 2))
                            LDD[b].inc(s.dma_start(out=lbuf[b][:, :],
                                                   in_=ap["lnx_my"][128 * t:128 * t + 128, :]), 16)

                with nc.Block() as blk:
                    @blk.gpsimd
                    def _(g):
                        g.wait_ge(GID[0].h, 32 * 16)
                        g.wait_ge(GID[1].h, 32 * 16)
                        with allow_ici_all_to_all():
                            cc2.inc(g.collective_compute(
                                "AllToAll", ALU.bypass,
                                replica_groups=[list(range(NCORE))],
                                ins=[ap["ldisp"]], outs=[ap["ddisp"]]))

                    @blk.sync
                    def _(s):
                        s.wait_ge(cc1.h, 1)
                        LD4.inc(s.dma_start(out=cnt_sb[:, :], in_=ap["cnt_all"]), 16)

            with ExitStack() as c5:
                psq = pst(c5, tag + "psq", [1, 16], F32)
                psq2 = pst(c5, tag + "psq2", [1, 16], F32)
                MM5 = Sem(nc, c5, tag + "mm5")
                EV5 = Sem(nc, c5, tag + "ev5")
                ST5 = Sem(nc, c5, tag + "st5")
                LD5 = Sem(nc, c5, tag + "ld5")
                with nc.Block() as blk:
                    @blk.tensor
                    def _(t_):
                        t_.wait_ge(LD4.h, 16)
                        t_.matmul(psq[:, :], C["prefv"][:, :], cnt_sb[:, :],
                                  start=True, stop=True)
                        MM5.inc(t_.matmul(psq2[:, :], C["ones8"][:, :], cnt_sb[:, :],
                                          start=True, stop=True))

                    @blk.vector
                    def _(v):
                        v.wait_ge(MM5.h, 1)
                        v.tensor_copy(pq[0:1, 0:16], psq[:, :]); D_(v)
                        EV5.inc(v.tensor_copy(pq[0:1, 16:32], psq2[:, :]))
                        v.wait_ge(LD5.h, 32)
                        v.tensor_tensor(scr[...], ohb[:, :, 0:8],
                                        prefbc[:, 0:8].rearrange("p (a e) -> p a e", a=1).to_broadcast([128, NT, 8]),
                                        ALU.mult); D_(v)
                        v.tensor_reduce(tmp1[...], scr[...], AX.X, ALU.add); D_(v)
                        v.tensor_tensor(tmp1[...], tmp1[...], locP[:, :, 0:1], ALU.add); D_(v)
                        v.tensor_scalar(tmp1[...], tmp1[...], -1.0, None, op0=ALU.add); D_(v)
                        v.tensor_scalar(msk[:, :, 0:1], tmp1[...], 8192.0, None, op0=ALU.is_lt); D_(v)
                        v.tensor_copy(tmp1[...], msk[:, :, 0:1]); D_(v)
                        v.tensor_tensor(geff[:, :, 0:1], g01[:, :, 0:1], tmp1[...], ALU.mult); D_(v)
                        v.tensor_tensor(scr[...], ohb[:, :, 8:16],
                                        prefbc[:, 8:16].rearrange("p (a e) -> p a e", a=1).to_broadcast([128, NT, 8]),
                                        ALU.mult); D_(v)
                        v.tensor_reduce(tmp1[...], scr[...], AX.X, ALU.add); D_(v)
                        v.tensor_tensor(scr[...], ohb[:, :, 8:16],
                                        totA[:, 0:8].rearrange("p (a e) -> p a e", a=1).to_broadcast([128, NT, 8]),
                                        ALU.mult); D_(v)
                        v.tensor_reduce(sm1[...], scr[...], AX.X, ALU.add); D_(v)
                        v.tensor_tensor(tmp1[...], tmp1[...], sm1[...], ALU.add); D_(v)
                        v.tensor_tensor(scr[...], ohb[:, :, 8:16],
                                        totbc[:, 0:8].rearrange("p (a e) -> p a e", a=1).to_broadcast([128, NT, 8]),
                                        ALU.mult); D_(v)
                        v.tensor_reduce(sm1[...], scr[...], AX.X, ALU.add); D_(v)
                        v.tensor_tensor(tmp1[...], tmp1[...], sm1[...], ALU.subtract); D_(v)
                        v.tensor_tensor(tmp1[...], tmp1[...], locP[:, :, 1:2], ALU.add); D_(v)
                        v.tensor_scalar(tmp1[...], tmp1[...], -1.0, None, op0=ALU.add); D_(v)
                        v.tensor_scalar(msk[:, :, 0:1], tmp1[...], 8192.0, None, op0=ALU.is_lt); D_(v)
                        v.tensor_copy(tmp1[...], msk[:, :, 0:1]); D_(v)
                        v.tensor_tensor(geff[:, :, 1:2], g01[:, :, 1:2], tmp1[...], ALU.mult)
                        v.drain()
                        EV5.inc(v.tensor_copy(geff[:, 0:1, 0:1], geff[:, 0:1, 0:1]))
                        grdy.inc(v.tensor_copy(sm1[:, 0:1, 0:1], sm1[:, 0:1, 0:1]))

                    @blk.sync
                    def _(s):
                        s.wait_ge(EV5.h, 1)
                        ST5.inc(s.dma_start(out=ap["pq_d"], in_=pq[:, :]), 16)

                    @blk.scalar
                    def _(sc):
                        sc.wait_ge(ST5.h, 16)
                        b0 = bass.AP(_th["pq_d"].ap().tensor, 0, [[0, 128], [1, 16]])
                        b1 = bass.AP(_th["pq_d"].ap().tensor, 16, [[0, 128], [1, 16]])
                        LD5.inc(sc.dma_start(out=prefbc[:, :], in_=b0), 16)
                        LD5.inc(sc.dma_start(out=totA[:, :], in_=b1), 16)

        if _DEBUG and tag == "f1":
            dbs = Sem(nc, fctx, tag + "dbs")
            with nc.Block() as blk:
                @blk.sync
                def _(s):
                    dbs.inc(s.dma_start(out=ap["dbg_rt"][:, 0:32, 0:1].rearrange("p a b -> p (a b)"), in_=r2(geff[:, :, 0:1]), allow_slow_non_contiguous=True), 16)
                    dbs.inc(s.dma_start(out=ap["dbg_rt"][:, 0:32, 1:2].rearrange("p a b -> p (a b)"), in_=r2(geff[:, :, 1:2]), allow_slow_non_contiguous=True), 16)
                    dbs.inc(s.dma_start(out=ap["dbg_rt"][:, 0:64, 2:3].rearrange("p a b -> p (a b)"), in_=sidx[:, :].bitcast(F32), allow_slow_non_contiguous=True), 16)
                    dbs.inc(s.dma_start(out=ap["dbg_rt"][:, 0:64, 3:4].rearrange("p a b -> p (a b)"), in_=cidx[:, :].bitcast(F32), allow_slow_non_contiguous=True), 16)
                    dbs.inc(s.dma_start(out=ap["dbg_rt"][:, 0:32, 4:5].rearrange("p a b -> p (a b)"), in_=r2(logits[:, :, 0:1]), allow_slow_non_contiguous=True), 16)
                    dbs.inc(s.dma_start(out=ap["dbg_lg"], in_=logits[:, :, :]), 16)
                    dbs.inc(s.dma_start(out=ap["dbg_oh"], in_=dbg_oh_sb[:, :, :]), 16)
                    dbs.inc(s.dma_start(out=ap["dbg_pp"], in_=dbg_pp_sb[:, :, :]), 16)
                    s.wait_ge(dbs.h, dbs.n)
        emit_expert(tag, wa, wb, cc2)
        with nc.Block() as blk:
            @blk.gpsimd
            def _(g):
                with allow_ici_all_to_all():
                    cc3.inc(g.collective_compute(
                        "AllToAll", ALU.bypass,
                        replica_groups=[list(range(NCORE))],
                        ins=[ap["eo_my"]], outs=[ap["eo_ret"]]))
        emit_combine(tag, src_x, dst_x, cidx, geff, cc3, grdy, 1, done)
        fctx.close()

    def emit_expert(tag, wa, wb, ccdep):
        with ExitStack() as ce:
            dbuf = [[sbuf(ce, tag + "db%d_%d" % (i, j), [128, D], BF16)
                     for j in range(4)] for i in range(2)]
            xT = sbuf(ce, tag + "xT", [128, 8, 512], BF16)
            hT = sbuf(ce, tag + "hT", [128, 32, 512], BF16)
            obuf = sbuf(ce, tag + "obuf", [128, 4, D], BF16)
            ptT = [pst(ce, tag + "ptT%d" % i, [128, 1024], BF16) for i in range(2)]
            ph = [pst(ce, tag + "ph%d" % i, [128, 512], F32) for i in range(2)]
            po = [pst(ce, tag + "po%d" % i, [128, 512], F32) for i in range(2)]
            LDE = [[Sem(nc, ce, tag + "lde%d_%d" % (i, j)) for j in range(4)]
                   for i in range(2)]
            STE = [Sem(nc, ce, tag + "ste%d" % r) for r in range(4)]
            MME = Sem(nc, ce, tag + "mme")
            EVE = Sem(nc, ce, tag + "eve")
            EAE = Sem(nc, ce, tag + "eae")

            def dd_row(t):
                return (t // 10) * SHARD + (t % 10) * 128

            with nc.Block() as blk:
                @blk.sync
                def _(s):
                    s.wait_ge(ccdep.h, 1)
                    for st in range(NST):
                        i = st % 2
                        for j in range(4):
                            t = 4 * st + j
                            if st >= 2:
                                s.wait_ge(MME.h, 44 * (st - 2) + j + 1)
                            r = dd_row(t)
                            LDE[i][j].inc(s.dma_start(out=dbuf[i][j][:, :],
                                                      in_=ap["ddisp"][r:r + 128, :]), 16)

                @blk.tensor
                def _(t_):
                    for st in range(NST):
                        i = st % 2
                        for j in range(4):
                            t_.wait_ge(LDE[i][j].h, 16 * (st // 2 + 1))
                            if 4 * st + j >= 2:
                                prev = 4 * st + j - 2
                                t_.wait_ge(EVE.h, 12 * (prev // 4) + (prev % 4) + 1)
                            pt = ptT[j % 2]
                            for k in range(8):
                                ins = t_.transpose(pt[:, 128 * k:128 * k + 128],
                                                   dbuf[i][j][:, 128 * k:128 * k + 128],
                                                   C["ident"][:, :])
                            MME.inc(ins)
                        t_.wait_ge(EVE.h, 12 * st + 4)
                        for hg in range(32):
                            gi = 32 * st + hg
                            if gi >= 2:
                                t_.wait_ge(EAE.h, gi - 1)
                            p = ph[hg % 2]
                            for k in range(8):
                                ins = t_.matmul(p[:, :], wa[:, k, 128 * hg:128 * hg + 128],
                                                xT[:, k, :], start=(k == 0), stop=(k == 7))
                            MME.inc(ins)
                        t_.wait_ge(EAE.h, 32 * (st + 1))
                        for rs in range(4):
                            for dh in range(2):
                                m = 8 * st + rs * 2 + dh
                                if m >= 2:
                                    pm = m - 2
                                    t_.wait_ge(EVE.h, 12 * (pm // 8) + 4 + (pm % 8) + 1)
                                p = po[dh]
                                for hg in range(32):
                                    ins = t_.matmul(p[:, :], hT[:, hg, 128 * rs:128 * rs + 128],
                                                    wb[:, hg, 512 * dh:512 * dh + 512],
                                                    start=(hg == 0), stop=(hg == 31))
                                MME.inc(ins)

                @blk.vector
                def _(v):
                    for st in range(NST):
                        for j in range(4):
                            v.wait_ge(MME.h, 44 * st + j + 1)
                            pt = ptT[j % 2]
                            EVE.inc(v.tensor_copy(xT[:, :, 128 * j:128 * j + 128],
                                                  pt[:, :].rearrange("p (k q) -> p k q", q=128)))
                        for rs in range(4):
                            for dh in range(2):
                                m = rs * 2 + dh
                                v.wait_ge(MME.h, 44 * st + 36 + m + 1)
                                if st >= 1:
                                    v.wait_ge(STE[rs].h, 16 * st)
                                EVE.inc(v.tensor_tensor(obuf[:, rs, 512 * dh:512 * dh + 512],
                                                        po[dh][:, :],
                                                        C["b2bc"][:, 512 * dh:512 * dh + 512],
                                                        ALU.add))

                @blk.scalar
                def _(sc):
                    for st in range(NST):
                        for hg in range(32):
                            sc.wait_ge(MME.h, 44 * st + 4 + hg + 1)
                            EAE.inc(sc.activation(hT[:, hg, :], ph[hg % 2][:, :], AF.Relu,
                                                  bias=C["b1t"][:, hg:hg + 1]))

                @blk.gpsimd
                def _(g):
                    for st in range(NST):
                        for rs in range(4):
                            g.wait_ge(EVE.h, 12 * st + 4 + rs * 2 + 2)
                            STE[rs].inc(g.dma_start(
                                out=ap["eo_my"][512 * st + 128 * rs:512 * st + 128 * rs + 128, :],
                                in_=obuf[:, rs, :]), 16)

            with nc.Block() as blk:
                @blk.gpsimd
                def _(g):
                    for rs in range(4):
                        g.wait_ge(STE[rs].h, 16 * NST)

    def emit_combine(tag, src_x, dst_x, cidx, geff, ccdep, gsem, gval, done):
        with ExitStack() as cc_:
            ga = [[sbuf(cc_, tag + "ga%d_%d" % (i, j), [128, D], BF16)
                   for j in range(2)] for i in range(2)]
            xc = [sbuf(cc_, tag + "xc%d" % i, [128, D], F32) for i in range(2)]
            xo = [sbuf(cc_, tag + "xo%d" % i, [128, D], F32) for i in range(2)]
            tmpf = [sbuf(cc_, tag + "ctmp%d" % i, [128, D], F32) for i in range(2)]
            mk1 = sbuf(cc_, tag + "cmk", [1, 2], F32)
            GIC = [Sem(nc, cc_, tag + "gic%d" % i) for i in range(2)]
            LDC = [Sem(nc, cc_, tag + "ldc%d" % i) for i in range(2)]
            EVC = Sem(nc, cc_, tag + "evc")
            STC = [Sem(nc, cc_, tag + "stc%d" % i) for i in range(2)]
            with nc.Block() as blk:
                @blk.gpsimd
                def _(g):
                    g.wait_ge(ccdep.h, 1)
                    rb = g.to_reg(E * CAPL - 1)
                    for t in range(NT):
                        b = t % 2
                        if t >= 2:
                            g.wait_ge(EVC.h, 3 * (t - 2) + 2)
                        GIC[b].inc(g.indirect_dma_start(
                            out=ga[b][0][:, :], out_offset=None,
                            in_=ap["eo_ret"],
                            in_offset=bass.IndirectOffsetOnAxis(cidx[:, t:t + 1], 0),
                            bounds_check=rb, oob_is_err=False), 16)
                        GIC[b].inc(g.indirect_dma_start(
                            out=ga[b][1][:, :], out_offset=None,
                            in_=ap["eo_ret"],
                            in_offset=bass.IndirectOffsetOnAxis(cidx[:, 32 + t:33 + t], 0),
                            bounds_check=rb, oob_is_err=False), 16)

                @blk.sync
                def _(s):
                    for t in range(NT):
                        b = t % 2
                        if t >= 2:
                            s.wait_ge(EVC.h, 3 * (t - 2) + 3)
                        LDC[b].inc(s.dma_start(out=xc[b][:, :],
                                               in_=src_x[128 * t:128 * t + 128, :]), 16)

                @blk.vector
                def _(v):
                    v.wait_ge(gsem.h, gval)
                    for t in range(NT):
                        b = t % 2
                        v.wait_ge(GIC[b].h, 32 * (t // 2 + 1))
                        v.wait_ge(LDC[b].h, 16 * (t // 2 + 1))
                        if t >= 2:
                            v.wait_ge(STC[b].h, 16 * ((t - 2) // 2 + 1))
                        v.tensor_scalar(tmpf[b][:, :], ga[b][0][:, :],
                                        r2(geff[:, t:t + 1, 0:1]), None, op0=ALU.mult)
                        v.drain()
                        EVC.inc(v.scalar_tensor_tensor(tmpf[b][:, :], ga[b][1][:, :],
                                                       r2(geff[:, t:t + 1, 1:2]), tmpf[b][:, :],
                                                       op0=ALU.mult, op1=ALU.add))
                        v.drain()
                        EVC.inc(v.tensor_copy(mk1[0:1, 0:1], xc[b][0:1, 0:1]))
                        EVC.inc(v.tensor_tensor(xo[b][:, :], xc[b][:, :], tmpf[b][:, :], ALU.add))

                @blk.scalar
                def _(sc):
                    for t in range(NT):
                        b = t % 2
                        sc.wait_ge(EVC.h, 3 * t + 3)
                        STC[b].inc(sc.dma_start(out=dst_x[128 * t:128 * t + 128, :],
                                                in_=xo[b][:, :]), 16)
                    sc.wait_ge(STC[0].h, 16 * (NT // 2))
                    sc.wait_ge(STC[1].h, 16 * (NT // 2))
                    done.inc(sc.activation(tmpf[0][0:1, 0:1], tmpf[0][0:1, 0:1], AF.Copy))

    def emit_ln2(tag, src_x, dst_ln, dep, depval, done):
        with ExitStack() as cl:
            xin = [sbuf(cl, tag + "xi%d" % i, [128, D], F32) for i in range(2)]
            lnb = [sbuf(cl, tag + "lo%d" % i, [128, D], BF16) for i in range(2)]
            sqs = sbuf(cl, tag + "sq", [128, D], F32)
            st1 = sbuf(cl, tag + "st", [128, 8], F32)
            LD = [Sem(nc, cl, tag + "ld%d" % i) for i in range(2)]
            ST = [Sem(nc, cl, tag + "st%d" % i) for i in range(2)]
            EV = Sem(nc, cl, tag + "ev")
            EA = Sem(nc, cl, tag + "ea")
            with nc.Block() as blk:
                @blk.sync
                def _(s):
                    s.wait_ge(dep.h, depval)
                    for t in range(NT):
                        b = t % 2
                        if t >= 2:
                            s.wait_ge(EV.h, 2 * (t - 2) + 2)
                        LD[b].inc(s.dma_start(out=xin[b][:, :],
                                              in_=src_x[128 * t:128 * t + 128, :]), 16)
                        if t >= 1:
                            s.wait_ge(EV.h, 2 * (t - 1) + 2)
                            ST[(t - 1) % 2].inc(s.dma_start(
                                out=dst_ln[128 * (t - 1):128 * t, :],
                                in_=lnb[(t - 1) % 2][:, :]), 16)
                    s.wait_ge(EV.h, 2 * (NT - 1) + 2)
                    ST[(NT - 1) % 2].inc(s.dma_start(
                        out=dst_ln[128 * (NT - 1):128 * NT, :],
                        in_=lnb[(NT - 1) % 2][:, :]), 16)

                @blk.vector
                def _(v):
                    for t in range(NT):
                        b = t % 2
                        v.wait_ge(LD[b].h, 16 * (t // 2 + 1))
                        v.tensor_reduce(st1[:, 4 * b:4 * b + 1], xin[b][:, :], AX.X, ALU.add)
                        v.drain()
                        v.wait_ge(EA.h, 2 * t + 1)
                        v.tensor_scalar(st1[:, 4 * b:4 * b + 1], st1[:, 4 * b:4 * b + 1],
                                        1.0 / D, None, op0=ALU.mult)
                        v.drain()
                        v.tensor_tensor(st1[:, 4 * b + 2:4 * b + 3], st1[:, 4 * b:4 * b + 1],
                                        st1[:, 4 * b:4 * b + 1], ALU.mult)
                        v.drain()
                        EV.inc(v.scalar_tensor_tensor(st1[:, 4 * b + 1:4 * b + 2],
                                                      st1[:, 4 * b + 1:4 * b + 2], 1.0 / D,
                                                      st1[:, 4 * b + 2:4 * b + 3],
                                                      op0=ALU.mult, op1=ALU.subtract))
                        v.wait_ge(EA.h, 2 * t + 2)
                        if t >= 2:
                            v.wait_ge(ST[b].h, 16 * ((t - 2) // 2 + 1))
                        EV.inc(v.tensor_scalar(lnb[b][:, :], xin[b][:, :],
                                               st1[:, 4 * b:4 * b + 1],
                                               st1[:, 4 * b + 1:4 * b + 2],
                                               op0=ALU.subtract, op1=ALU.mult))

                @blk.scalar
                def _(sc):
                    for t in range(NT):
                        b = t % 2
                        sc.wait_ge(LD[b].h, 16 * (t // 2 + 1))
                        if t >= 2:
                            sc.wait_ge(EV.h, 2 * (t - 2) + 2)
                        EA.inc(sc.activation(sqs[:, :], xin[b][:, :], AF.Square,
                                             accum_out=st1[:, 4 * b + 1:4 * b + 2]))
                        sc.wait_ge(EV.h, 2 * t + 1)
                        sc.activation(st1[:, 4 * b + 1:4 * b + 2],
                                      st1[:, 4 * b + 1:4 * b + 2], AF.Ln,
                                      bias=C["fconst"][:, 0:1])
                        sc.drain()
                        EA.inc(sc.activation(st1[:, 4 * b + 1:4 * b + 2],
                                             st1[:, 4 * b + 1:4 * b + 2], AF.Exp,
                                             scale=C["fconst"][:, 1:2]))
                    sc.wait_ge(ST[0].h, 16 * (NT // 2))
                    sc.wait_ge(ST[1].h, 16 * (NT // 2))
                    done.inc(sc.activation(st1[0:1, 0:1], st1[0:1, 0:1], AF.Copy))

    def emit_mamba(mdone):
        PAIRS = [[0, 1], [2, 3], [4, 5], [6, 7]]
        mctx = ExitStack()
        wincb = sbuf(mctx, "wincb", [128, 8, D], BF16)
        woutb = sbuf(mctx, "woutb", [128, 8, D], BF16)
        wxpb = sbuf(mctx, "wxpb", [128, 4, 96], BF16)
        wdtcb = sbuf(mctx, "wdtcb", [64, 512], BF16)
        dtBC = sbuf(mctx, "dtBC", [128, SEQ], F32)
        ldm = Sem(nc, mctx, "ldm")
        lndone = Sem(nc, mctx, "lndone")
        ccM2 = Sem(nc, mctx, "ccM2")
        ccM3 = Sem(nc, mctx, "ccM3")
        ST13 = Sem(nc, mctx, "st13")

        with nc.Block() as blk:
            @blk.gpsimd
            def _(g):
                g.wait_ge(f1done.h, 1)
                ldm.inc(g.dma_start(out=wincb[:, :, :],
                        in_=ap["winc"].rearrange("(k p) i -> p k i", p=128)), 16)
                ldm.inc(g.dma_start(out=wxpb[:, :, :],
                        in_=ap["wxp"].rearrange("(k p) c -> p k c", p=128)), 16)
                ldm.inc(g.dma_start(out=wdtcb[:, :], in_=ap["wdtc"]), 16)
                ldm.inc(g.dma_start(out=woutb[:, :, :],
                        in_=ap["woutr"].rearrange("(k p) d -> p k d", p=128)), 16)

        emit_ln2("m9", ap["x1_d"], ap["lnx_my"], f1done, 1, lndone)

        with ExitStack() as c11:
            rT = [sbuf(c11, "m11r%d" % i, [128, 8, 512], BF16) for i in range(2)]
            xzs = [sbuf(c11, "m11x%d" % i, [128, 8, 512], BF16) for i in range(2)]
            pz = [pst(c11, "m11p%d" % i, [128, 512], F32) for i in range(2)]
            LTI = [Sem(nc, c11, "m11lt%d" % i) for i in range(2)]
            STI = [Sem(nc, c11, "m11st%d" % i) for i in range(2)]
            EVI = Sem(nc, c11, "m11ev")
            MMI = Sem(nc, c11, "m11mm")
            ccM1 = Sem(nc, c11, "ccM1")
            with nc.Block() as blk:
                @blk.gpsimd
                def _(g):
                    g.wait_ge(lndone.h, 1)
                    ccM1.inc(g.collective_compute(
                        "AllGather", ALU.bypass, replica_groups=PAIRS,
                        ins=[ap["lnx_my"]], outs=[ap["ln2pair"]]))

            with nc.Block() as blk:
                @blk.scalar
                def _(sc):
                    sc.wait_ge(ccM1.h, 1)
                    for tc in range(NTC):
                        b = tc % 2
                        if tc >= 2:
                            sc.wait_ge(MMI.h, 8 * (tc - 1))
                        for k in range(8):
                            LTI[b].inc(sc.dma_start(
                                out=rT[b][:, k, :],
                                in_=ap["ln2pair"][512 * tc:512 * tc + 512,
                                                  128 * k:128 * k + 128],
                                transpose=True), 16)

                @blk.tensor
                def _(t_):
                    t_.wait_ge(ldm.h, 64)
                    for tc in range(NTC):
                        b = tc % 2
                        t_.wait_ge(LTI[b].h, 128 * (tc // 2 + 1))
                        for m in range(8):
                            gidx = 8 * tc + m
                            if gidx >= 2:
                                t_.wait_ge(EVI.h, gidx - 1)
                            for k in range(8):
                                ins = t_.matmul(pz[m % 2][:, :],
                                                wincb[:, k, 128 * m:128 * m + 128],
                                                rT[b][:, k, :],
                                                start=(k == 0), stop=(k == 7))
                            MMI.inc(ins)

                @blk.vector
                def _(v):
                    for tc in range(NTC):
                        b2 = tc % 2
                        for m in range(8):
                            gidx = 8 * tc + m
                            v.wait_ge(MMI.h, gidx + 1)
                            if tc >= 2 and m == 0:
                                v.wait_ge(STI[b2].h, 16 * ((tc - 2) // 2 + 1))
                            EVI.inc(v.tensor_copy(xzs[b2][:, m, :], pz[m % 2][:, :]))

                @blk.sync
                def _(s):
                    for tc in range(NTC):
                        b2 = tc % 2
                        s.wait_ge(EVI.h, 8 * (tc + 1))
                        STI[b2].inc(s.dma_start(
                            out=ap["xz_d"].rearrange("(m p) t -> p m t", p=128)[
                                :, :, 512 * tc:512 * tc + 512],
                            in_=xzs[b2][:, :, :]), 16)
                    s.wait_ge(STI[0].h, 16 * 8)
                    s.wait_ge(STI[1].h, 16 * 8)

        with ExitStack() as c12:
            czin = [sbuf(c12, "m12c%d" % i, [128, 8, 516], BF16) for i in range(2)]
            cf = sbuf(c12, "m12f", [128, 8, 512], F32)
            xhzs = [sbuf(c12, "m12h%d" % i, [128, 8, 512], BF16) for i in range(2)]
            stg = [sbuf(c12, "m12s%d" % i, [128, 512], F32) for i in range(2)]
            px = pst(c12, "m12p", [128, 512], F32)
            LCZ = [Sem(nc, c12, "m12l%d" % i) for i in range(2)]
            STC2 = [Sem(nc, c12, "m12t%d" % i) for i in range(2)]
            STX = [Sem(nc, c12, "m12x%d" % i) for i in range(2)]
            EVC = Sem(nc, c12, "m12ev")
            EAC = Sem(nc, c12, "m12ea")
            EAX = Sem(nc, c12, "m12ax")
            MMX = Sem(nc, c12, "m12mm")
            with nc.Block() as blk:
                @blk.sync
                def _(s):
                    for tc in range(NTC):
                        b = tc % 2
                        if tc >= 2:
                            s.wait_ge(EVC.h, 32 * (tc - 1))
                        xzv = ap["xz_d"].rearrange("(g p) t -> p g t", p=128)
                        if tc == 0:
                            LCZ[b].inc(s.dma_start(out=czin[b][:, :, 1:516],
                                                   in_=xzv[:, :, 0:515]), 16)
                        elif tc == NTC - 1:
                            LCZ[b].inc(s.dma_start(out=czin[b][:, :, 0:513],
                                                   in_=xzv[:, :, 512 * tc - 1:SEQ]), 16)
                        else:
                            LCZ[b].inc(s.dma_start(
                                out=czin[b][:, :, 0:515],
                                in_=xzv[:, :, 512 * tc - 1:512 * tc + 514]), 16)
                        if tc >= 1:
                            bp = (tc - 1) % 2
                            s.wait_ge(EAC.h, 8 * (tc - 1) + 8)
                            STC2[bp].inc(s.dma_start(
                                out=ap["xhc_d"].rearrange("(g p) t -> p g t", p=128)[
                                    :, :, 512 * (tc - 1):512 * tc],
                                in_=xhzs[bp][:, 0:4, :]), 16)
                            STC2[bp].inc(s.dma_start(
                                out=ap["zc_d"].rearrange("(g p) t -> p g t", p=128)[
                                    :, :, 512 * (tc - 1):512 * tc],
                                in_=xhzs[bp][:, 4:8, :]), 16)
                            s.wait_ge(EAX.h, tc)
                            STX[bp].inc(s.dma_start(
                                out=ap["xdbl_my"][0:96, 512 * (tc - 1):512 * tc],
                                in_=stg[bp][0:96, :]), 16)
                    bp = (NTC - 1) % 2
                    s.wait_ge(EAC.h, 8 * NTC)
                    STC2[bp].inc(s.dma_start(
                        out=ap["xhc_d"].rearrange("(g p) t -> p g t", p=128)[
                            :, :, 512 * (NTC - 1):SEQ],
                        in_=xhzs[bp][:, 0:4, :]), 16)
                    STC2[bp].inc(s.dma_start(
                        out=ap["zc_d"].rearrange("(g p) t -> p g t", p=128)[
                            :, :, 512 * (NTC - 1):SEQ],
                        in_=xhzs[bp][:, 4:8, :]), 16)
                    s.wait_ge(EAX.h, NTC)
                    STX[bp].inc(s.dma_start(
                        out=ap["xdbl_my"][0:96, 512 * (NTC - 1):SEQ],
                        in_=stg[bp][0:96, :]), 16)
                    s.wait_ge(STC2[0].h, 16 * 16)
                    s.wait_ge(STC2[1].h, 16 * 16)
                    s.wait_ge(STX[0].h, 16 * 8)
                    s.wait_ge(STX[1].h, 16 * 8)

                @blk.vector
                def _(v):
                    v.memset(czin[0][:, :, 0:1], 0.0)
                    v.memset(czin[1][:, :, 0:1], 0.0)
                    v.memset(stg[0][0:1, 0:2], 0.0)
                    for tc in range(NTC):
                        b = tc % 2
                        v.wait_ge(LCZ[b].h, 16 * (tc // 2 + 1))
                        if tc == NTC - 1:
                            v.memset(czin[b][:, :, 513:515], 0.0)
                        for j in range(4):
                            for gch in range(8):
                                if j == 0:
                                    if tc >= 1:
                                        v.wait_ge(EAC.h, 8 * (tc - 1) + gch + 1)
                                    EVC.inc(v.tensor_scalar(
                                        cf[:, gch, :], czin[b][:, gch, 0:512],
                                        C["ck"][:, gch, 0:1], None, op0=ALU.mult))
                                else:
                                    EVC.inc(v.scalar_tensor_tensor(
                                        cf[:, gch, :], czin[b][:, gch, j:j + 512],
                                        C["ck"][:, gch, j:j + 1], cf[:, gch, :],
                                        op0=ALU.mult, op1=ALU.add))

                @blk.scalar
                def _(sc):
                    for tc in range(NTC):
                        b = tc % 2
                        for gch in range(8):
                            sc.wait_ge(EVC.h, 32 * tc + 25 + gch)
                            if gch == 0 and tc >= 2:
                                sc.wait_ge(STC2[b].h, 32 * ((tc - 2) // 2 + 1))
                                sc.wait_ge(MMX.h, tc - 1)
                            EAC.inc(sc.activation(xhzs[b][:, gch, :], cf[:, gch, :],
                                                  AF.Silu))
                        sc.wait_ge(MMX.h, tc + 1)
                        if tc >= 2:
                            sc.wait_ge(STX[b].h, 16 * ((tc - 2) // 2 + 1))
                        EAX.inc(sc.activation(stg[b][0:96, :], px[0:96, :], AF.Copy))

                @blk.tensor
                def _(t_):
                    t_.wait_ge(ldm.h, 64)
                    for tc in range(NTC):
                        b = tc % 2
                        t_.wait_ge(EAC.h, 8 * tc + 4)
                        if tc >= 1:
                            t_.wait_ge(EAX.h, tc)
                        for k in range(4):
                            ins = t_.matmul(px[0:96, :], wxpb[:, k, :],
                                            xhzs[b][:, k, :],
                                            start=(k == 0), stop=(k == 3))
                        MMX.inc(ins)

            with nc.Block() as blk:
                @blk.gpsimd
                def _(g):
                    g.wait_ge(STX[0].h, 16 * 8)
                    g.wait_ge(STX[1].h, 16 * 8)
                    ccM2.inc(g.collective_compute(
                        "AllGather", ALU.bypass, replica_groups=PAIRS,
                        ins=[ap["xdbl_my"]], outs=[ap["xdbl_pair"]]))

        with ExitStack() as c13:
            L0 = sbuf(c13, "m13a", [128, SEQ], F32)
            L1 = sbuf(c13, "m13b", [128, SEQ], F32)
            LD13 = Sem(nc, c13, "m13l")
            EV13 = Sem(nc, c13, "m13e")
            with nc.Block() as blk:
                @blk.sync
                def _(s):
                    s.wait_ge(ccM2.h, 1)
                    LD13.inc(s.dma_start(out=L0[0:96, :], in_=ap["xdbl_pair"][0:96, :]), 16)
                    LD13.inc(s.dma_start(out=L1[0:96, :], in_=ap["xdbl_pair"][96:192, :]), 16)

                @blk.vector
                def _(v):
                    v.wait_ge(LD13.h, 32)
                    EV13.inc(v.tensor_tensor(dtBC[0:96, :], L0[0:96, :], L1[0:96, :],
                                             ALU.add))

                @blk.gpsimd
                def _(g):
                    g.wait_ge(EV13.h, 1)
                    ST13.inc(g.dma_start(out=ap["bd_d"], in_=dtBC[64:80, :]), 16)
                    ST13.inc(g.dma_start(out=ap["cd_d"], in_=dtBC[80:96, :]), 16)

        VPT, APT, PPT = 297, 80, 12
        ev_dtrb = lambda tc: VPT * tc + 1
        ev_deltaT = lambda tc, m: VPT * tc + 2 + m
        ev_yinit = lambda tc: VPT * tc + 13
        wb_done = lambda tc, n: VPT * tc + 13 + 17 * n + 4
        scan_done = lambda tc, n: VPT * tc + 13 + 17 * n + 8
        hc_done = lambda tc, n: VPT * tc + 13 + 17 * n + 12
        ev_ybf = lambda tc: VPT * tc + 289
        ev_evac = lambda tc, m: VPT * tc + 290 + m
        ea_abs = lambda tc, m: APT * tc + 2 * m + 2
        ea_ln = lambda tc, m: APT * tc + 13 + m
        ea_da = lambda tc, n, g: APT * tc + 17 + 4 * n + g
        mm_dt = lambda tc, m: PPT * tc + 1 + m
        mm_op = lambda tc, m: PPT * tc + 5 + m
        wbf_ = lambda fn: wb_done(fn // 16, fn % 16)
        scf_ = lambda fn: scan_done(fn // 16, fn % 16)
        hcf_ = lambda fn: hc_done(fn // 16, fn % 16)
        with ExitStack() as c14:
            xhcb = [sbuf(c14, "m14u%d" % i, [128, 4, 512], BF16) for i in range(2)]
            zcb = [sbuf(c14, "m14z%d" % i, [128, 4, 512], BF16) for i in range(2)]
            dtrb = [sbuf(c14, "m14d%d" % i, [64, 512], BF16) for i in range(2)]
            rbuf = sbuf(c14, "m14r", [128, 4, 512], F32)
            abuf = sbuf(c14, "m14a", [128, 4, 512], F32)
            deltaT = sbuf(c14, "m14t", [128, 4, 512], F32)
            dub = sbuf(c14, "m14w", [128, 4, 512], BF16)
            yac = sbuf(c14, "m14y", [128, 4, 512], F32)
            wb = sbuf(c14, "m14v", [128, 4, 512], F32)
            dab = [sbuf(c14, "m14e%d" % i, [128, 4, 512], F32) for i in range(2)]
            hcb = sbuf(c14, "m14h", [128, 4, 512], F32)
            hng = sbuf(c14, "m14g", [128, 4, 512], F32)
            Bbc = [sbuf(c14, "m14B%d" % i, [128, 512], BF16) for i in range(2)]
            Cbc = [sbuf(c14, "m14C%d" % i, [128, 512], BF16) for i in range(2)]
            ybfb = [sbuf(c14, "m14q%d" % i, [128, 4, 512], BF16) for i in range(2)]
            ost = [sbuf(c14, "m14o%d" % i, [128, 8, 512], BF16) for i in range(2)]
            carry = sbuf(c14, "m14k", [128, 64], F32)
            psd2 = [pst(c14, "m14pd%d" % i, [128, 512], F32) for i in range(2)]
            pso = [pst(c14, "m14po%d" % i, [128, 512], F32) for i in range(2)]
            LXH = [Sem(nc, c14, "m14lx%d" % i) for i in range(2)]
            LZC = [Sem(nc, c14, "m14lz%d" % i) for i in range(2)]
            LBB = [Sem(nc, c14, "m14lb%d" % i) for i in range(2)]
            LCC = [Sem(nc, c14, "m14lc%d" % i) for i in range(2)]
            STO = [Sem(nc, c14, "m14so%d" % i) for i in range(2)]
            EVS = Sem(nc, c14, "m14ev")
            EAS = Sem(nc, c14, "m14ea")
            MMP = Sem(nc, c14, "m14mm")
            with nc.Block() as blk:
                @blk.sync
                def _(s):
                    s.wait_ge(ST13.h, 32)
                    for tc in range(NTC):
                        b = tc % 2
                        if tc >= 2:
                            s.wait_ge(EVS.h, ev_yinit(tc - 2))
                        LXH[b].inc(s.dma_start(
                            out=xhcb[b][:, :, :],
                            in_=ap["xhc_d"].rearrange("(g p) t -> p g t", p=128)[
                                :, :, 512 * tc:512 * tc + 512]), 16)
                        if tc >= 2:
                            s.wait_ge(MMP.h, PPT * (tc - 1))
                        LZC[b].inc(s.dma_start(
                            out=zcb[b][:, :, :],
                            in_=ap["zc_d"].rearrange("(g p) t -> p g t", p=128)[
                                :, :, 512 * tc:512 * tc + 512]), 16)
                        for n in range(16):
                            fn = 16 * tc + n
                            if fn >= 2:
                                s.wait_ge(EVS.h, wbf_(fn - 2))
                            src = bass.AP(_th["bd_d"].ap().tensor,
                                          n * SEQ + 512 * tc, [[0, 128], [1, 512]])
                            LBB[n % 2].inc(s.dma_start(out=Bbc[n % 2][:, :], in_=src), 16)

                @blk.scalar
                def _(sc):
                    sc.wait_ge(ST13.h, 32)
                    for tc in range(NTC):
                        b = tc % 2
                        if tc >= 1:
                            sc.wait_ge(EVS.h, ev_evac(tc - 1, 7))
                            STO[(tc - 1) % 2].inc(sc.dma_start(
                                out=ap["op_send"].rearrange("(m p) t -> p m t", p=128)[
                                    :, 8 * ((tc - 1) // 8):8 * ((tc - 1) // 8) + 8,
                                    (512 * (tc - 1)) % 4096:(512 * (tc - 1)) % 4096 + 512],
                                in_=ost[(tc - 1) % 2][:, :, :]), 16)
                        for n in range(2):
                            fn = 16 * tc + n
                            if fn >= 2:
                                sc.wait_ge(EVS.h, hcf_(fn - 2))
                            src = bass.AP(_th["cd_d"].ap().tensor,
                                          n * SEQ + 512 * tc, [[0, 128], [1, 512]])
                            LCC[n % 2].inc(sc.dma_start(out=Cbc[n % 2][:, :], in_=src), 16)
                        for m in range(4):
                            sc.wait_ge(MMP.h, mm_dt(tc, m))
                            if m == 0 and tc >= 1:
                                sc.wait_ge(EVS.h, ev_deltaT(tc - 1, 3))
                            EAS.inc(sc.activation(rbuf[:, m, :], psd2[m % 2][:, :],
                                                  AF.Relu, bias=C["bdt2"][:, m:m + 1]))
                            EAS.inc(sc.activation(abuf[:, m, :], psd2[m % 2][:, :],
                                                  AF.Abs, bias=C["bdt2"][:, m:m + 1]))
                        for m in range(4):
                            EAS.inc(sc.activation(abuf[:, m, :], abuf[:, m, :],
                                                  AF.Exp, scale=C["fconst"][:, 2:3]))
                        for m in range(4):
                            EAS.inc(sc.activation(abuf[:, m, :], abuf[:, m, :],
                                                  AF.Ln, bias=C["fconst"][:, 3:4]))
                        for n in range(16):
                            fn = 16 * tc + n
                            for g in range(4):
                                if n == 0:
                                    sc.wait_ge(EVS.h, ev_deltaT(tc, g))
                                if g == 0 and fn >= 2:
                                    sc.wait_ge(EVS.h, scf_(fn - 2))
                                EAS.inc(sc.activation(dab[n % 2][:, g, :],
                                                      deltaT[:, g, :], AF.Exp,
                                                      scale=C["A"][:, g, n:n + 1]))
                            if n + 2 < 16:
                                sc.wait_ge(EVS.h, hcf_(16 * tc + n))
                                src = bass.AP(_th["cd_d"].ap().tensor,
                                              (n + 2) * SEQ + 512 * tc,
                                              [[0, 128], [1, 512]])
                                LCC[n % 2].inc(sc.dma_start(out=Cbc[n % 2][:, :],
                                                            in_=src), 16)
                    sc.wait_ge(EVS.h, ev_evac(NTC - 1, 7))
                    STO[(NTC - 1) % 2].inc(sc.dma_start(
                        out=ap["op_send"].rearrange("(m p) t -> p m t", p=128)[
                            :, 8 * ((NTC - 1) // 8):8 * ((NTC - 1) // 8) + 8,
                            (512 * (NTC - 1)) % 4096:(512 * (NTC - 1)) % 4096 + 512],
                        in_=ost[(NTC - 1) % 2][:, :, :]), 16)
                    sc.wait_ge(STO[0].h, 16 * 8)
                    sc.wait_ge(STO[1].h, 16 * 8)

                @blk.tensor
                def _(t_):
                    t_.wait_ge(ldm.h, 64)
                    for tc in range(NTC):
                        b = tc % 2
                        for m in range(4):
                            q = 4 * tc + m
                            if m == 0:
                                t_.wait_ge(EVS.h, ev_dtrb(tc))
                            if q >= 2:
                                t_.wait_ge(EAS.h, ea_abs((q - 2) // 4, (q - 2) % 4))
                            MMP.inc(t_.matmul(psd2[m % 2][:, :],
                                              wdtcb[0:64, 128 * m:128 * m + 128],
                                              dtrb[b][0:64, :],
                                              start=True, stop=True))
                        for m8 in range(8):
                            r = 8 * tc + m8
                            if m8 == 0:
                                t_.wait_ge(EVS.h, ev_ybf(tc))
                                t_.wait_ge(LZC[b].h, 16 * (tc // 2 + 1))
                            if r >= 2:
                                t_.wait_ge(EVS.h, ev_evac((r - 2) // 8, (r - 2) % 8))
                            for k in range(8):
                                rhs = ybfb[b][:, k, :] if k < 4 else zcb[b][:, k - 4, :]
                                ins = t_.matmul(pso[m8 % 2][:, :],
                                                woutb[:, k, 128 * m8:128 * m8 + 128],
                                                rhs, start=(k == 0), stop=(k == 7))
                            MMP.inc(ins)

                @blk.vector
                def _(v):
                    for tc in range(NTC):
                        b = tc % 2
                        if tc >= 2:
                            v.wait_ge(MMP.h, mm_dt(tc - 2, 3))
                        EVS.inc(v.tensor_copy(dtrb[b][0:64, :],
                                              dtBC[0:64, 512 * tc:512 * tc + 512]))
                        for m in range(4):
                            v.wait_ge(EAS.h, ea_ln(tc, m))
                            EVS.inc(v.tensor_tensor(deltaT[:, m, :], rbuf[:, m, :],
                                                    abuf[:, m, :], ALU.add))
                        for g in range(4):
                            if g == 0:
                                v.wait_ge(LXH[b].h, 16 * (tc // 2 + 1))
                            EVS.inc(v.tensor_tensor(dub[:, g, :], deltaT[:, g, :],
                                                    xhcb[b][:, g, :], ALU.mult))
                        for g in range(4):
                            EVS.inc(v.tensor_scalar(yac[:, g, :], xhcb[b][:, g, :],
                                                    C["dvec"][:, g:g + 1], None,
                                                    op0=ALU.mult))
                        for n in range(16):
                            fn = 16 * tc + n
                            for g in range(4):
                                if g == 0:
                                    v.wait_ge(LBB[n % 2].h, 16 * (8 * tc + n // 2 + 1))
                                EVS.inc(v.tensor_tensor(wb[:, g, :], dub[:, g, :],
                                                        Bbc[n % 2][:, :], ALU.mult))
                            for g in range(4):
                                if g == 0:
                                    v.wait_ge(EAS.h, ea_da(tc, n, 3))
                                EVS.inc(v.tensor_tensor_scan(
                                    hng[:, g, :], dab[n % 2][:, g, :], wb[:, g, :],
                                    0.0 if tc == 0 else carry[:, 4 * n + g:4 * n + g + 1],
                                    ALU.mult, ALU.add))
                            for g in range(4):
                                if g == 0:
                                    v.wait_ge(LCC[n % 2].h, 16 * (8 * tc + n // 2 + 1))
                                EVS.inc(v.tensor_tensor(hcb[:, g, :], hng[:, g, :],
                                                        Cbc[n % 2][:, :], ALU.mult))
                            for g in range(4):
                                EVS.inc(v.tensor_tensor(yac[:, g, :], yac[:, g, :],
                                                        hcb[:, g, :], ALU.add))
                            EVS.inc(v.tensor_copy(carry[:, 4 * n:4 * n + 4],
                                                  r2(hng[:, :, 511:512])))
                        if tc >= 2:
                            v.wait_ge(MMP.h, mm_op(tc - 2, 7))
                        for g in range(4):
                            EVS.inc(v.tensor_copy(ybfb[b][:, g, :], yac[:, g, :]))
                        for m8 in range(8):
                            v.wait_ge(MMP.h, mm_op(tc, m8))
                            if m8 == 0 and tc >= 2:
                                v.wait_ge(STO[b].h, 16 * ((tc - 2) // 2 + 1))
                            EVS.inc(v.tensor_copy(ost[b][:, m8, :], pso[m8 % 2][:, :]))

            with nc.Block() as blk:
                @blk.gpsimd
                def _(g):
                    g.wait_ge(STO[0].h, 16 * 8)
                    g.wait_ge(STO[1].h, 16 * 8)
                    ccM3.inc(g.collective_compute(
                        "AllGather", ALU.bypass, replica_groups=PAIRS,
                        ins=[ap["op_send"]], outs=[ap["op_pag"]]))

        with ExitStack() as c15:
            la = [sbuf(c15, "m15a%d" % i, [128, TCOR], BF16) for i in range(2)]
            lb = [sbuf(c15, "m15b%d" % i, [128, TCOR], BF16) for i in range(2)]
            mos = [sbuf(c15, "m15m%d" % i, [128, TCOR], BF16) for i in range(2)]
            GI15 = [Sem(nc, c15, "m15g%d" % i) for i in range(2)]
            ST15 = [Sem(nc, c15, "m15s%d" % i) for i in range(2)]
            EV15 = Sem(nc, c15, "m15e")
            with nc.Block() as blk:
                @blk.gpsimd
                def _(g):
                    g.wait_ge(ccM3.h, 1)
                    rb = g.to_reg(4 * D - 1)
                    for m in range(8):
                        b = m % 2
                        if m >= 2:
                            g.wait_ge(EV15.h, m - 1)
                        GI15[b].inc(g.indirect_dma_start(
                            out=la[b][:, :], out_offset=None, in_=ap["op_pag"],
                            in_offset=bass.IndirectOffsetOnAxis(
                                C["mo_rows"][:, 2 * m:2 * m + 1], 0),
                            bounds_check=rb, oob_is_err=False), 16)
                        GI15[b].inc(g.indirect_dma_start(
                            out=lb[b][:, :], out_offset=None, in_=ap["op_pag"],
                            in_offset=bass.IndirectOffsetOnAxis(
                                C["mo_rows"][:, 2 * m + 1:2 * m + 2], 0),
                            bounds_check=rb, oob_is_err=False), 16)

                @blk.vector
                def _(v):
                    for m in range(8):
                        b = m % 2
                        v.wait_ge(GI15[b].h, 32 * (m // 2 + 1))
                        if m >= 2:
                            v.wait_ge(ST15[b].h, 16 * (m // 2))
                        EV15.inc(v.scalar_tensor_tensor(
                            mos[b][:, :], la[b][:, :], C["bout"][:, m:m + 1],
                            lb[b][:, :], op0=ALU.add, op1=ALU.add))

                @blk.sync
                def _(s):
                    for m in range(8):
                        s.wait_ge(EV15.h, m + 1)
                        ST15[m % 2].inc(s.dma_start(
                            out=ap["mo_d"][128 * m:128 * m + 128, :],
                            in_=mos[m % 2][:, :]), 16)
                    s.wait_ge(ST15[0].h, 16 * 4)
                    s.wait_ge(ST15[1].h, 16 * 4)

        with ExitStack() as c16:
            mt = [sbuf(c16, "m16m%d" % i, [128, D], BF16) for i in range(2)]
            x1b = [sbuf(c16, "m16a%d" % i, [128, D], F32) for i in range(2)]
            x2b = [sbuf(c16, "m16b%d" % i, [128, D], F32) for i in range(2)]
            LT16 = [Sem(nc, c16, "m16l%d" % i) for i in range(2)]
            LX16 = [Sem(nc, c16, "m16x%d" % i) for i in range(2)]
            ST16 = [Sem(nc, c16, "m16s%d" % i) for i in range(2)]
            EV16 = Sem(nc, c16, "m16e")
            with nc.Block() as blk:
                @blk.sync
                def _(s):
                    for t in range(NT):
                        b = t % 2
                        if t >= 2:
                            s.wait_ge(EV16.h, t - 1)
                        LT16[b].inc(s.dma_start(
                            out=mt[b][:, :],
                            in_=ap["mo_d"][0:D, 128 * t:128 * t + 128],
                            transpose=True), 16)
                        LX16[b].inc(s.dma_start(
                            out=x1b[b][:, :],
                            in_=ap["x1_d"][128 * t:128 * t + 128, :]), 16)

                @blk.vector
                def _(v):
                    for t in range(NT):
                        b = t % 2
                        v.wait_ge(LT16[b].h, 16 * (t // 2 + 1))
                        v.wait_ge(LX16[b].h, 16 * (t // 2 + 1))
                        if t >= 2:
                            v.wait_ge(ST16[b].h, 16 * ((t - 2) // 2 + 1))
                        EV16.inc(v.tensor_tensor(x2b[b][:, :], x1b[b][:, :],
                                                 mt[b][:, :], ALU.add))

                @blk.scalar
                def _(sc):
                    for t in range(NT):
                        sc.wait_ge(EV16.h, t + 1)
                        ST16[t % 2].inc(sc.dma_start(
                            out=ap["x2_d"][128 * t:128 * t + 128, :],
                            in_=x2b[t % 2][:, :]), 16)
                    sc.wait_ge(ST16[0].h, 16 * (NT // 2))
                    sc.wait_ge(ST16[1].h, 16 * (NT // 2))
                    mdone.inc(sc.activation(x2b[0][0:1, 0:1], x2b[0][0:1, 0:1],
                                            AF.Copy))
        mctx.close()

    f1done = Sem(nc, pctx, "f1done")
    emit_ffn("f1", ap["x_my"], ap["x1_d"], w1s, w2s, f1done)
    f1ctx.close()

    if _UPTO >= 9:
        x2done = Sem(nc, pctx, "x2done")
        emit_mamba(x2done)
        f2ctx = ExitStack()
        w1s2 = sbuf(f2ctx, "w1s2", [128, 8, H], BF16)
        w2s2 = sbuf(f2ctx, "w2s2", [128, 32, D], BF16)
        ldw = Sem(nc, f2ctx, "ldw2")
        with nc.Block() as blk:
            @blk.gpsimd
            def _(g):
                g.wait_ge(x2done.h, 1)
                ldw.inc(g.dma_start(out=w1s2[:, :, :], in_=ap["wbf1_d"]), 16)
                ldw.inc(g.dma_start(out=w2s2[:, :, :], in_=ap["wbf2_d"]), 16)
                g.wait_ge(ldw.h, 32)
        f2done = Sem(nc, pctx, "f2done")
        emit_ffn("f2", ap["x2_d"], ap["y_out"], w1s2, w2s2, f2done, xdep=(x2done, 1))
        f2ctx.close()
    if _DEBUG:
        dbf = Sem(nc, pctx, "dbf")
        with nc.Block() as blk:
            @blk.sync
            def _(s):
                s.wait_ge(f1done.h, 1)
                for src, dst in [("lnx_my", "dbg_lnx"), ("totd", "dbg_tot"),
                                 ("cnt_all", "dbg_cnt"), ("pq_d", "dbg_pq"),
                                 ("ldisp", "dbg_ldisp"), ("ddisp", "dbg_ddisp"),
                                 ("eo_my", "dbg_eo"), ("eo_ret", "dbg_eor")]:
                    dbf.inc(s.dma_start(out=ap[dst], in_=ap[src]), 16)
                s.wait_ge(dbf.h, dbf.n)
    if _UPTO >= 9:
        pass
    else:
        with ExitStack() as cd:
            tb = sbuf(cd, "devcp", [128, D], F32)
            LDz = Sem(nc, cd, "ldz")
            STz = Sem(nc, cd, "stz")
            with nc.Block() as blk:
                @blk.sync
                def _(s):
                    s.wait_ge(f1done.h, 1)
                    for t in range(NT):
                        if t >= 1:
                            s.wait_ge(STz.h, 16 * t)
                        LDz.inc(s.dma_start(out=tb[:, :],
                                            in_=ap["x1_d"][128 * t:128 * t + 128, :]), 16)
                        s.wait_ge(LDz.h, 16 * (t + 1))
                        STz.inc(s.dma_start(out=ap["y_out"][128 * t:128 * t + 128, :],
                                            in_=tb[:, :]), 16)
                    s.wait_ge(STz.h, 16 * NT)

    pctx.close()
    return nc



def _prep_core(c, I):
    b, k = c // 2, c % 2
    xh = np.arange(512 * k, 512 * k + 512)
    f32 = np.float32
    bf = ml_dtypes.bfloat16
    d = {}
    X = np.asarray(I["L0_attn_x"], f32).reshape(32768, D)
    d["x_my"] = np.ascontiguousarray(X[4096 * c:4096 * (c + 1)])
    d["fc1wT"] = np.ascontiguousarray(np.asarray(I["L0_ffn_fc1_w"], f32)[c].T)
    d["fc2w"] = np.ascontiguousarray(np.asarray(I["L0_ffn_fc2_w"], f32)[c])
    d["b1t"] = np.ascontiguousarray(np.asarray(I["L0_ffn_fc1_b"], f32)[c, 0].reshape(32, 128).T)
    d["b2"] = np.ascontiguousarray(np.asarray(I["L0_ffn_fc2_b"], f32)[c, 0].reshape(1, D))
    d["gwT"] = np.ascontiguousarray(np.asarray(I["L0_ffn_gate_w"], f32).T)
    cols = np.concatenate([xh, 1024 + xh])
    d["winc"] = np.ascontiguousarray(np.asarray(I["L0_attn_W_in"], f32)[:, cols])
    d["woutr"] = np.ascontiguousarray(np.asarray(I["L0_attn_W_out"], f32)[cols, :])
    d["wxp"] = np.ascontiguousarray(np.asarray(I["L0_attn_W_xproj"], f32)[xh, :])
    d["wdtc"] = np.ascontiguousarray(np.asarray(I["L0_attn_W_dt"], f32)[:, xh])
    d["bdt"] = np.ascontiguousarray(np.asarray(I["L0_attn_b_dt"], f32)[xh].reshape(4, 128).T)
    d["alog"] = np.ascontiguousarray(
        np.asarray(I["L0_attn_A_log"], f32)[xh].reshape(4, 128, 16).transpose(1, 0, 2))
    d["dvec"] = np.ascontiguousarray(np.asarray(I["L0_attn_D"], f32)[xh].reshape(4, 128).T)
    ckx = np.asarray(I["L0_attn_conv_x_k"], f32)[:, 0, :][:, xh]
    ckz = np.asarray(I["L0_attn_conv_z_k"], f32)[:, 0, :][:, xh]
    ck = np.zeros((128, 8, 4), f32)
    ck[:, 0:4, :] = ckx.T.reshape(4, 128, 4).transpose(1, 0, 2)
    ck[:, 4:8, :] = ckz.T.reshape(4, 128, 4).transpose(1, 0, 2)
    d["ck"] = ck
    d["bout"] = np.ascontiguousarray(np.asarray(I["L0_attn_b_out"], f32).reshape(8, 128).T)
    d["ident"] = np.eye(128, dtype=bf)
    d["utri"] = np.triu(np.ones((128, 128), bf))
    d["ltri"] = np.tril(np.ones((128, 128), bf), -1)
    d["onesm"] = np.ones((128, 128), bf)
    d["iotaE"] = np.broadcast_to(np.arange(E, dtype=f32), (128, NT, E)).copy()
    d["prefv"] = (np.arange(8) < c).astype(f32).reshape(8, 1)
    d["ones8"] = np.ones((8, 1), f32)
    p = np.arange(128)
    d["mam_rows"] = (b * 8192 + (np.arange(64) * 128)[None, :] + p[:, None]).astype(np.int32)
    xr = np.zeros((128, 4), np.int32)
    xr[:, 0] = (2 * b) * 256 + p
    xr[:, 1] = (2 * b + 1) * 256 + p
    xr[:, 2] = (2 * b) * 256 + 128 + p
    xr[:, 3] = (2 * b + 1) * 256 + 128 + p
    d["xdb_rows"] = xr
    mo = np.zeros((128, 16), np.int32)
    for m in range(8):
        for src in range(2):
            mo[:, 2 * m + src] = 2048 * src + 1024 * k + 128 * m + p
    d["mo_rows"] = mo
    m01 = np.zeros((128, 2), f32)
    m01[:, 0] = 1.0 if k == 0 else 0.0
    m01[:, 1] = 1.0 if k == 1 else 0.0
    d["m01"] = m01
    fc = np.zeros((128, 8), f32)
    fc[:, 0] = 1e-5
    fc[:, 1] = -0.5
    fc[:, 2] = -1.0
    fc[:, 3] = 1.0
    fc[:, 4] = 0.5
    fc[:, 6] = 8192.0
    d["fconst"] = fc
    return d


_last_results = None


def kernel(**inputs):
    global _last_results
    nc = build_nc()
    in_maps = [_prep_core(c, inputs) for c in range(NCORE)]
    res = run_bass_kernel_spmd(nc, in_maps, core_ids=list(range(NCORE)))
    _last_results = res.results
    y = np.concatenate([np.asarray(res.results[c]["y_out"]) for c in range(NCORE)], axis=0)
    return y.reshape(4, 8192, 1024).astype(np.float32)


# version 02
# speedup vs baseline: 1.0136x; 1.0136x over previous
import os
os.environ.setdefault("TRN_ALLOW_ICI_ALL_TO_ALL", "1")

import numpy as np
import ml_dtypes
from contextlib import ExitStack

import concourse.bass as bass
import concourse.mybir as mybir
from concourse.bass_utils import run_bass_kernel_spmd
from concourse.aot_env import allow_ici_all_to_all

F32 = mybir.dt.float32
BF16 = mybir.dt.bfloat16
I32 = mybir.dt.int32
U8 = mybir.dt.uint8
AF = mybir.ActivationFunctionType
ALU = mybir.AluOpType
AX = mybir.AxisListType

NCORE = 8
TCOR = 4096
NT = 32
D = 1024
H = 4096
E = 8
CAPL = 1280
SHARD = 1296
NST = 20
SEQ = 8192
TCH = 512
NTC = 16
BIGF = 1.0e9

_UPTO = 99
_DEBUG = False


class _SemScope:

    def __init__(self, nc, ctx):
        self.nc = nc
        self.handles = []
        ctx.callback(self._close)

    def _close(self):
        if self.handles:
            self.nc.all_engine_barrier()
            self.nc.clear_and_free_semaphores(self.handles)


class Sem:
    def __init__(self, nc, ctx, name):
        if not hasattr(ctx, "_semscope"):
            ctx._semscope = _SemScope(nc, ctx)
        self.h = nc.alloc_semaphore(name=name)
        ctx._semscope.handles.append(self.h)
        self.n = 0

    def inc(self, ins, k=1):
        ins.then_inc(self.h, k)
        self.n += k
        return self.n


def r2(t):
    return t.rearrange("p a b -> p (a b)")


def build_nc():
    nc = bass.Bass("TRN2", target_bir_lowering=False, debug=False,
                 detect_race_conditions=False)
    _th = {}

    class _ApDict:
        def __getitem__(self, name):
            return _th[name].ap()

    ap = _ApDict()

    def din(name, shape, dt):
        _th[name] = nc.dram_tensor(name, shape, dt, kind="ExternalInput")

    def dint(name, shape, dt):
        _th[name] = nc.dram_tensor(name, shape, dt)

    din("x_my", [TCOR, D], F32)
    din("fc1wT", [D, H], F32)
    din("fc2w", [H, D], F32)
    din("b1t", [128, 32], F32)
    din("b2", [1, D], F32)
    din("gwT", [D, E], F32)
    din("winc", [D, D], F32)
    din("woutr", [D, D], F32)
    din("wxp", [512, 96], F32)
    din("wdtc", [64, 512], F32)
    din("bdt", [128, 4], F32)
    din("alog", [128, 4, 16], F32)
    din("dvec", [128, 4], F32)
    din("ck", [128, 8, 4], F32)
    din("bout", [128, 8], F32)
    din("ident", [128, 128], BF16)
    din("utri", [128, 128], BF16)
    din("ltri", [128, 128], BF16)
    din("onesm", [128, 128], BF16)
    din("iotaE", [128, NT, E], F32)
    din("prefv", [8, 1], F32)
    din("ones8", [8, 1], F32)
    din("mam_rows", [128, 64], I32)
    din("xdb_rows", [128, 4], I32)
    din("mo_rows", [128, 16], I32)
    din("m01", [128, 2], F32)
    din("diagd", [128, 4, 128], BF16)
    din("fconst", [128, 8], F32)

    _th["y_out"] = nc.dram_tensor("y_out", [TCOR, D], F32, kind="ExternalOutput")

    dint("lnx_my", [TCOR, D], BF16)
    dint("ldisp", [E * SHARD, D], BF16)
    dint("ddisp", [E * SHARD, D], BF16)
    dint("eo_my", [E * CAPL, D], BF16)
    dint("eo_ret", [E * CAPL, D], BF16)
    dint("totd", [1, 16], F32)
    dint("cnt_all", [8, 16], F32)
    dint("pq_d", [1, 32], F32)
    dint("x1_d", [TCOR, D], F32)
    dint("x2_d", [TCOR, D], F32)
    dint("ln2pair", [2 * TCOR, D], BF16)
    dint("xdbl_pair", [192, SEQ], F32)
    dint("xz_d", [D, SEQ], BF16)
    dint("xhc_d", [512, SEQ], BF16)
    dint("zc_d", [512, SEQ], BF16)
    dint("xdbl_my", [96, SEQ], F32)
    dint("bd_d", [16, SEQ], BF16)
    dint("cd_d", [16, SEQ], BF16)
    dint("op_send", [2 * D, TCOR], BF16)
    dint("op_pag", [4 * D, TCOR], BF16)
    dint("mo_d", [D, TCOR], BF16)
    dint("wbf1_d", [128, 8, H], BF16)
    dint("wbf2_d", [128, 32, D], BF16)
    if _DEBUG:
        for nm, shape, dt in [("dbg_lnx", [TCOR, D], BF16), ("dbg_tot", [1, 16], F32),
                              ("dbg_cnt", [8, 16], F32), ("dbg_pq", [1, 32], F32),
                              ("dbg_ldisp", [E * SHARD, D], BF16),
                              ("dbg_ddisp", [E * SHARD, D], BF16),
                              ("dbg_eo", [E * CAPL, D], BF16),
                              ("dbg_eor", [E * CAPL, D], BF16),
                              ("dbg_rt", [128, 64, 8], F32), ("dbg_lg", [128, NT, 8], F32),
                              ("dbg_oh", [128, NT, 16], F32), ("dbg_pp", [128, NT, 16], F32)]:
            _th[nm] = nc.dram_tensor(nm, shape, dt, kind="ExternalOutput")

    pctx = ExitStack()

    def sbuf(ctx, name, shape, dt):
        return ctx.enter_context(nc.sbuf_tensor(name, shape, dt))

    def pst(ctx, name, shape, dt):
        return ctx.enter_context(nc.psum_tensor(name, shape, dt))

    C = {}
    for nm, shape, dt in [("ident", [128, 128], BF16), ("utri", [128, 128], BF16),
                          ("ltri", [128, 128], BF16), ("onesm", [128, 128], BF16),
                          ("iotaE", [128, NT, E], F32),
                          ("prefv", [8, 1], F32), ("ones8", [8, 1], F32),
                          ("b1t", [128, 32], F32), ("m01", [128, 2], F32),
                          ("bout", [128, 8], F32), ("mam_rows", [128, 64], I32),
                          ("xdb_rows", [128, 4], I32), ("mo_rows", [128, 16], I32),
                          ("bdt", [128, 4], F32), ("alog", [128, 4, 16], F32),
                          ("dvec", [128, 4], F32), ("ck", [128, 8, 4], F32),
                          ("diagd", [128, 4, 128], BF16),
                          ("fconst", [128, 8], F32)]:
        C[nm] = sbuf(pctx, "c_" + nm, shape, dt)
    C["b2bc"] = sbuf(pctx, "c_b2bc", [128, D], F32)
    C["gwTb"] = sbuf(pctx, "c_gwTb", [128, 8, 8], BF16)
    C["A"] = sbuf(pctx, "c_A", [128, 4, 16], F32)
    C["bdt2"] = sbuf(pctx, "c_bdt2", [128, 4], F32)

    ld0 = Sem(nc, pctx, "ld0")
    ldg = Sem(nc, pctx, "ldg")
    lds = Sem(nc, pctx, "lds")
    ldz = Sem(nc, pctx, "ldz")
    cm0 = Sem(nc, pctx, "cm0")

    f1ctx = ExitStack()
    w1s = sbuf(f1ctx, "w1s", [128, 8, H], BF16)
    w2s = sbuf(f1ctx, "w2s", [128, 32, D], BF16)

    with ExitStack() as zctx:
        zt = sbuf(zctx, "zt", [128, 10240], BF16)
        with nc.Block() as blk:
            @blk.sync
            def _(s):
                for nm in ["ident", "utri", "ltri", "iotaE", "prefv", "ones8",
                           "b1t", "m01", "bout", "mam_rows", "xdb_rows",
                           "mo_rows", "bdt", "alog", "dvec", "ck", "fconst",
                           "onesm", "diagd"]:
                    ld0.inc(s.dma_start(out=C[nm][...], in_=ap[nm]), 16)

            @blk.scalar
            def _(sc):
                bap = bass.AP(ap["b2"].tensor, 0, [[0, 128], [1, D]])
                lds.inc(sc.dma_start(out=C["b2bc"][:, :], in_=bap), 16)

            @blk.gpsimd
            def _(g):
                ldg.inc(g.dma_start(out=w1s[:, :, :],
                        in_=ap["fc1wT"].rearrange("(k p) h -> p k h", p=128)), 16)
                ldg.inc(g.dma_start(out=w2s[:, :, :],
                        in_=ap["fc2w"].rearrange("(g p) d -> p g d", p=128)), 16)
                ldg.inc(g.dma_start(out=C["gwTb"][:, :, :],
                        in_=ap["gwT"].rearrange("(k p) e -> p k e", p=128)), 16)

            @blk.vector
            def _(v):
                cm0.inc(v.memset(zt[:, :], 0))

        with nc.Block() as blk:
            mk_ld = ld0.n

            @blk.scalar
            def _(sc):
                sc.wait_ge(ld0.h, mk_ld)
                sc.wait_ge(ldg.h, 48)
                cm0.inc(sc.activation(C["A"][...], C["alog"][...], AF.Exp))

            @blk.vector
            def _(v):
                v.wait_ge(cm0.h, 2)
                cm0.inc(v.tensor_scalar(C["A"][...], C["A"][...], -1.0, None, op0=ALU.mult))
                cm0.inc(v.tensor_scalar(C["bdt2"][...], C["bdt"][...], 2.0, None, op0=ALU.mult))

            @blk.sync
            def _(s):
                s.wait_ge(cm0.h, 1)
                rows = E * SHARD
                r0 = 0
                while r0 < rows:
                    n = min(1280, rows - r0)
                    nb = n // 128
                    ldz.inc(s.dma_start(
                        out=ap["ldisp"][r0:r0 + n, :].rearrange("(a p) d -> p a d", p=128),
                        in_=zt[:, 0:nb * 1024].rearrange("p (a d) -> p a d", d=1024)), 16)
                    r0 += n
                s.wait_ge(ld0.h, mk_ld)

            @blk.gpsimd
            def _(g):
                g.wait_ge(ldg.h, 48)
                ldg.inc(g.dma_start(out=ap["wbf1_d"], in_=w1s[:, :, :]), 16)
                ldg.inc(g.dma_start(out=ap["wbf2_d"], in_=w2s[:, :, :]), 16)

    P0_LD = ld0.n

    def emit_ffn(tag, src_x, dst_x, wa, wb, done, xdep=None):
        fctx = ExitStack()
        logits = sbuf(fctx, tag + "logits", [128, NT, 8], F32)
        if _DEBUG:
            dbg_oh_sb = sbuf(fctx, tag + "dbgoh", [128, NT, 16], F32)
            dbg_pp_sb = sbuf(fctx, tag + "dbgpp", [128, NT, 16], F32)
        geff = sbuf(fctx, tag + "geff", [128, NT, 2], F32)
        sidx = sbuf(fctx, tag + "sidx", [128, 64], I32)
        cidx = sbuf(fctx, tag + "cidx", [128, 64], I32)
        grdy = Sem(nc, fctx, tag + "grdy")
        cc2 = Sem(nc, fctx, tag + "cc2")
        cc3 = Sem(nc, fctx, tag + "cc3")

        with ExitStack() as c1:
            xin = [sbuf(c1, tag + "xin%d" % i, [128, D], F32) for i in range(2)]
            lnb = [sbuf(c1, tag + "lnb%d" % i, [128, D], BF16) for i in range(2)]
            sqs = sbuf(c1, tag + "sqs", [128, D], F32)
            st1 = sbuf(c1, tag + "st1", [128, 8], F32)
            lnxT = sbuf(c1, tag + "lnxT", [128, 8, 128], BF16)
            ptA = pst(c1, tag + "ptA", [128, 1024], BF16)
            ptB = pst(c1, tag + "ptB", [128, 1024], BF16)
            psL = pst(c1, tag + "psL", [128, 8], F32)
            LD = [Sem(nc, c1, tag + "p1ld%d" % i) for i in range(2)]
            ST = [Sem(nc, c1, tag + "p1st%d" % i) for i in range(2)]
            EV = Sem(nc, c1, tag + "p1ev")
            EA = Sem(nc, c1, tag + "p1ea")
            MM = Sem(nc, c1, tag + "p1mm")
            with nc.Block() as blk:
                @blk.sync
                def _(s):
                    s.wait_ge(ld0.h, P0_LD)
                    s.wait_ge(ldg.h, 80)
                    s.wait_ge(lds.h, 16)
                    s.wait_ge(ldz.h, 144)
                    if xdep is not None:
                        s.wait_ge(xdep[0].h, xdep[1])
                    for t in range(NT):
                        b = t % 2
                        if t >= 2:
                            s.wait_ge(EV.h, 4 * (t - 2) + 2)
                        LD[b].inc(s.dma_start(out=xin[b][:, :],
                                              in_=src_x[128 * t:128 * t + 128, :]), 16)
                        if t >= 1:
                            s.wait_ge(EV.h, 4 * (t - 1) + 2)
                            ST[(t - 1) % 2].inc(s.dma_start(
                                out=ap["lnx_my"][128 * (t - 1):128 * t, :],
                                in_=lnb[(t - 1) % 2][:, :]), 16)
                    s.wait_ge(EV.h, 4 * (NT - 1) + 2)
                    ST[(NT - 1) % 2].inc(s.dma_start(
                        out=ap["lnx_my"][128 * (NT - 1):128 * NT, :],
                        in_=lnb[(NT - 1) % 2][:, :]), 16)
                    s.wait_ge(ST[0].h, 16 * (NT // 2))
                    s.wait_ge(ST[1].h, 16 * (NT // 2))

                @blk.vector
                def _(v):
                    for t in range(NT):
                        b = t % 2
                        v.wait_ge(LD[b].h, 16 * (t // 2 + 1))
                        v.tensor_reduce(st1[:, 4 * b:4 * b + 1], xin[b][:, :], AX.X, ALU.add)
                        v.drain()
                        v.wait_ge(EA.h, 2 * t + 1)
                        v.tensor_scalar(st1[:, 4 * b:4 * b + 1], st1[:, 4 * b:4 * b + 1],
                                        1.0 / D, None, op0=ALU.mult)
                        v.drain()
                        v.tensor_tensor(st1[:, 4 * b + 2:4 * b + 3], st1[:, 4 * b:4 * b + 1],
                                        st1[:, 4 * b:4 * b + 1], ALU.mult)
                        v.drain()
                        EV.inc(v.scalar_tensor_tensor(st1[:, 4 * b + 1:4 * b + 2],
                                                      st1[:, 4 * b + 1:4 * b + 2], 1.0 / D,
                                                      st1[:, 4 * b + 2:4 * b + 3],
                                                      op0=ALU.mult, op1=ALU.subtract))
                        v.wait_ge(EA.h, 2 * t + 2)
                        if t >= 2:
                            v.wait_ge(MM.h, 2 * (t - 2) + 1)
                            v.wait_ge(ST[b].h, 16 * ((t - 2) // 2 + 1))
                        EV.inc(v.tensor_scalar(lnb[b][:, :], xin[b][:, :],
                                               st1[:, 4 * b:4 * b + 1],
                                               st1[:, 4 * b + 1:4 * b + 2],
                                               op0=ALU.subtract, op1=ALU.mult))
                        v.wait_ge(MM.h, 2 * t + 1)
                        EV.inc(v.tensor_copy(lnxT[:, :, :],
                                             (ptA if b == 0 else ptB)[:, :].rearrange(
                                                 "p (k q) -> p k q", q=128)))
                        v.wait_ge(MM.h, 2 * t + 2)
                        EV.inc(v.tensor_copy(logits[:, t, :], psL[:, :]))

                @blk.scalar
                def _(sc):
                    for t in range(NT):
                        b = t % 2
                        sc.wait_ge(LD[b].h, 16 * (t // 2 + 1))
                        if t >= 2:
                            sc.wait_ge(EV.h, 4 * (t - 2) + 2)
                        EA.inc(sc.activation(sqs[:, :], xin[b][:, :], AF.Square,
                                             accum_out=st1[:, 4 * b + 1:4 * b + 2]))
                        sc.wait_ge(EV.h, 4 * t + 1)
                        sc.activation(st1[:, 4 * b + 1:4 * b + 2],
                                      st1[:, 4 * b + 1:4 * b + 2], AF.Ln,
                                      bias=C["fconst"][:, 0:1])
                        sc.drain()
                        EA.inc(sc.activation(st1[:, 4 * b + 1:4 * b + 2],
                                             st1[:, 4 * b + 1:4 * b + 2], AF.Exp,
                                             scale=C["fconst"][:, 1:2]))

                @blk.tensor
                def _(t_):
                    for t in range(NT):
                        b = t % 2
                        pt = ptA if b == 0 else ptB
                        t_.wait_ge(EV.h, 4 * t + 2)
                        if t >= 2:
                            t_.wait_ge(EV.h, 4 * (t - 2) + 3)
                        for k in range(8):
                            ins = t_.transpose(pt[:, 128 * k:128 * k + 128],
                                               lnb[b][:, 128 * k:128 * k + 128],
                                               C["ident"][:, :])
                        MM.inc(ins)
                        t_.wait_ge(EV.h, 4 * t + 3)
                        for k in range(8):
                            ins = t_.matmul(psL[:, :], lnxT[:, k, :], C["gwTb"][:, k, :],
                                            start=(k == 0), stop=(k == 7))
                        MM.inc(ins)

        with ExitStack() as c2:
            ohb = sbuf(c2, tag + "ohb", [128, NT, 16], BF16)
            locP = sbuf(c2, tag + "locP", [128, NT, 2], F32)
            g01 = sbuf(c2, tag + "g01", [128, NT, 2], F32)
            scr = sbuf(c2, tag + "scr", [128, NT, 8], F32)
            scr2 = sbuf(c2, tag + "scr2", [128, NT, 8], F32)
            msk = sbuf(c2, tag + "msk", [128, NT, 8], U8)
            sm1 = sbuf(c2, tag + "sm1", [128, NT, 1], F32)
            sm2 = sbuf(c2, tag + "sm2", [128, NT, 1], F32)
            si1 = sbuf(c2, tag + "si1", [128, NT, 1], F32)
            si2 = sbuf(c2, tag + "si2", [128, NT, 1], F32)
            Pp = sbuf(c2, tag + "Pp", [128, NT, 16], F32)
            Pf = sbuf(c2, tag + "Pf", [128, NT, 16], F32)
            Tot2 = sbuf(c2, tag + "Tot2", [128, 16, NT], F32)
            Incl = sbuf(c2, tag + "Incl", [128, 16, NT], F32)
            onv = sbuf(c2, tag + "onv", [128, NT], F32)
            tot = sbuf(c2, tag + "tot", [1, 16], F32)
            totbc = sbuf(c2, tag + "totbc", [128, 16], F32)
            tmp1 = sbuf(c2, tag + "tmp1", [128, NT, 1], F32)
            cnt_sb = sbuf(c2, tag + "cnt", [8, 16], F32)
            prefbc = sbuf(c2, tag + "prefbc", [128, 16], F32)
            totA = sbuf(c2, tag + "totA", [128, 16], F32)
            pq = sbuf(c2, tag + "pq", [1, 32], F32)
            psP = pst(c2, tag + "psP", [128, 16], F32)
            psT = pst(c2, tag + "psT", [128, 16], F32)
            EVr = Sem(nc, c2, tag + "p2ev")
            MMr = Sem(nc, c2, tag + "p2mm")
            STr = Sem(nc, c2, tag + "p2st")
            LDr = Sem(nc, c2, tag + "p2ldr")
            EAr = Sem(nc, c2, tag + "p2ea")
            cc1 = Sem(nc, c2, tag + "cc1")
            GID = [Sem(nc, c2, tag + "gid%d" % i) for i in range(2)]
            LDD = [Sem(nc, c2, tag + "ldd%d" % i) for i in range(2)]
            LD4 = Sem(nc, c2, tag + "ld4")

            def D_(v):
                v.drain()

            with nc.Block() as blk:
                @blk.vector
                def _(v):
                    io = C["iotaE"]
                    L = logits
                    v.tensor_reduce(sm1[...], L[...], AX.X, ALU.max); D_(v)
                    v.tensor_tensor(msk[...], L[...], sm1[...].to_broadcast([128, NT, 8]), ALU.is_equal); D_(v)
                    v.memset(scr[...], BIGF); D_(v)
                    v.copy_predicated(scr[...], msk[...], io[...]); D_(v)
                    v.tensor_reduce(si1[...], scr[...], AX.X, ALU.min); D_(v)
                    v.tensor_tensor(msk[...], io[...], si1[...].to_broadcast([128, NT, 8]), ALU.is_equal); D_(v)
                    v.tensor_copy(ohb[:, :, 0:8], msk[...]); D_(v)
                    v.scalar_tensor_tensor(scr2[...], ohb[:, :, 0:8], -BIGF, L[...],
                                           op0=ALU.mult, op1=ALU.add); D_(v)
                    v.tensor_reduce(sm2[...], scr2[...], AX.X, ALU.max); D_(v)
                    v.tensor_tensor(msk[...], scr2[...], sm2[...].to_broadcast([128, NT, 8]), ALU.is_equal); D_(v)
                    v.memset(scr[...], BIGF); D_(v)
                    v.copy_predicated(scr[...], msk[...], io[...]); D_(v)
                    v.tensor_reduce(si2[...], scr[...], AX.X, ALU.min); D_(v)
                    v.tensor_tensor(msk[...], io[...], si2[...].to_broadcast([128, NT, 8]), ALU.is_equal); D_(v)
                    EV0 = EVr.inc(v.tensor_copy(ohb[:, :, 8:16], msk[...]))
                    v.drain()
                    v.tensor_tensor(sm2[...], sm2[...], sm1[...], ALU.subtract); D_(v)
                    EVr.inc(v.tensor_copy(sm2[...], sm2[...]))
                    v.memset(onv[:, :], 1.0)
                    v.drain()
                    for t in range(NT):
                        v.wait_ge(MMr.h, 2 * t + 1)
                        EVr.inc(v.tensor_copy(Pp[:, t, :], psP[:, :]))
                        v.wait_ge(MMr.h, 2 * t + 2)
                        EVr.inc(v.tensor_copy(Tot2[:, :, t], psT[:, :]))
                    v.drain()
                    for e8 in range(16):
                        v.tensor_tensor_scan(Incl[:, e8, :], onv[:, :], Tot2[:, e8, :],
                                             0.0, ALU.mult, ALU.add)
                    v.drain()
                    v.tensor_tensor(Incl[:, :, :], Incl[:, :, :], Tot2[:, :, :], ALU.subtract)
                    v.drain()
                    v.tensor_tensor(Pf[:, :, :], Pp[:, :, :],
                                    Incl[:, :, :].rearrange("p e t -> p t e"), ALU.add)
                    v.drain()
                    EVr.inc(v.tensor_tensor(tot[0:1, :], r2(Incl[0:1, :, NT - 1:NT]),
                                             r2(Tot2[0:1, :, NT - 1:NT]), ALU.add))
                    v.drain()
                    v.tensor_tensor(scr[...], ohb[:, :, 0:8], Pf[:, :, 0:8], ALU.mult); D_(v)
                    v.tensor_reduce(locP[:, :, 0:1], scr[...], AX.X, ALU.add); D_(v)
                    v.tensor_tensor(scr[...], ohb[:, :, 8:16], Pf[:, :, 8:16], ALU.mult); D_(v)
                    v.tensor_reduce(locP[:, :, 1:2], scr[...], AX.X, ALU.add); D_(v)
                    v.wait_ge(EAr.h, 1)
                    v.tensor_scalar(sm1[...], sm2[...], 1.0, None, op0=ALU.add); D_(v)
                    v.reciprocal(sm1[...], sm1[...]); D_(v)
                    v.tensor_scalar(g01[:, :, 0:1], sm1[...], 0.5, None, op0=ALU.mult); D_(v)
                    v.tensor_tensor(tmp1[...], sm2[...], sm1[...], ALU.mult); D_(v)
                    v.tensor_scalar(g01[:, :, 1:2], tmp1[...], 0.5, None, op0=ALU.mult); D_(v)
                    v.tensor_tensor(totbc[:, :], r2(Incl[:, :, NT - 1:NT]),
                                    r2(Tot2[:, :, NT - 1:NT]), ALU.add)
                    v.drain()
                    v.tensor_tensor(scr[...], ohb[:, :, 8:16],
                                    totbc[:, 0:8].rearrange("p (a e) -> p a e", a=1).to_broadcast([128, NT, 8]),
                                    ALU.mult); D_(v)
                    v.tensor_reduce(tmp1[...], scr[...], AX.X, ALU.add); D_(v)
                    v.tensor_tensor(locP[:, :, 1:2], locP[:, :, 1:2], tmp1[...], ALU.add); D_(v)
                    v.tensor_scalar(sm1[...], locP[:, :, 0:1], -1.0, None, op0=ALU.add); D_(v)
                    v.tensor_scalar(sm2[...], locP[:, :, 1:2], -1.0, None, op0=ALU.add); D_(v)
                    v.tensor_scalar(scr[:, :, 0:1], sm1[...], float(SHARD - 8), None, op0=ALU.min); D_(v)
                    v.tensor_scalar(scr[:, :, 1:2], sm2[...], float(SHARD - 8), None, op0=ALU.min); D_(v)
                    v.scalar_tensor_tensor(scr[:, :, 2:3], si1[...], float(SHARD),
                                           scr[:, :, 0:1], op0=ALU.mult, op1=ALU.add); D_(v)
                    v.scalar_tensor_tensor(scr[:, :, 3:4], si2[...], float(SHARD),
                                           scr[:, :, 1:2], op0=ALU.mult, op1=ALU.add); D_(v)
                    v.tensor_copy(sidx[:, 0:32], r2(scr[:, :, 2:3])); D_(v)
                    v.tensor_copy(sidx[:, 32:64], r2(scr[:, :, 3:4])); D_(v)
                    v.tensor_scalar(scr[:, :, 0:1], sm1[...], float(CAPL - 1), None, op0=ALU.min); D_(v)
                    v.tensor_scalar(scr[:, :, 1:2], sm2[...], float(CAPL - 1), None, op0=ALU.min); D_(v)
                    v.scalar_tensor_tensor(scr[:, :, 2:3], si1[...], float(CAPL),
                                           scr[:, :, 0:1], op0=ALU.mult, op1=ALU.add); D_(v)
                    v.scalar_tensor_tensor(scr[:, :, 3:4], si2[...], float(CAPL),
                                           scr[:, :, 1:2], op0=ALU.mult, op1=ALU.add); D_(v)
                    v.tensor_copy(cidx[:, 0:32], r2(scr[:, :, 2:3])); D_(v)
                    if _DEBUG:
                        v.tensor_copy(dbg_oh_sb[:, :, :], ohb[:, :, :]); D_(v)
                        v.tensor_copy(dbg_pp_sb[:, :, :], Pp[:, :, :]); D_(v)
                    EVr.inc(v.tensor_copy(cidx[:, 32:64], r2(scr[:, :, 3:4])))

                @blk.scalar
                def _(sc):
                    sc.wait_ge(EVr.h, 2)
                    EAr.inc(sc.activation(sm2[...], sm2[...], AF.Exp))

                @blk.tensor
                def _(t_):
                    t_.wait_ge(EVr.h, 1)
                    for t in range(NT):
                        if t > 0:
                            t_.wait_ge(EVr.h, 2 + 2 * (t - 1) + 1)
                        t_.matmul(psP[:, 0:8], C["utri"][:, :], ohb[:, t, 0:8],
                                  start=True, stop=True)
                        ins = t_.matmul(psP[:, 8:16], C["utri"][:, :], ohb[:, t, 8:16],
                                        start=True, stop=True)
                        MMr.inc(ins)
                        if t > 0:
                            t_.wait_ge(EVr.h, 2 + 2 * (t - 1) + 2)
                        t_.matmul(psT[:, 0:8], C["onesm"][:, :], ohb[:, t, 0:8],
                                  start=True, stop=True)
                        ins = t_.matmul(psT[:, 8:16], C["onesm"][:, :], ohb[:, t, 8:16],
                                        start=True, stop=True)
                        MMr.inc(ins)

                @blk.sync
                def _(s):
                    s.wait_ge(EVr.h, 2 + 2 * NT + 1)
                    STr.inc(s.dma_start(out=ap["totd"], in_=tot[:, :]), 16)

            EVr_done = EVr.n
            with ExitStack() as c3:
                lbuf = [sbuf(c3, tag + "lb%d" % i, [128, D], BF16) for i in range(2)]
                with nc.Block() as blk:
                    @blk.gpsimd
                    def _(g):
                        g.wait_ge(ld0.h, P0_LD)
                        g.wait_ge(ldg.h, 80)
                        g.wait_ge(lds.h, 16)
                        g.wait_ge(ldz.h, 144)
                        g.wait_ge(STr.h, 16)
                        cc1.inc(g.collective_compute(
                            "AllGather", ALU.bypass,
                            replica_groups=[list(range(NCORE))],
                            ins=[ap["totd"]], outs=[ap["cnt_all"]]))
                        g.wait_ge(EVr.h, EVr_done)
                        rb = g.to_reg(E * SHARD - 1)
                        for t in range(NT):
                            b = t % 2
                            g.wait_ge(LDD[b].h, 16 * (t // 2 + 1))
                            GID[b].inc(g.indirect_dma_start(
                                out=ap["ldisp"],
                                out_offset=bass.IndirectOffsetOnAxis(sidx[:, t:t + 1], 0),
                                in_=lbuf[b][:, :], in_offset=None,
                                bounds_check=rb, oob_is_err=False), 16)
                            GID[b].inc(g.indirect_dma_start(
                                out=ap["ldisp"],
                                out_offset=bass.IndirectOffsetOnAxis(sidx[:, 32 + t:33 + t], 0),
                                in_=lbuf[b][:, :], in_offset=None,
                                bounds_check=rb, oob_is_err=False), 16)

                    @blk.sync
                    def _(s):
                        for t in range(NT):
                            b = t % 2
                            if t >= 2:
                                s.wait_ge(GID[b].h, 32 * (t // 2))
                            LDD[b].inc(s.dma_start(out=lbuf[b][:, :],
                                                   in_=ap["lnx_my"][128 * t:128 * t + 128, :]), 16)

                with nc.Block() as blk:
                    @blk.gpsimd
                    def _(g):
                        g.wait_ge(GID[0].h, 32 * 16)
                        g.wait_ge(GID[1].h, 32 * 16)
                        with allow_ici_all_to_all():
                            cc2.inc(g.collective_compute(
                                "AllToAll", ALU.bypass,
                                replica_groups=[list(range(NCORE))],
                                ins=[ap["ldisp"]], outs=[ap["ddisp"]]))

                    @blk.sync
                    def _(s):
                        s.wait_ge(cc1.h, 1)
                        LD4.inc(s.dma_start(out=cnt_sb[:, :], in_=ap["cnt_all"]), 16)

            with ExitStack() as c5:
                psq = pst(c5, tag + "psq", [1, 16], F32)
                psq2 = pst(c5, tag + "psq2", [1, 16], F32)
                MM5 = Sem(nc, c5, tag + "mm5")
                EV5 = Sem(nc, c5, tag + "ev5")
                ST5 = Sem(nc, c5, tag + "st5")
                LD5 = Sem(nc, c5, tag + "ld5")
                with nc.Block() as blk:
                    @blk.tensor
                    def _(t_):
                        t_.wait_ge(LD4.h, 16)
                        t_.matmul(psq[:, :], C["prefv"][:, :], cnt_sb[:, :],
                                  start=True, stop=True)
                        MM5.inc(t_.matmul(psq2[:, :], C["ones8"][:, :], cnt_sb[:, :],
                                          start=True, stop=True))

                    @blk.vector
                    def _(v):
                        v.wait_ge(MM5.h, 1)
                        v.tensor_copy(pq[0:1, 0:16], psq[:, :]); D_(v)
                        EV5.inc(v.tensor_copy(pq[0:1, 16:32], psq2[:, :]))
                        v.wait_ge(LD5.h, 32)
                        v.tensor_tensor(scr[...], ohb[:, :, 0:8],
                                        prefbc[:, 0:8].rearrange("p (a e) -> p a e", a=1).to_broadcast([128, NT, 8]),
                                        ALU.mult); D_(v)
                        v.tensor_reduce(tmp1[...], scr[...], AX.X, ALU.add); D_(v)
                        v.tensor_tensor(tmp1[...], tmp1[...], locP[:, :, 0:1], ALU.add); D_(v)
                        v.tensor_scalar(tmp1[...], tmp1[...], -1.0, None, op0=ALU.add); D_(v)
                        v.tensor_scalar(msk[:, :, 0:1], tmp1[...], 8192.0, None, op0=ALU.is_lt); D_(v)
                        v.tensor_copy(tmp1[...], msk[:, :, 0:1]); D_(v)
                        v.tensor_tensor(geff[:, :, 0:1], g01[:, :, 0:1], tmp1[...], ALU.mult); D_(v)
                        v.tensor_tensor(scr[...], ohb[:, :, 8:16],
                                        prefbc[:, 8:16].rearrange("p (a e) -> p a e", a=1).to_broadcast([128, NT, 8]),
                                        ALU.mult); D_(v)
                        v.tensor_reduce(tmp1[...], scr[...], AX.X, ALU.add); D_(v)
                        v.tensor_tensor(scr[...], ohb[:, :, 8:16],
                                        totA[:, 0:8].rearrange("p (a e) -> p a e", a=1).to_broadcast([128, NT, 8]),
                                        ALU.mult); D_(v)
                        v.tensor_reduce(sm1[...], scr[...], AX.X, ALU.add); D_(v)
                        v.tensor_tensor(tmp1[...], tmp1[...], sm1[...], ALU.add); D_(v)
                        v.tensor_tensor(scr[...], ohb[:, :, 8:16],
                                        totbc[:, 0:8].rearrange("p (a e) -> p a e", a=1).to_broadcast([128, NT, 8]),
                                        ALU.mult); D_(v)
                        v.tensor_reduce(sm1[...], scr[...], AX.X, ALU.add); D_(v)
                        v.tensor_tensor(tmp1[...], tmp1[...], sm1[...], ALU.subtract); D_(v)
                        v.tensor_tensor(tmp1[...], tmp1[...], locP[:, :, 1:2], ALU.add); D_(v)
                        v.tensor_scalar(tmp1[...], tmp1[...], -1.0, None, op0=ALU.add); D_(v)
                        v.tensor_scalar(msk[:, :, 0:1], tmp1[...], 8192.0, None, op0=ALU.is_lt); D_(v)
                        v.tensor_copy(tmp1[...], msk[:, :, 0:1]); D_(v)
                        v.tensor_tensor(geff[:, :, 1:2], g01[:, :, 1:2], tmp1[...], ALU.mult)
                        v.drain()
                        EV5.inc(v.tensor_copy(geff[:, 0:1, 0:1], geff[:, 0:1, 0:1]))
                        grdy.inc(v.tensor_copy(sm1[:, 0:1, 0:1], sm1[:, 0:1, 0:1]))

                    @blk.sync
                    def _(s):
                        s.wait_ge(EV5.h, 1)
                        ST5.inc(s.dma_start(out=ap["pq_d"], in_=pq[:, :]), 16)

                    @blk.scalar
                    def _(sc):
                        sc.wait_ge(ST5.h, 16)
                        b0 = bass.AP(_th["pq_d"].ap().tensor, 0, [[0, 128], [1, 16]])
                        b1 = bass.AP(_th["pq_d"].ap().tensor, 16, [[0, 128], [1, 16]])
                        LD5.inc(sc.dma_start(out=prefbc[:, :], in_=b0), 16)
                        LD5.inc(sc.dma_start(out=totA[:, :], in_=b1), 16)

        if _DEBUG and tag == "f1":
            dbs = Sem(nc, fctx, tag + "dbs")
            with nc.Block() as blk:
                @blk.sync
                def _(s):
                    dbs.inc(s.dma_start(out=ap["dbg_rt"][:, 0:32, 0:1].rearrange("p a b -> p (a b)"), in_=r2(geff[:, :, 0:1]), allow_slow_non_contiguous=True), 16)
                    dbs.inc(s.dma_start(out=ap["dbg_rt"][:, 0:32, 1:2].rearrange("p a b -> p (a b)"), in_=r2(geff[:, :, 1:2]), allow_slow_non_contiguous=True), 16)
                    dbs.inc(s.dma_start(out=ap["dbg_rt"][:, 0:64, 2:3].rearrange("p a b -> p (a b)"), in_=sidx[:, :].bitcast(F32), allow_slow_non_contiguous=True), 16)
                    dbs.inc(s.dma_start(out=ap["dbg_rt"][:, 0:64, 3:4].rearrange("p a b -> p (a b)"), in_=cidx[:, :].bitcast(F32), allow_slow_non_contiguous=True), 16)
                    dbs.inc(s.dma_start(out=ap["dbg_rt"][:, 0:32, 4:5].rearrange("p a b -> p (a b)"), in_=r2(logits[:, :, 0:1]), allow_slow_non_contiguous=True), 16)
                    dbs.inc(s.dma_start(out=ap["dbg_lg"], in_=logits[:, :, :]), 16)
                    dbs.inc(s.dma_start(out=ap["dbg_oh"], in_=dbg_oh_sb[:, :, :]), 16)
                    dbs.inc(s.dma_start(out=ap["dbg_pp"], in_=dbg_pp_sb[:, :, :]), 16)
                    s.wait_ge(dbs.h, dbs.n)
        emit_expert(tag, wa, wb, cc2)
        with nc.Block() as blk:
            @blk.gpsimd
            def _(g):
                with allow_ici_all_to_all():
                    cc3.inc(g.collective_compute(
                        "AllToAll", ALU.bypass,
                        replica_groups=[list(range(NCORE))],
                        ins=[ap["eo_my"]], outs=[ap["eo_ret"]]))
        emit_combine(tag, src_x, dst_x, cidx, geff, cc3, grdy, 1, done)
        fctx.close()

    def emit_expert(tag, wa, wb, ccdep):
        with ExitStack() as ce:
            dbuf = [[sbuf(ce, tag + "db%d_%d" % (i, j), [128, D], BF16)
                     for j in range(4)] for i in range(2)]
            xT = sbuf(ce, tag + "xT", [128, 8, 512], BF16)
            hT = sbuf(ce, tag + "hT", [128, 32, 512], BF16)
            obuf = sbuf(ce, tag + "obuf", [128, 4, D], BF16)
            ptT = [pst(ce, tag + "ptT%d" % i, [128, 1024], BF16) for i in range(2)]
            ph = [pst(ce, tag + "ph%d" % i, [128, 512], F32) for i in range(2)]
            po = [pst(ce, tag + "po%d" % i, [128, 512], F32) for i in range(2)]
            LDE = [[Sem(nc, ce, tag + "lde%d_%d" % (i, j)) for j in range(4)]
                   for i in range(2)]
            STE = [Sem(nc, ce, tag + "ste%d" % r) for r in range(4)]
            MME = Sem(nc, ce, tag + "mme")
            EVE = Sem(nc, ce, tag + "eve")
            EAE = Sem(nc, ce, tag + "eae")

            def dd_row(t):
                return (t // 10) * SHARD + (t % 10) * 128

            with nc.Block() as blk:
                @blk.sync
                def _(s):
                    s.wait_ge(ccdep.h, 1)
                    for st in range(NST):
                        i = st % 2
                        for j in range(4):
                            t = 4 * st + j
                            if st >= 2:
                                s.wait_ge(MME.h, 44 * (st - 2) + j + 1)
                            r = dd_row(t)
                            LDE[i][j].inc(s.dma_start(out=dbuf[i][j][:, :],
                                                      in_=ap["ddisp"][r:r + 128, :]), 16)

                @blk.tensor
                def _(t_):
                    for st in range(NST):
                        i = st % 2
                        for j in range(4):
                            t_.wait_ge(LDE[i][j].h, 16 * (st // 2 + 1))
                            if 4 * st + j >= 2:
                                prev = 4 * st + j - 2
                                t_.wait_ge(EVE.h, 12 * (prev // 4) + (prev % 4) + 1)
                            pt = ptT[j % 2]
                            for k in range(8):
                                ins = t_.transpose(pt[:, 128 * k:128 * k + 128],
                                                   dbuf[i][j][:, 128 * k:128 * k + 128],
                                                   C["ident"][:, :])
                            MME.inc(ins)
                        t_.wait_ge(EVE.h, 12 * st + 4)
                        for hg in range(32):
                            gi = 32 * st + hg
                            if gi >= 2:
                                t_.wait_ge(EAE.h, gi - 1)
                            p = ph[hg % 2]
                            for k in range(8):
                                ins = t_.matmul(p[:, :], wa[:, k, 128 * hg:128 * hg + 128],
                                                xT[:, k, :], start=(k == 0), stop=(k == 7))
                            MME.inc(ins)
                        t_.wait_ge(EAE.h, 32 * (st + 1))
                        for rs in range(4):
                            for dh in range(2):
                                m = 8 * st + rs * 2 + dh
                                if m >= 2:
                                    pm = m - 2
                                    t_.wait_ge(EVE.h, 12 * (pm // 8) + 4 + (pm % 8) + 1)
                                p = po[dh]
                                for hg in range(32):
                                    ins = t_.matmul(p[:, :], hT[:, hg, 128 * rs:128 * rs + 128],
                                                    wb[:, hg, 512 * dh:512 * dh + 512],
                                                    start=(hg == 0), stop=(hg == 31))
                                MME.inc(ins)

                @blk.vector
                def _(v):
                    for st in range(NST):
                        for j in range(4):
                            v.wait_ge(MME.h, 44 * st + j + 1)
                            pt = ptT[j % 2]
                            EVE.inc(v.tensor_copy(xT[:, :, 128 * j:128 * j + 128],
                                                  pt[:, :].rearrange("p (k q) -> p k q", q=128)))
                        for rs in range(4):
                            for dh in range(2):
                                m = rs * 2 + dh
                                v.wait_ge(MME.h, 44 * st + 36 + m + 1)
                                if st >= 1:
                                    v.wait_ge(STE[rs].h, 16 * st)
                                EVE.inc(v.tensor_tensor(obuf[:, rs, 512 * dh:512 * dh + 512],
                                                        po[dh][:, :],
                                                        C["b2bc"][:, 512 * dh:512 * dh + 512],
                                                        ALU.add))

                @blk.scalar
                def _(sc):
                    for st in range(NST):
                        for hg in range(32):
                            sc.wait_ge(MME.h, 44 * st + 4 + hg + 1)
                            EAE.inc(sc.activation(hT[:, hg, :], ph[hg % 2][:, :], AF.Relu,
                                                  bias=C["b1t"][:, hg:hg + 1]))

                @blk.gpsimd
                def _(g):
                    for st in range(NST):
                        for rs in range(4):
                            g.wait_ge(EVE.h, 12 * st + 4 + rs * 2 + 2)
                            STE[rs].inc(g.dma_start(
                                out=ap["eo_my"][512 * st + 128 * rs:512 * st + 128 * rs + 128, :],
                                in_=obuf[:, rs, :]), 16)

            with nc.Block() as blk:
                @blk.gpsimd
                def _(g):
                    for rs in range(4):
                        g.wait_ge(STE[rs].h, 16 * NST)

    def emit_combine(tag, src_x, dst_x, cidx, geff, ccdep, gsem, gval, done):
        with ExitStack() as cc_:
            ga = [[sbuf(cc_, tag + "ga%d_%d" % (i, j), [128, D], BF16)
                   for j in range(2)] for i in range(2)]
            xc = [sbuf(cc_, tag + "xc%d" % i, [128, D], F32) for i in range(2)]
            xo = [sbuf(cc_, tag + "xo%d" % i, [128, D], F32) for i in range(2)]
            tmpf = [sbuf(cc_, tag + "ctmp%d" % i, [128, D], F32) for i in range(2)]
            mk1 = sbuf(cc_, tag + "cmk", [1, 2], F32)
            GIC = [Sem(nc, cc_, tag + "gic%d" % i) for i in range(2)]
            LDC = [Sem(nc, cc_, tag + "ldc%d" % i) for i in range(2)]
            EVC = Sem(nc, cc_, tag + "evc")
            STC = [Sem(nc, cc_, tag + "stc%d" % i) for i in range(2)]
            with nc.Block() as blk:
                @blk.gpsimd
                def _(g):
                    g.wait_ge(ccdep.h, 1)
                    rb = g.to_reg(E * CAPL - 1)
                    for t in range(NT):
                        b = t % 2
                        if t >= 2:
                            g.wait_ge(EVC.h, 3 * (t - 2) + 2)
                        GIC[b].inc(g.indirect_dma_start(
                            out=ga[b][0][:, :], out_offset=None,
                            in_=ap["eo_ret"],
                            in_offset=bass.IndirectOffsetOnAxis(cidx[:, t:t + 1], 0),
                            bounds_check=rb, oob_is_err=False), 16)
                        GIC[b].inc(g.indirect_dma_start(
                            out=ga[b][1][:, :], out_offset=None,
                            in_=ap["eo_ret"],
                            in_offset=bass.IndirectOffsetOnAxis(cidx[:, 32 + t:33 + t], 0),
                            bounds_check=rb, oob_is_err=False), 16)

                @blk.sync
                def _(s):
                    for t in range(NT):
                        b = t % 2
                        if t >= 2:
                            s.wait_ge(EVC.h, 3 * (t - 2) + 3)
                        LDC[b].inc(s.dma_start(out=xc[b][:, :],
                                               in_=src_x[128 * t:128 * t + 128, :]), 16)

                @blk.vector
                def _(v):
                    v.wait_ge(gsem.h, gval)
                    for t in range(NT):
                        b = t % 2
                        v.wait_ge(GIC[b].h, 32 * (t // 2 + 1))
                        v.wait_ge(LDC[b].h, 16 * (t // 2 + 1))
                        if t >= 2:
                            v.wait_ge(STC[b].h, 16 * ((t - 2) // 2 + 1))
                        v.tensor_scalar(tmpf[b][:, :], ga[b][0][:, :],
                                        r2(geff[:, t:t + 1, 0:1]), None, op0=ALU.mult)
                        v.drain()
                        EVC.inc(v.scalar_tensor_tensor(tmpf[b][:, :], ga[b][1][:, :],
                                                       r2(geff[:, t:t + 1, 1:2]), tmpf[b][:, :],
                                                       op0=ALU.mult, op1=ALU.add))
                        v.drain()
                        EVC.inc(v.tensor_copy(mk1[0:1, 0:1], xc[b][0:1, 0:1]))
                        EVC.inc(v.tensor_tensor(xo[b][:, :], xc[b][:, :], tmpf[b][:, :], ALU.add))

                @blk.scalar
                def _(sc):
                    for t in range(NT):
                        b = t % 2
                        sc.wait_ge(EVC.h, 3 * t + 3)
                        STC[b].inc(sc.dma_start(out=dst_x[128 * t:128 * t + 128, :],
                                                in_=xo[b][:, :]), 16)
                    sc.wait_ge(STC[0].h, 16 * (NT // 2))
                    sc.wait_ge(STC[1].h, 16 * (NT // 2))
                    done.inc(sc.activation(tmpf[0][0:1, 0:1], tmpf[0][0:1, 0:1], AF.Copy))

    def emit_ln2(tag, src_x, dst_ln, dep, depval, done):
        with ExitStack() as cl:
            xin = [sbuf(cl, tag + "xi%d" % i, [128, D], F32) for i in range(2)]
            lnb = [sbuf(cl, tag + "lo%d" % i, [128, D], BF16) for i in range(2)]
            sqs = sbuf(cl, tag + "sq", [128, D], F32)
            st1 = sbuf(cl, tag + "st", [128, 8], F32)
            LD = [Sem(nc, cl, tag + "ld%d" % i) for i in range(2)]
            ST = [Sem(nc, cl, tag + "st%d" % i) for i in range(2)]
            EV = Sem(nc, cl, tag + "ev")
            EA = Sem(nc, cl, tag + "ea")
            with nc.Block() as blk:
                @blk.sync
                def _(s):
                    s.wait_ge(dep.h, depval)
                    for t in range(NT):
                        b = t % 2
                        if t >= 2:
                            s.wait_ge(EV.h, 2 * (t - 2) + 2)
                        LD[b].inc(s.dma_start(out=xin[b][:, :],
                                              in_=src_x[128 * t:128 * t + 128, :]), 16)
                        if t >= 1:
                            s.wait_ge(EV.h, 2 * (t - 1) + 2)
                            ST[(t - 1) % 2].inc(s.dma_start(
                                out=dst_ln[128 * (t - 1):128 * t, :],
                                in_=lnb[(t - 1) % 2][:, :]), 16)
                    s.wait_ge(EV.h, 2 * (NT - 1) + 2)
                    ST[(NT - 1) % 2].inc(s.dma_start(
                        out=dst_ln[128 * (NT - 1):128 * NT, :],
                        in_=lnb[(NT - 1) % 2][:, :]), 16)

                @blk.vector
                def _(v):
                    for t in range(NT):
                        b = t % 2
                        v.wait_ge(LD[b].h, 16 * (t // 2 + 1))
                        v.tensor_reduce(st1[:, 4 * b:4 * b + 1], xin[b][:, :], AX.X, ALU.add)
                        v.drain()
                        v.wait_ge(EA.h, 2 * t + 1)
                        v.tensor_scalar(st1[:, 4 * b:4 * b + 1], st1[:, 4 * b:4 * b + 1],
                                        1.0 / D, None, op0=ALU.mult)
                        v.drain()
                        v.tensor_tensor(st1[:, 4 * b + 2:4 * b + 3], st1[:, 4 * b:4 * b + 1],
                                        st1[:, 4 * b:4 * b + 1], ALU.mult)
                        v.drain()
                        EV.inc(v.scalar_tensor_tensor(st1[:, 4 * b + 1:4 * b + 2],
                                                      st1[:, 4 * b + 1:4 * b + 2], 1.0 / D,
                                                      st1[:, 4 * b + 2:4 * b + 3],
                                                      op0=ALU.mult, op1=ALU.subtract))
                        v.wait_ge(EA.h, 2 * t + 2)
                        if t >= 2:
                            v.wait_ge(ST[b].h, 16 * ((t - 2) // 2 + 1))
                        EV.inc(v.tensor_scalar(lnb[b][:, :], xin[b][:, :],
                                               st1[:, 4 * b:4 * b + 1],
                                               st1[:, 4 * b + 1:4 * b + 2],
                                               op0=ALU.subtract, op1=ALU.mult))

                @blk.scalar
                def _(sc):
                    for t in range(NT):
                        b = t % 2
                        sc.wait_ge(LD[b].h, 16 * (t // 2 + 1))
                        if t >= 2:
                            sc.wait_ge(EV.h, 2 * (t - 2) + 2)
                        EA.inc(sc.activation(sqs[:, :], xin[b][:, :], AF.Square,
                                             accum_out=st1[:, 4 * b + 1:4 * b + 2]))
                        sc.wait_ge(EV.h, 2 * t + 1)
                        sc.activation(st1[:, 4 * b + 1:4 * b + 2],
                                      st1[:, 4 * b + 1:4 * b + 2], AF.Ln,
                                      bias=C["fconst"][:, 0:1])
                        sc.drain()
                        EA.inc(sc.activation(st1[:, 4 * b + 1:4 * b + 2],
                                             st1[:, 4 * b + 1:4 * b + 2], AF.Exp,
                                             scale=C["fconst"][:, 1:2]))
                    sc.wait_ge(ST[0].h, 16 * (NT // 2))
                    sc.wait_ge(ST[1].h, 16 * (NT // 2))
                    done.inc(sc.activation(st1[0:1, 0:1], st1[0:1, 0:1], AF.Copy))

    def emit_mamba(mdone):
        PAIRS = [[0, 1], [2, 3], [4, 5], [6, 7]]
        mctx = ExitStack()
        woutb = sbuf(mctx, "woutb", [128, 8, D], BF16)
        wxpb = sbuf(mctx, "wxpb", [128, 4, 96], BF16)
        wdtcb = sbuf(mctx, "wdtcb", [64, 512], BF16)
        dtBC = sbuf(mctx, "dtBC", [128, SEQ], F32)
        ldm = Sem(nc, mctx, "ldm")
        lndone = Sem(nc, mctx, "lndone")
        ccM2 = Sem(nc, mctx, "ccM2")
        ccM3 = Sem(nc, mctx, "ccM3")
        ST13 = Sem(nc, mctx, "st13")

        with nc.Block() as blk:
            @blk.gpsimd
            def _(g):
                g.wait_ge(f1done.h, 1)
                ldm.inc(g.dma_start(out=wxpb[:, :, :],
                        in_=ap["wxp"].rearrange("(k p) c -> p k c", p=128)), 16)
                ldm.inc(g.dma_start(out=wdtcb[:, :], in_=ap["wdtc"]), 16)
                ldm.inc(g.dma_start(out=woutb[:, :, :],
                        in_=ap["woutr"].rearrange("(k p) d -> p k d", p=128)), 16)

        emit_ln2("m9", ap["x1_d"], ap["lnx_my"], f1done, 1, lndone)

        with ExitStack() as c11:
            wincb = sbuf(c11, "wincb", [128, 8, D], BF16)
            rT = [sbuf(c11, "m11r%d" % i, [128, 8, 512], BF16) for i in range(2)]
            xzs = [sbuf(c11, "m11x%d" % i, [128, 8, 512], BF16) for i in range(2)]
            pz = [pst(c11, "m11p%d" % i, [128, 512], F32) for i in range(2)]
            LTI = [Sem(nc, c11, "m11lt%d" % i) for i in range(2)]
            STI = [Sem(nc, c11, "m11st%d" % i) for i in range(2)]
            EVI = Sem(nc, c11, "m11ev")
            MMI = Sem(nc, c11, "m11mm")
            ccM1 = Sem(nc, c11, "ccM1")
            with nc.Block() as blk:
                @blk.gpsimd
                def _(g):
                    g.wait_ge(f1done.h, 1)
                    ldm.inc(g.dma_start(out=wincb[:, :, :],
                            in_=ap["winc"].rearrange("(k p) i -> p k i", p=128)), 16)
                    g.wait_ge(lndone.h, 1)
                    ccM1.inc(g.collective_compute(
                        "AllGather", ALU.bypass, replica_groups=PAIRS,
                        ins=[ap["lnx_my"]], outs=[ap["ln2pair"]]))

            with nc.Block() as blk:
                @blk.scalar
                def _(sc):
                    sc.wait_ge(ccM1.h, 1)
                    for tc in range(NTC):
                        b = tc % 2
                        if tc >= 2:
                            sc.wait_ge(MMI.h, 8 * (tc - 1))
                        for k in range(8):
                            LTI[b].inc(sc.dma_start(
                                out=rT[b][:, k, :],
                                in_=ap["ln2pair"][512 * tc:512 * tc + 512,
                                                  128 * k:128 * k + 128],
                                transpose=True), 16)

                @blk.tensor
                def _(t_):
                    t_.wait_ge(ldm.h, 64)
                    for tc in range(NTC):
                        b = tc % 2
                        t_.wait_ge(LTI[b].h, 128 * (tc // 2 + 1))
                        for m in range(8):
                            gidx = 8 * tc + m
                            if gidx >= 2:
                                t_.wait_ge(EVI.h, gidx - 1)
                            for k in range(8):
                                ins = t_.matmul(pz[m % 2][:, :],
                                                wincb[:, k, 128 * m:128 * m + 128],
                                                rT[b][:, k, :],
                                                start=(k == 0), stop=(k == 7))
                            MMI.inc(ins)

                @blk.vector
                def _(v):
                    for tc in range(NTC):
                        b2 = tc % 2
                        for m in range(8):
                            gidx = 8 * tc + m
                            v.wait_ge(MMI.h, gidx + 1)
                            if tc >= 2 and m == 0:
                                v.wait_ge(STI[b2].h, 16 * ((tc - 2) // 2 + 1))
                            EVI.inc(v.tensor_copy(xzs[b2][:, m, :], pz[m % 2][:, :]))

                @blk.sync
                def _(s):
                    for tc in range(NTC):
                        b2 = tc % 2
                        s.wait_ge(EVI.h, 8 * (tc + 1))
                        STI[b2].inc(s.dma_start(
                            out=ap["xz_d"].rearrange("(m p) t -> p m t", p=128)[
                                :, :, 512 * tc:512 * tc + 512],
                            in_=xzs[b2][:, :, :]), 16)
                    s.wait_ge(STI[0].h, 16 * 8)
                    s.wait_ge(STI[1].h, 16 * 8)

        with ExitStack() as c12:
            czin = [sbuf(c12, "m12c%d" % i, [128, 8, 516], BF16) for i in range(2)]
            cf = sbuf(c12, "m12f", [128, 8, 512], F32)
            xhzs = [sbuf(c12, "m12h%d" % i, [128, 8, 512], BF16) for i in range(2)]
            stg = [sbuf(c12, "m12s%d" % i, [128, 512], F32) for i in range(2)]
            px = pst(c12, "m12p", [128, 512], F32)
            LCZ = [Sem(nc, c12, "m12l%d" % i) for i in range(2)]
            STC2 = [Sem(nc, c12, "m12t%d" % i) for i in range(2)]
            STX = [Sem(nc, c12, "m12x%d" % i) for i in range(2)]
            EVC = Sem(nc, c12, "m12ev")
            EAC = Sem(nc, c12, "m12ea")
            EAX = Sem(nc, c12, "m12ax")
            MMX = Sem(nc, c12, "m12mm")
            with nc.Block() as blk:
                @blk.sync
                def _(s):
                    for tc in range(NTC):
                        b = tc % 2
                        if tc >= 2:
                            s.wait_ge(EVC.h, 32 * (tc - 1))
                        xzv = ap["xz_d"].rearrange("(g p) t -> p g t", p=128)
                        if tc == 0:
                            LCZ[b].inc(s.dma_start(out=czin[b][:, :, 1:516],
                                                   in_=xzv[:, :, 0:515]), 16)
                        elif tc == NTC - 1:
                            LCZ[b].inc(s.dma_start(out=czin[b][:, :, 0:513],
                                                   in_=xzv[:, :, 512 * tc - 1:SEQ]), 16)
                        else:
                            LCZ[b].inc(s.dma_start(
                                out=czin[b][:, :, 0:515],
                                in_=xzv[:, :, 512 * tc - 1:512 * tc + 514]), 16)
                        if tc >= 1:
                            bp = (tc - 1) % 2
                            s.wait_ge(EAC.h, 8 * (tc - 1) + 8)
                            STC2[bp].inc(s.dma_start(
                                out=ap["xhc_d"].rearrange("(g p) t -> p g t", p=128)[
                                    :, :, 512 * (tc - 1):512 * tc],
                                in_=xhzs[bp][:, 0:4, :]), 16)
                            STC2[bp].inc(s.dma_start(
                                out=ap["zc_d"].rearrange("(g p) t -> p g t", p=128)[
                                    :, :, 512 * (tc - 1):512 * tc],
                                in_=xhzs[bp][:, 4:8, :]), 16)
                            s.wait_ge(EAX.h, tc)
                            STX[bp].inc(s.dma_start(
                                out=ap["xdbl_my"][0:96, 512 * (tc - 1):512 * tc],
                                in_=stg[bp][0:96, :]), 16)
                    bp = (NTC - 1) % 2
                    s.wait_ge(EAC.h, 8 * NTC)
                    STC2[bp].inc(s.dma_start(
                        out=ap["xhc_d"].rearrange("(g p) t -> p g t", p=128)[
                            :, :, 512 * (NTC - 1):SEQ],
                        in_=xhzs[bp][:, 0:4, :]), 16)
                    STC2[bp].inc(s.dma_start(
                        out=ap["zc_d"].rearrange("(g p) t -> p g t", p=128)[
                            :, :, 512 * (NTC - 1):SEQ],
                        in_=xhzs[bp][:, 4:8, :]), 16)
                    s.wait_ge(EAX.h, NTC)
                    STX[bp].inc(s.dma_start(
                        out=ap["xdbl_my"][0:96, 512 * (NTC - 1):SEQ],
                        in_=stg[bp][0:96, :]), 16)
                    s.wait_ge(STC2[0].h, 16 * 16)
                    s.wait_ge(STC2[1].h, 16 * 16)
                    s.wait_ge(STX[0].h, 16 * 8)
                    s.wait_ge(STX[1].h, 16 * 8)

                @blk.vector
                def _(v):
                    v.memset(czin[0][:, :, 0:1], 0.0)
                    v.memset(czin[1][:, :, 0:1], 0.0)
                    v.memset(stg[0][0:1, 0:2], 0.0)
                    for tc in range(NTC):
                        b = tc % 2
                        v.wait_ge(LCZ[b].h, 16 * (tc // 2 + 1))
                        if tc == NTC - 1:
                            v.memset(czin[b][:, :, 513:515], 0.0)
                        for j in range(4):
                            for gch in range(8):
                                if j == 0:
                                    if tc >= 1:
                                        v.wait_ge(EAC.h, 8 * (tc - 1) + gch + 1)
                                    EVC.inc(v.tensor_scalar(
                                        cf[:, gch, :], czin[b][:, gch, 0:512],
                                        C["ck"][:, gch, 0:1], None, op0=ALU.mult))
                                else:
                                    EVC.inc(v.scalar_tensor_tensor(
                                        cf[:, gch, :], czin[b][:, gch, j:j + 512],
                                        C["ck"][:, gch, j:j + 1], cf[:, gch, :],
                                        op0=ALU.mult, op1=ALU.add))

                @blk.scalar
                def _(sc):
                    for tc in range(NTC):
                        b = tc % 2
                        for gch in range(8):
                            sc.wait_ge(EVC.h, 32 * tc + 25 + gch)
                            if gch == 0 and tc >= 2:
                                sc.wait_ge(STC2[b].h, 32 * ((tc - 2) // 2 + 1))
                                sc.wait_ge(MMX.h, tc - 1)
                            EAC.inc(sc.activation(xhzs[b][:, gch, :], cf[:, gch, :],
                                                  AF.Silu))
                        sc.wait_ge(MMX.h, tc + 1)
                        if tc >= 2:
                            sc.wait_ge(STX[b].h, 16 * ((tc - 2) // 2 + 1))
                        EAX.inc(sc.activation(stg[b][0:96, :], px[0:96, :], AF.Copy))

                @blk.tensor
                def _(t_):
                    t_.wait_ge(ldm.h, 64)
                    for tc in range(NTC):
                        b = tc % 2
                        t_.wait_ge(EAC.h, 8 * tc + 4)
                        if tc >= 1:
                            t_.wait_ge(EAX.h, tc)
                        for k in range(4):
                            ins = t_.matmul(px[0:96, :], wxpb[:, k, :],
                                            xhzs[b][:, k, :],
                                            start=(k == 0), stop=(k == 3))
                        MMX.inc(ins)

            with nc.Block() as blk:
                @blk.gpsimd
                def _(g):
                    g.wait_ge(STX[0].h, 16 * 8)
                    g.wait_ge(STX[1].h, 16 * 8)
                    ccM2.inc(g.collective_compute(
                        "AllGather", ALU.bypass, replica_groups=PAIRS,
                        ins=[ap["xdbl_my"]], outs=[ap["xdbl_pair"]]))

        with ExitStack() as c13:
            L0 = sbuf(c13, "m13a", [128, SEQ], F32)
            L1 = sbuf(c13, "m13b", [128, SEQ], F32)
            LD13 = Sem(nc, c13, "m13l")
            EV13 = Sem(nc, c13, "m13e")
            with nc.Block() as blk:
                @blk.sync
                def _(s):
                    s.wait_ge(ccM2.h, 1)
                    LD13.inc(s.dma_start(out=L0[0:96, :], in_=ap["xdbl_pair"][0:96, :]), 16)
                    LD13.inc(s.dma_start(out=L1[0:96, :], in_=ap["xdbl_pair"][96:192, :]), 16)

                @blk.vector
                def _(v):
                    v.wait_ge(LD13.h, 32)
                    EV13.inc(v.tensor_tensor(dtBC[0:96, :], L0[0:96, :], L1[0:96, :],
                                             ALU.add))

                @blk.gpsimd
                def _(g):
                    g.wait_ge(EV13.h, 1)
                    ST13.inc(g.dma_start(out=ap["bd_d"], in_=dtBC[64:80, :]), 16)
                    ST13.inc(g.dma_start(out=ap["cd_d"], in_=dtBC[80:96, :]), 16)

        VPT, APT, PPT = 117, 80, 80
        ev_dtrb = lambda tc: VPT * tc + 1
        ev_deltaT = lambda tc, m: VPT * tc + 2 + m
        ev_du = lambda tc: VPT * tc + 9
        scan_done = lambda tc, n: VPT * tc + (13 if n == 0 else 6 * n + 11)
        hc_done = lambda tc, n: VPT * tc + (104 if n == 15 else 6 * n + 18)
        ev_ybf = lambda tc: VPT * tc + 109
        ev_evac = lambda tc, m: VPT * tc + 110 + m
        ea_abs = lambda tc, m: APT * tc + 2 * m + 2
        ea_ln = lambda tc, m: APT * tc + 13 + m
        ea_da = lambda tc, n, g: APT * tc + 17 + 4 * n + g
        mm_dt = lambda tc, m: PPT * tc + 1 + m
        mm_yi = lambda tc, g: PPT * tc + 5 + g
        mm_ya = lambda tc, n, g: PPT * tc + 9 + 4 * n + g
        mm_op = lambda tc, m: PPT * tc + 73 + m
        scf_ = lambda fn: scan_done(fn // 16, fn % 16)
        hcf_ = lambda fn: hc_done(fn // 16, fn % 16)
        with ExitStack() as c14:
            xhcb = [sbuf(c14, "m14u%d" % i, [128, 4, 512], BF16) for i in range(2)]
            zcb = [sbuf(c14, "m14z%d" % i, [128, 4, 512], BF16) for i in range(2)]
            dtrb = [sbuf(c14, "m14d%d" % i, [64, 512], BF16) for i in range(2)]
            rbuf = sbuf(c14, "m14r", [128, 4, 512], BF16)
            abuf = sbuf(c14, "m14a", [128, 4, 512], BF16)
            deltaT = sbuf(c14, "m14t", [128, 4, 512], BF16)
            dub = sbuf(c14, "m14w", [128, 4, 512], BF16)
            wb = [sbuf(c14, "m14v%d" % i, [128, 4, 512], F32) for i in range(2)]
            dab = [sbuf(c14, "m14e%d" % i, [128, 4, 512], F32) for i in range(2)]
            hcb = [sbuf(c14, "m14h%d" % i, [128, 4, 512], BF16) for i in range(2)]
            hng = [sbuf(c14, "m14g%d" % i, [128, 4, 512], F32) for i in range(4)]
            Bbc = [sbuf(c14, "m14B%d" % i, [128, 512], BF16) for i in range(2)]
            Cbc = [sbuf(c14, "m14C%d" % i, [128, 512], BF16) for i in range(2)]
            ybfb = [sbuf(c14, "m14q%d" % i, [128, 4, 512], BF16) for i in range(2)]
            ost = [sbuf(c14, "m14o%d" % i, [128, 8, 512], BF16) for i in range(2)]
            carry = sbuf(c14, "m14k", [128, 64], F32)
            psd2 = [pst(c14, "m14pd%d" % i, [128, 512], F32) for i in range(2)]
            pso = [pst(c14, "m14po%d" % i, [128, 512], F32) for i in range(2)]
            psy = [pst(c14, "m14py%d" % i, [128, 512], F32) for i in range(4)]
            LXH = [Sem(nc, c14, "m14lx%d" % i) for i in range(2)]
            LZC = [Sem(nc, c14, "m14lz%d" % i) for i in range(2)]
            LBB = [Sem(nc, c14, "m14lb%d" % i) for i in range(2)]
            LCC = [Sem(nc, c14, "m14lc%d" % i) for i in range(2)]
            STO = [Sem(nc, c14, "m14so%d" % i) for i in range(2)]
            EVS = Sem(nc, c14, "m14ev")
            EAS = Sem(nc, c14, "m14ea")
            MMP = Sem(nc, c14, "m14mm")
            GPW = Sem(nc, c14, "m14gw")
            with nc.Block() as blk:
                @blk.sync
                def _(s):
                    s.wait_ge(ST13.h, 32)
                    for tc in range(NTC):
                        b = tc % 2
                        if tc >= 2:
                            s.wait_ge(EVS.h, ev_du(tc - 2))
                            s.wait_ge(MMP.h, mm_yi(tc - 2, 3))
                        LXH[b].inc(s.dma_start(
                            out=xhcb[b][:, :, :],
                            in_=ap["xhc_d"].rearrange("(g p) t -> p g t", p=128)[
                                :, :, 512 * tc:512 * tc + 512]), 16)
                        if tc >= 2:
                            s.wait_ge(MMP.h, PPT * (tc - 1))
                        LZC[b].inc(s.dma_start(
                            out=zcb[b][:, :, :],
                            in_=ap["zc_d"].rearrange("(g p) t -> p g t", p=128)[
                                :, :, 512 * tc:512 * tc + 512]), 16)
                        for n in range(16):
                            fn = 16 * tc + n
                            if fn >= 2:
                                s.wait_ge(GPW.h, fn - 1)
                            src = bass.AP(_th["bd_d"].ap().tensor,
                                          n * SEQ + 512 * tc, [[0, 128], [1, 512]])
                            LBB[n % 2].inc(s.dma_start(out=Bbc[n % 2][:, :], in_=src), 16)

                @blk.gpsimd
                def _(g):
                    for tc in range(NTC):
                        for n in range(16):
                            fn = 16 * tc + n
                            if n == 0:
                                g.wait_ge(EVS.h, ev_du(tc))
                            g.wait_ge(LBB[n % 2].h, 16 * (8 * tc + n // 2 + 1))
                            if fn >= 2:
                                g.wait_ge(EVS.h, scf_(fn - 2))
                            GPW.inc(g.tensor_tensor(
                                wb[n % 2][:, :, :], dub[:, :, :],
                                Bbc[n % 2][:, :].rearrange(
                                    "p (a t) -> p a t", a=1).to_broadcast([128, 4, 512]),
                                ALU.mult))

                @blk.scalar
                def _(sc):
                    sc.wait_ge(ST13.h, 32)
                    for tc in range(NTC):
                        b = tc % 2
                        if tc >= 1:
                            sc.wait_ge(EVS.h, ev_evac(tc - 1, 7))
                            STO[(tc - 1) % 2].inc(sc.dma_start(
                                out=ap["op_send"].rearrange("(m p) t -> p m t", p=128)[
                                    :, 8 * ((tc - 1) // 8):8 * ((tc - 1) // 8) + 8,
                                    (512 * (tc - 1)) % 4096:(512 * (tc - 1)) % 4096 + 512],
                                in_=ost[(tc - 1) % 2][:, :, :]), 16)
                        for n in range(2):
                            fn = 16 * tc + n
                            if fn >= 2:
                                sc.wait_ge(EVS.h, hcf_(fn - 2))
                            src = bass.AP(_th["cd_d"].ap().tensor,
                                          n * SEQ + 512 * tc, [[0, 128], [1, 512]])
                            LCC[n % 2].inc(sc.dma_start(out=Cbc[n % 2][:, :], in_=src), 16)
                        for m in range(4):
                            sc.wait_ge(MMP.h, mm_dt(tc, m))
                            if m == 0 and tc >= 1:
                                sc.wait_ge(EVS.h, ev_deltaT(tc - 1, 3))
                            EAS.inc(sc.activation(rbuf[:, m, :], psd2[m % 2][:, :],
                                                  AF.Relu, bias=C["bdt2"][:, m:m + 1]))
                            EAS.inc(sc.activation(abuf[:, m, :], psd2[m % 2][:, :],
                                                  AF.Abs, bias=C["bdt2"][:, m:m + 1]))
                        for m in range(4):
                            EAS.inc(sc.activation(abuf[:, m, :], abuf[:, m, :],
                                                  AF.Exp, scale=C["fconst"][:, 2:3]))
                        for m in range(4):
                            EAS.inc(sc.activation(abuf[:, m, :], abuf[:, m, :],
                                                  AF.Ln, bias=C["fconst"][:, 3:4]))
                        for n in range(16):
                            fn = 16 * tc + n
                            for g in range(4):
                                if n == 0:
                                    sc.wait_ge(EVS.h, ev_deltaT(tc, g))
                                if g == 0 and fn >= 2:
                                    sc.wait_ge(EVS.h, scf_(fn - 2))
                                EAS.inc(sc.activation(dab[n % 2][:, g, :],
                                                      deltaT[:, g, :], AF.Exp,
                                                      scale=C["A"][:, g, n:n + 1]))
                            if 1 <= n < 15:
                                sc.wait_ge(EVS.h, hcf_(16 * tc + n - 1))
                                src = bass.AP(_th["cd_d"].ap().tensor,
                                              (n + 1) * SEQ + 512 * tc,
                                              [[0, 128], [1, 512]])
                                LCC[(n + 1) % 2].inc(sc.dma_start(out=Cbc[(n + 1) % 2][:, :],
                                                                  in_=src), 16)
                    sc.wait_ge(EVS.h, ev_evac(NTC - 1, 7))
                    STO[(NTC - 1) % 2].inc(sc.dma_start(
                        out=ap["op_send"].rearrange("(m p) t -> p m t", p=128)[
                            :, 8 * ((NTC - 1) // 8):8 * ((NTC - 1) // 8) + 8,
                            (512 * (NTC - 1)) % 4096:(512 * (NTC - 1)) % 4096 + 512],
                        in_=ost[(NTC - 1) % 2][:, :, :]), 16)
                    sc.wait_ge(STO[0].h, 16 * 8)
                    sc.wait_ge(STO[1].h, 16 * 8)

                @blk.tensor
                def _(t_):
                    t_.wait_ge(ldm.h, 64)
                    for tc in range(NTC):
                        b = tc % 2
                        for m in range(4):
                            q = 4 * tc + m
                            if m == 0:
                                t_.wait_ge(EVS.h, ev_dtrb(tc))
                            if q >= 2:
                                t_.wait_ge(EAS.h, ea_abs((q - 2) // 4, (q - 2) % 4))
                            MMP.inc(t_.matmul(psd2[m % 2][:, :],
                                              wdtcb[0:64, 128 * m:128 * m + 128],
                                              dtrb[b][0:64, :],
                                              start=True, stop=True))
                        for g in range(4):
                            if g == 0:
                                t_.wait_ge(LXH[b].h, 16 * (tc // 2 + 1))
                                if tc >= 1:
                                    t_.wait_ge(EVS.h, ev_ybf(tc - 1))
                            MMP.inc(t_.matmul(psy[g][:, :], C["diagd"][:, g, :],
                                              xhcb[b][:, g, :],
                                              start=True, stop=False))
                        for n in range(16):
                            for g in range(4):
                                if g == 0:
                                    t_.wait_ge(EVS.h, hc_done(tc, n))
                                MMP.inc(t_.matmul(psy[g][:, :], C["ident"][:, :],
                                                  hcb[n % 2][:, g, :],
                                                  start=False, stop=(n == 15)))
                        for m8 in range(8):
                            r = 8 * tc + m8
                            if m8 == 0:
                                t_.wait_ge(EVS.h, ev_ybf(tc))
                                t_.wait_ge(LZC[b].h, 16 * (tc // 2 + 1))
                            if r >= 2:
                                t_.wait_ge(EVS.h, ev_evac((r - 2) // 8, (r - 2) % 8))
                            for k in range(8):
                                rhs = ybfb[b][:, k, :] if k < 4 else zcb[b][:, k - 4, :]
                                ins = t_.matmul(pso[m8 % 2][:, :],
                                                woutb[:, k, 128 * m8:128 * m8 + 128],
                                                rhs, start=(k == 0), stop=(k == 7))
                            MMP.inc(ins)

                @blk.vector
                def _(v):
                    for tc in range(NTC):
                        b = tc % 2
                        if tc >= 2:
                            v.wait_ge(MMP.h, mm_dt(tc - 2, 3))
                        EVS.inc(v.tensor_copy(dtrb[b][0:64, :],
                                              dtBC[0:64, 512 * tc:512 * tc + 512]))
                        for m in range(4):
                            v.wait_ge(EAS.h, ea_ln(tc, m))
                            EVS.inc(v.tensor_tensor(deltaT[:, m, :], rbuf[:, m, :],
                                                    abuf[:, m, :], ALU.add))
                        for g in range(4):
                            if g == 0:
                                v.wait_ge(LXH[b].h, 16 * (tc // 2 + 1))
                            EVS.inc(v.tensor_tensor(dub[:, g, :], deltaT[:, g, :],
                                                    xhcb[b][:, g, :], ALU.mult))
                        for n in range(16):
                            fn = 16 * tc + n
                            for g in range(4):
                                if g == 0:
                                    v.wait_ge(GPW.h, fn + 1)
                                    v.wait_ge(EAS.h, ea_da(tc, n, 3))
                                EVS.inc(v.tensor_tensor_scan(
                                    hng[n % 4][:, g, :], dab[n % 2][:, g, :],
                                    wb[n % 2][:, g, :],
                                    0.0 if tc == 0 else carry[:, 4 * n + g:4 * n + g + 1],
                                    ALU.mult, ALU.add))
                            if n >= 1:
                                j = n - 1
                                fj = 16 * tc + j
                                v.wait_ge(LCC[j % 2].h, 16 * (8 * tc + j // 2 + 1))
                                if fj >= 2:
                                    v.wait_ge(MMP.h,
                                              mm_ya((fj - 2) // 16, (fj - 2) % 16, 3))
                                EVS.inc(v.tensor_tensor(
                                    hcb[j % 2][:, :, :], hng[j % 4][:, :, :],
                                    Cbc[j % 2][:, :].rearrange(
                                        "p (a t) -> p a t", a=1).to_broadcast([128, 4, 512]),
                                    ALU.mult))
                                EVS.inc(v.tensor_copy(
                                    carry[:, 4 * j:4 * j + 4],
                                    r2(hng[j % 4][:, :, 511:512])))
                        v.wait_ge(LCC[15 % 2].h, 16 * (8 * tc + 8))
                        v.wait_ge(MMP.h, mm_ya(tc, 13, 3))
                        EVS.inc(v.tensor_tensor(
                            hcb[15 % 2][:, :, :], hng[15 % 4][:, :, :],
                            Cbc[15 % 2][:, :].rearrange(
                                "p (a t) -> p a t", a=1).to_broadcast([128, 4, 512]),
                            ALU.mult))
                        EVS.inc(v.tensor_copy(carry[:, 60:64],
                                              r2(hng[15 % 4][:, :, 511:512])))
                        for g in range(4):
                            if g == 0:
                                v.wait_ge(MMP.h, mm_ya(tc, 15, 3))
                            EVS.inc(v.tensor_copy(ybfb[b][:, g, :], psy[g][:, :]))
                        for m8 in range(8):
                            v.wait_ge(MMP.h, mm_op(tc, m8))
                            if m8 == 0 and tc >= 2:
                                v.wait_ge(STO[b].h, 16 * ((tc - 2) // 2 + 1))
                            EVS.inc(v.tensor_copy(ost[b][:, m8, :], pso[m8 % 2][:, :]))


            with nc.Block() as blk:
                @blk.gpsimd
                def _(g):
                    g.wait_ge(STO[0].h, 16 * 8)
                    g.wait_ge(STO[1].h, 16 * 8)
                    ccM3.inc(g.collective_compute(
                        "AllGather", ALU.bypass, replica_groups=PAIRS,
                        ins=[ap["op_send"]], outs=[ap["op_pag"]]))

        with ExitStack() as c15:
            la = [sbuf(c15, "m15a%d" % i, [128, TCOR], BF16) for i in range(2)]
            lb = [sbuf(c15, "m15b%d" % i, [128, TCOR], BF16) for i in range(2)]
            mos = [sbuf(c15, "m15m%d" % i, [128, TCOR], BF16) for i in range(2)]
            GI15 = [Sem(nc, c15, "m15g%d" % i) for i in range(2)]
            ST15 = [Sem(nc, c15, "m15s%d" % i) for i in range(2)]
            EV15 = Sem(nc, c15, "m15e")
            with nc.Block() as blk:
                @blk.gpsimd
                def _(g):
                    g.wait_ge(ccM3.h, 1)
                    rb = g.to_reg(4 * D - 1)
                    for m in range(8):
                        b = m % 2
                        if m >= 2:
                            g.wait_ge(EV15.h, m - 1)
                        GI15[b].inc(g.indirect_dma_start(
                            out=la[b][:, :], out_offset=None, in_=ap["op_pag"],
                            in_offset=bass.IndirectOffsetOnAxis(
                                C["mo_rows"][:, 2 * m:2 * m + 1], 0),
                            bounds_check=rb, oob_is_err=False), 16)
                        GI15[b].inc(g.indirect_dma_start(
                            out=lb[b][:, :], out_offset=None, in_=ap["op_pag"],
                            in_offset=bass.IndirectOffsetOnAxis(
                                C["mo_rows"][:, 2 * m + 1:2 * m + 2], 0),
                            bounds_check=rb, oob_is_err=False), 16)

                @blk.vector
                def _(v):
                    for m in range(8):
                        b = m % 2
                        v.wait_ge(GI15[b].h, 32 * (m // 2 + 1))
                        if m >= 2:
                            v.wait_ge(ST15[b].h, 16 * (m // 2))
                        EV15.inc(v.scalar_tensor_tensor(
                            mos[b][:, :], la[b][:, :], C["bout"][:, m:m + 1],
                            lb[b][:, :], op0=ALU.add, op1=ALU.add))

                @blk.sync
                def _(s):
                    for m in range(8):
                        s.wait_ge(EV15.h, m + 1)
                        ST15[m % 2].inc(s.dma_start(
                            out=ap["mo_d"][128 * m:128 * m + 128, :],
                            in_=mos[m % 2][:, :]), 16)
                    s.wait_ge(ST15[0].h, 16 * 4)
                    s.wait_ge(ST15[1].h, 16 * 4)

        with ExitStack() as c16:
            mt = [sbuf(c16, "m16m%d" % i, [128, D], BF16) for i in range(2)]
            x1b = [sbuf(c16, "m16a%d" % i, [128, D], F32) for i in range(2)]
            x2b = [sbuf(c16, "m16b%d" % i, [128, D], F32) for i in range(2)]
            LT16 = [Sem(nc, c16, "m16l%d" % i) for i in range(2)]
            LX16 = [Sem(nc, c16, "m16x%d" % i) for i in range(2)]
            ST16 = [Sem(nc, c16, "m16s%d" % i) for i in range(2)]
            EV16 = Sem(nc, c16, "m16e")
            with nc.Block() as blk:
                @blk.sync
                def _(s):
                    for t in range(NT):
                        b = t % 2
                        if t >= 2:
                            s.wait_ge(EV16.h, t - 1)
                        LT16[b].inc(s.dma_start(
                            out=mt[b][:, :],
                            in_=ap["mo_d"][0:D, 128 * t:128 * t + 128],
                            transpose=True), 16)
                        LX16[b].inc(s.dma_start(
                            out=x1b[b][:, :],
                            in_=ap["x1_d"][128 * t:128 * t + 128, :]), 16)

                @blk.vector
                def _(v):
                    for t in range(NT):
                        b = t % 2
                        v.wait_ge(LT16[b].h, 16 * (t // 2 + 1))
                        v.wait_ge(LX16[b].h, 16 * (t // 2 + 1))
                        if t >= 2:
                            v.wait_ge(ST16[b].h, 16 * ((t - 2) // 2 + 1))
                        EV16.inc(v.tensor_tensor(x2b[b][:, :], x1b[b][:, :],
                                                 mt[b][:, :], ALU.add))

                @blk.scalar
                def _(sc):
                    for t in range(NT):
                        sc.wait_ge(EV16.h, t + 1)
                        ST16[t % 2].inc(sc.dma_start(
                            out=ap["x2_d"][128 * t:128 * t + 128, :],
                            in_=x2b[t % 2][:, :]), 16)
                    sc.wait_ge(ST16[0].h, 16 * (NT // 2))
                    sc.wait_ge(ST16[1].h, 16 * (NT // 2))
                    mdone.inc(sc.activation(x2b[0][0:1, 0:1], x2b[0][0:1, 0:1],
                                            AF.Copy))
        mctx.close()

    f1done = Sem(nc, pctx, "f1done")
    emit_ffn("f1", ap["x_my"], ap["x1_d"], w1s, w2s, f1done)
    f1ctx.close()

    if _UPTO >= 9:
        x2done = Sem(nc, pctx, "x2done")
        emit_mamba(x2done)
        f2ctx = ExitStack()
        w1s2 = sbuf(f2ctx, "w1s2", [128, 8, H], BF16)
        w2s2 = sbuf(f2ctx, "w2s2", [128, 32, D], BF16)
        ldw = Sem(nc, f2ctx, "ldw2")
        with nc.Block() as blk:
            @blk.gpsimd
            def _(g):
                g.wait_ge(x2done.h, 1)
                ldw.inc(g.dma_start(out=w1s2[:, :, :], in_=ap["wbf1_d"]), 16)
                ldw.inc(g.dma_start(out=w2s2[:, :, :], in_=ap["wbf2_d"]), 16)
                g.wait_ge(ldw.h, 32)
        f2done = Sem(nc, pctx, "f2done")
        emit_ffn("f2", ap["x2_d"], ap["y_out"], w1s2, w2s2, f2done, xdep=(x2done, 1))
        f2ctx.close()
    if _DEBUG:
        dbf = Sem(nc, pctx, "dbf")
        with nc.Block() as blk:
            @blk.sync
            def _(s):
                s.wait_ge(f1done.h, 1)
                for src, dst in [("lnx_my", "dbg_lnx"), ("totd", "dbg_tot"),
                                 ("cnt_all", "dbg_cnt"), ("pq_d", "dbg_pq"),
                                 ("ldisp", "dbg_ldisp"), ("ddisp", "dbg_ddisp"),
                                 ("eo_my", "dbg_eo"), ("eo_ret", "dbg_eor")]:
                    dbf.inc(s.dma_start(out=ap[dst], in_=ap[src]), 16)
                s.wait_ge(dbf.h, dbf.n)
    if _UPTO >= 9:
        pass
    else:
        with ExitStack() as cd:
            tb = sbuf(cd, "devcp", [128, D], F32)
            LDz = Sem(nc, cd, "ldz")
            STz = Sem(nc, cd, "stz")
            with nc.Block() as blk:
                @blk.sync
                def _(s):
                    s.wait_ge(f1done.h, 1)
                    for t in range(NT):
                        if t >= 1:
                            s.wait_ge(STz.h, 16 * t)
                        LDz.inc(s.dma_start(out=tb[:, :],
                                            in_=ap["x1_d"][128 * t:128 * t + 128, :]), 16)
                        s.wait_ge(LDz.h, 16 * (t + 1))
                        STz.inc(s.dma_start(out=ap["y_out"][128 * t:128 * t + 128, :],
                                            in_=tb[:, :]), 16)
                    s.wait_ge(STz.h, 16 * NT)

    pctx.close()
    return nc



def _prep_core(c, I):
    b, k = c // 2, c % 2
    xh = np.arange(512 * k, 512 * k + 512)
    f32 = np.float32
    bf = ml_dtypes.bfloat16
    d = {}
    X = np.asarray(I["L0_attn_x"], f32).reshape(32768, D)
    d["x_my"] = np.ascontiguousarray(X[4096 * c:4096 * (c + 1)])
    d["fc1wT"] = np.ascontiguousarray(np.asarray(I["L0_ffn_fc1_w"], f32)[c].T)
    d["fc2w"] = np.ascontiguousarray(np.asarray(I["L0_ffn_fc2_w"], f32)[c])
    d["b1t"] = np.ascontiguousarray(np.asarray(I["L0_ffn_fc1_b"], f32)[c, 0].reshape(32, 128).T)
    d["b2"] = np.ascontiguousarray(np.asarray(I["L0_ffn_fc2_b"], f32)[c, 0].reshape(1, D))
    d["gwT"] = np.ascontiguousarray(np.asarray(I["L0_ffn_gate_w"], f32).T)
    cols = np.concatenate([xh, 1024 + xh])
    d["winc"] = np.ascontiguousarray(np.asarray(I["L0_attn_W_in"], f32)[:, cols])
    d["woutr"] = np.ascontiguousarray(np.asarray(I["L0_attn_W_out"], f32)[cols, :])
    d["wxp"] = np.ascontiguousarray(np.asarray(I["L0_attn_W_xproj"], f32)[xh, :])
    d["wdtc"] = np.ascontiguousarray(np.asarray(I["L0_attn_W_dt"], f32)[:, xh])
    d["bdt"] = np.ascontiguousarray(np.asarray(I["L0_attn_b_dt"], f32)[xh].reshape(4, 128).T)
    d["alog"] = np.ascontiguousarray(
        np.asarray(I["L0_attn_A_log"], f32)[xh].reshape(4, 128, 16).transpose(1, 0, 2))
    d["dvec"] = np.ascontiguousarray(np.asarray(I["L0_attn_D"], f32)[xh].reshape(4, 128).T)
    ckx = np.asarray(I["L0_attn_conv_x_k"], f32)[:, 0, :][:, xh]
    ckz = np.asarray(I["L0_attn_conv_z_k"], f32)[:, 0, :][:, xh]
    ck = np.zeros((128, 8, 4), f32)
    ck[:, 0:4, :] = ckx.T.reshape(4, 128, 4).transpose(1, 0, 2)
    ck[:, 4:8, :] = ckz.T.reshape(4, 128, 4).transpose(1, 0, 2)
    d["ck"] = ck
    d["bout"] = np.ascontiguousarray(np.asarray(I["L0_attn_b_out"], f32).reshape(8, 128).T)
    d["ident"] = np.eye(128, dtype=bf)
    d["utri"] = np.triu(np.ones((128, 128), bf))
    d["ltri"] = np.tril(np.ones((128, 128), bf), -1)
    d["onesm"] = np.ones((128, 128), bf)
    d["iotaE"] = np.broadcast_to(np.arange(E, dtype=f32), (128, NT, E)).copy()
    d["prefv"] = (np.arange(8) < c).astype(f32).reshape(8, 1)
    d["ones8"] = np.ones((8, 1), f32)
    p = np.arange(128)
    d["mam_rows"] = (b * 8192 + (np.arange(64) * 128)[None, :] + p[:, None]).astype(np.int32)
    xr = np.zeros((128, 4), np.int32)
    xr[:, 0] = (2 * b) * 256 + p
    xr[:, 1] = (2 * b + 1) * 256 + p
    xr[:, 2] = (2 * b) * 256 + 128 + p
    xr[:, 3] = (2 * b + 1) * 256 + 128 + p
    d["xdb_rows"] = xr
    mo = np.zeros((128, 16), np.int32)
    for m in range(8):
        for src in range(2):
            mo[:, 2 * m + src] = 2048 * src + 1024 * k + 128 * m + p
    d["mo_rows"] = mo
    dg = np.zeros((128, 4, 128), bf)
    for g_ in range(4):
        dg[np.arange(128), g_, np.arange(128)] = d["dvec"][:, g_].astype(bf)
    d["diagd"] = dg
    m01 = np.zeros((128, 2), f32)
    m01[:, 0] = 1.0 if k == 0 else 0.0
    m01[:, 1] = 1.0 if k == 1 else 0.0
    d["m01"] = m01
    fc = np.zeros((128, 8), f32)
    fc[:, 0] = 1e-5
    fc[:, 1] = -0.5
    fc[:, 2] = -1.0
    fc[:, 3] = 1.0
    fc[:, 4] = 0.5
    fc[:, 6] = 8192.0
    d["fconst"] = fc
    return d


_last_results = None


def kernel(**inputs):
    global _last_results
    nc = build_nc()
    in_maps = [_prep_core(c, inputs) for c in range(NCORE)]
    res = run_bass_kernel_spmd(nc, in_maps, core_ids=list(range(NCORE)))
    _last_results = res.results
    y = np.concatenate([np.asarray(res.results[c]["y_out"]) for c in range(NCORE)], axis=0)
    return y.reshape(4, 8192, 1024).astype(np.float32)


# version 04
# speedup vs baseline: 1.0247x; 1.0110x over previous
import os
os.environ.setdefault("TRN_ALLOW_ICI_ALL_TO_ALL", "1")

import numpy as np
import ml_dtypes
from contextlib import ExitStack

import concourse.bass as bass
import concourse.mybir as mybir
from concourse.bass_utils import run_bass_kernel_spmd
from concourse.aot_env import allow_ici_all_to_all

F32 = mybir.dt.float32
BF16 = mybir.dt.bfloat16
I32 = mybir.dt.int32
U8 = mybir.dt.uint8
AF = mybir.ActivationFunctionType
ALU = mybir.AluOpType
AX = mybir.AxisListType

NCORE = 8
TCOR = 4096
NT = 32
D = 1024
H = 4096
E = 8
CAPL = 1280
SHARD = 1296
NST = 20
SEQ = 8192
TCH = 512
NTC = 16
BIGF = 1.0e9

_UPTO = 99
_DEBUG = False


class _SemScope:

    def __init__(self, nc, ctx):
        self.nc = nc
        self.handles = []
        ctx.callback(self._close)

    def _close(self):
        if self.handles:
            self.nc.all_engine_barrier()
            self.nc.clear_and_free_semaphores(self.handles)


class Sem:
    def __init__(self, nc, ctx, name):
        if not hasattr(ctx, "_semscope"):
            ctx._semscope = _SemScope(nc, ctx)
        self.h = nc.alloc_semaphore(name=name)
        ctx._semscope.handles.append(self.h)
        self.n = 0

    def inc(self, ins, k=1):
        ins.then_inc(self.h, k)
        self.n += k
        return self.n


def r2(t):
    return t.rearrange("p a b -> p (a b)")


def build_nc():
    nc = bass.Bass("TRN2", target_bir_lowering=False, debug=False,
                 detect_race_conditions=False)
    _th = {}

    class _ApDict:
        def __getitem__(self, name):
            return _th[name].ap()

    ap = _ApDict()

    def din(name, shape, dt):
        _th[name] = nc.dram_tensor(name, shape, dt, kind="ExternalInput")

    def dint(name, shape, dt):
        _th[name] = nc.dram_tensor(name, shape, dt)

    din("x_my", [TCOR, D], F32)
    din("fc1wT", [D, H], F32)
    din("fc2w", [H, D], F32)
    din("b1t", [128, 32], F32)
    din("b2", [1, D], F32)
    din("gwT", [D, E], F32)
    din("winc", [D, D], F32)
    din("woutr", [D, D], F32)
    din("wxp", [512, 96], F32)
    din("wdtc", [64, 512], F32)
    din("bdt", [128, 4], F32)
    din("alog", [128, 4, 16], F32)
    din("dvec", [128, 4], F32)
    din("ck", [128, 8, 4], F32)
    din("bout", [128, 8], F32)
    din("ident", [128, 128], BF16)
    din("utri", [128, 128], BF16)
    din("ltri", [128, 128], BF16)
    din("onesm", [128, 128], BF16)
    din("iotaE", [128, NT, E], F32)
    din("prefv", [8, 1], F32)
    din("ones8", [8, 1], F32)
    din("mam_rows", [128, 64], I32)
    din("xdb_rows", [128, 4], I32)
    din("mo_rows", [128, 16], I32)
    din("m01", [128, 2], F32)
    din("diagd", [128, 4, 128], BF16)
    din("fconst", [128, 8], F32)

    _th["y_out"] = nc.dram_tensor("y_out", [TCOR, D], F32, kind="ExternalOutput")

    dint("lnx_my", [TCOR, D], BF16)
    dint("ldisp", [E * SHARD, D], BF16)
    dint("ddisp", [E * SHARD, D], BF16)
    dint("eo_my", [E * CAPL, D], BF16)
    dint("eo_ret", [E * CAPL, D], BF16)
    dint("totd", [1, 16], F32)
    dint("cnt_all", [8, 16], F32)
    dint("pq_d", [1, 32], F32)
    dint("x1_d", [TCOR, D], F32)
    dint("x2_d", [TCOR, D], F32)
    dint("ln2pair", [2 * TCOR, D], BF16)
    dint("xdbl_pair", [192, SEQ], F32)
    dint("xz_d", [D, SEQ], BF16)
    dint("xhc_d", [512, SEQ], BF16)
    dint("zc_d", [512, SEQ], BF16)
    dint("xdbl_my", [96, SEQ], F32)
    dint("bd_d", [16, SEQ], BF16)
    dint("cd_d", [16, SEQ], BF16)
    dint("op_send", [2 * D, TCOR], BF16)
    dint("op_pag", [4 * D, TCOR], BF16)
    dint("mo_d", [D, TCOR], BF16)
    dint("wbf1_d", [128, 8, H], BF16)
    dint("wbf2_d", [128, 32, D], BF16)
    if _DEBUG:
        for nm, shape, dt in [("dbg_lnx", [TCOR, D], BF16), ("dbg_tot", [1, 16], F32),
                              ("dbg_cnt", [8, 16], F32), ("dbg_pq", [1, 32], F32),
                              ("dbg_ldisp", [E * SHARD, D], BF16),
                              ("dbg_ddisp", [E * SHARD, D], BF16),
                              ("dbg_eo", [E * CAPL, D], BF16),
                              ("dbg_eor", [E * CAPL, D], BF16),
                              ("dbg_rt", [128, 64, 8], F32), ("dbg_lg", [128, NT, 8], F32),
                              ("dbg_oh", [128, NT, 16], F32), ("dbg_pp", [128, NT, 16], F32)]:
            _th[nm] = nc.dram_tensor(nm, shape, dt, kind="ExternalOutput")

    pctx = ExitStack()

    def sbuf(ctx, name, shape, dt):
        return ctx.enter_context(nc.sbuf_tensor(name, shape, dt))

    def pst(ctx, name, shape, dt):
        return ctx.enter_context(nc.psum_tensor(name, shape, dt))

    C = {}
    for nm, shape, dt in [("ident", [128, 128], BF16), ("utri", [128, 128], BF16),
                          ("ltri", [128, 128], BF16), ("onesm", [128, 128], BF16),
                          ("iotaE", [128, NT, E], F32),
                          ("prefv", [8, 1], F32), ("ones8", [8, 1], F32),
                          ("b1t", [128, 32], F32), ("m01", [128, 2], F32),
                          ("bout", [128, 8], F32), ("mam_rows", [128, 64], I32),
                          ("xdb_rows", [128, 4], I32), ("mo_rows", [128, 16], I32),
                          ("bdt", [128, 4], F32), ("alog", [128, 4, 16], F32),
                          ("dvec", [128, 4], F32), ("ck", [128, 8, 4], F32),
                          ("diagd", [128, 4, 128], BF16),
                          ("fconst", [128, 8], F32)]:
        C[nm] = sbuf(pctx, "c_" + nm, shape, dt)
    C["b2bc"] = sbuf(pctx, "c_b2bc", [128, D], F32)
    C["gwTb"] = sbuf(pctx, "c_gwTb", [128, 8, 8], BF16)
    C["A"] = sbuf(pctx, "c_A", [128, 4, 16], F32)
    C["bdt2"] = sbuf(pctx, "c_bdt2", [128, 4], F32)

    ld0 = Sem(nc, pctx, "ld0")
    ldg = Sem(nc, pctx, "ldg")
    lds = Sem(nc, pctx, "lds")
    ldz = Sem(nc, pctx, "ldz")
    cm0 = Sem(nc, pctx, "cm0")

    f1ctx = ExitStack()
    w1s = sbuf(f1ctx, "w1s", [128, 8, H], BF16)
    w2s = sbuf(f1ctx, "w2s", [128, 32, D], BF16)

    with ExitStack() as zctx:
        zt = sbuf(zctx, "zt", [128, 10240], BF16)
        with nc.Block() as blk:
            @blk.sync
            def _(s):
                for nm in ["ident", "utri", "ltri", "iotaE", "prefv", "ones8",
                           "b1t", "m01", "bout", "mam_rows", "xdb_rows",
                           "mo_rows", "bdt", "alog", "dvec", "ck", "fconst",
                           "onesm", "diagd"]:
                    ld0.inc(s.dma_start(out=C[nm][...], in_=ap[nm]), 16)

            @blk.scalar
            def _(sc):
                bap = bass.AP(ap["b2"].tensor, 0, [[0, 128], [1, D]])
                lds.inc(sc.dma_start(out=C["b2bc"][:, :], in_=bap), 16)

            @blk.gpsimd
            def _(g):
                ldg.inc(g.dma_start(out=w1s[:, :, :],
                        in_=ap["fc1wT"].rearrange("(k p) h -> p k h", p=128)), 16)
                ldg.inc(g.dma_start(out=w2s[:, :, :],
                        in_=ap["fc2w"].rearrange("(g p) d -> p g d", p=128)), 16)
                ldg.inc(g.dma_start(out=C["gwTb"][:, :, :],
                        in_=ap["gwT"].rearrange("(k p) e -> p k e", p=128)), 16)

            @blk.vector
            def _(v):
                cm0.inc(v.memset(zt[:, :], 0))

        with nc.Block() as blk:
            mk_ld = ld0.n

            @blk.scalar
            def _(sc):
                sc.wait_ge(ld0.h, mk_ld)
                sc.wait_ge(ldg.h, 48)
                cm0.inc(sc.activation(C["A"][...], C["alog"][...], AF.Exp))

            @blk.vector
            def _(v):
                v.wait_ge(cm0.h, 2)
                cm0.inc(v.tensor_scalar(C["A"][...], C["A"][...], -1.0, None, op0=ALU.mult))
                cm0.inc(v.tensor_scalar(C["bdt2"][...], C["bdt"][...], 2.0, None, op0=ALU.mult))

            @blk.sync
            def _(s):
                s.wait_ge(cm0.h, 1)
                rows = E * SHARD
                r0 = 0
                while r0 < rows:
                    n = min(1280, rows - r0)
                    nb = n // 128
                    ldz.inc(s.dma_start(
                        out=ap["ldisp"][r0:r0 + n, :].rearrange("(a p) d -> p a d", p=128),
                        in_=zt[:, 0:nb * 1024].rearrange("p (a d) -> p a d", d=1024)), 16)
                    r0 += n
                s.wait_ge(ld0.h, mk_ld)

            @blk.gpsimd
            def _(g):
                g.wait_ge(ldg.h, 48)
                ldg.inc(g.dma_start(out=ap["wbf1_d"], in_=w1s[:, :, :]), 16)
                ldg.inc(g.dma_start(out=ap["wbf2_d"], in_=w2s[:, :, :]), 16)

    P0_LD = ld0.n

    def emit_ffn(tag, src_x, dst_x, wa, wb, done, xdep=None):
        fctx = ExitStack()
        logits = sbuf(fctx, tag + "logits", [128, NT, 8], F32)
        if _DEBUG:
            dbg_oh_sb = sbuf(fctx, tag + "dbgoh", [128, NT, 16], F32)
            dbg_pp_sb = sbuf(fctx, tag + "dbgpp", [128, NT, 16], F32)
        geff = sbuf(fctx, tag + "geff", [128, NT, 2], F32)
        sidx = sbuf(fctx, tag + "sidx", [128, 64], I32)
        cidx = sbuf(fctx, tag + "cidx", [128, 64], I32)
        grdy = Sem(nc, fctx, tag + "grdy")
        cc2 = Sem(nc, fctx, tag + "cc2")
        cc3 = Sem(nc, fctx, tag + "cc3")

        with ExitStack() as c1:
            xin = [sbuf(c1, tag + "xin%d" % i, [128, D], F32) for i in range(2)]
            lnb = [sbuf(c1, tag + "lnb%d" % i, [128, D], BF16) for i in range(2)]
            sqs = sbuf(c1, tag + "sqs", [128, D], F32)
            st1 = sbuf(c1, tag + "st1", [128, 8], F32)
            lnxT = sbuf(c1, tag + "lnxT", [128, 8, 128], BF16)
            ptA = pst(c1, tag + "ptA", [128, 1024], BF16)
            ptB = pst(c1, tag + "ptB", [128, 1024], BF16)
            psL = pst(c1, tag + "psL", [128, 8], F32)
            LD = [Sem(nc, c1, tag + "p1ld%d" % i) for i in range(2)]
            ST = [Sem(nc, c1, tag + "p1st%d" % i) for i in range(2)]
            EV = Sem(nc, c1, tag + "p1ev")
            EA = Sem(nc, c1, tag + "p1ea")
            MM = Sem(nc, c1, tag + "p1mm")
            with nc.Block() as blk:
                @blk.sync
                def _(s):
                    s.wait_ge(ld0.h, P0_LD)
                    s.wait_ge(ldg.h, 80)
                    s.wait_ge(lds.h, 16)
                    s.wait_ge(ldz.h, 144)
                    if xdep is not None:
                        s.wait_ge(xdep[0].h, xdep[1])
                    for t in range(NT):
                        b = t % 2
                        if t >= 2:
                            s.wait_ge(EV.h, 4 * (t - 2) + 2)
                        LD[b].inc(s.dma_start(out=xin[b][:, :],
                                              in_=src_x[128 * t:128 * t + 128, :]), 16)
                        if t >= 1:
                            s.wait_ge(EV.h, 4 * (t - 1) + 2)
                            ST[(t - 1) % 2].inc(s.dma_start(
                                out=ap["lnx_my"][128 * (t - 1):128 * t, :],
                                in_=lnb[(t - 1) % 2][:, :]), 16)
                    s.wait_ge(EV.h, 4 * (NT - 1) + 2)
                    ST[(NT - 1) % 2].inc(s.dma_start(
                        out=ap["lnx_my"][128 * (NT - 1):128 * NT, :],
                        in_=lnb[(NT - 1) % 2][:, :]), 16)
                    s.wait_ge(ST[0].h, 16 * (NT // 2))
                    s.wait_ge(ST[1].h, 16 * (NT // 2))

                @blk.vector
                def _(v):
                    for t in range(NT):
                        b = t % 2
                        v.wait_ge(LD[b].h, 16 * (t // 2 + 1))
                        v.tensor_reduce(st1[:, 4 * b:4 * b + 1], xin[b][:, :], AX.X, ALU.add)
                        v.drain()
                        v.wait_ge(EA.h, 2 * t + 1)
                        v.tensor_scalar(st1[:, 4 * b:4 * b + 1], st1[:, 4 * b:4 * b + 1],
                                        1.0 / D, None, op0=ALU.mult)
                        v.drain()
                        v.tensor_tensor(st1[:, 4 * b + 2:4 * b + 3], st1[:, 4 * b:4 * b + 1],
                                        st1[:, 4 * b:4 * b + 1], ALU.mult)
                        v.drain()
                        EV.inc(v.scalar_tensor_tensor(st1[:, 4 * b + 1:4 * b + 2],
                                                      st1[:, 4 * b + 1:4 * b + 2], 1.0 / D,
                                                      st1[:, 4 * b + 2:4 * b + 3],
                                                      op0=ALU.mult, op1=ALU.subtract))
                        v.wait_ge(EA.h, 2 * t + 2)
                        if t >= 2:
                            v.wait_ge(MM.h, 2 * (t - 2) + 1)
                            v.wait_ge(ST[b].h, 16 * ((t - 2) // 2 + 1))
                        EV.inc(v.tensor_scalar(lnb[b][:, :], xin[b][:, :],
                                               st1[:, 4 * b:4 * b + 1],
                                               st1[:, 4 * b + 1:4 * b + 2],
                                               op0=ALU.subtract, op1=ALU.mult))
                        v.wait_ge(MM.h, 2 * t + 1)
                        EV.inc(v.tensor_copy(lnxT[:, :, :],
                                             (ptA if b == 0 else ptB)[:, :].rearrange(
                                                 "p (k q) -> p k q", q=128)))
                        v.wait_ge(MM.h, 2 * t + 2)
                        EV.inc(v.tensor_copy(logits[:, t, :], psL[:, :]))

                @blk.scalar
                def _(sc):
                    for t in range(NT):
                        b = t % 2
                        sc.wait_ge(LD[b].h, 16 * (t // 2 + 1))
                        if t >= 2:
                            sc.wait_ge(EV.h, 4 * (t - 2) + 2)
                        EA.inc(sc.activation(sqs[:, :], xin[b][:, :], AF.Square,
                                             accum_out=st1[:, 4 * b + 1:4 * b + 2]))
                        sc.wait_ge(EV.h, 4 * t + 1)
                        sc.activation(st1[:, 4 * b + 1:4 * b + 2],
                                      st1[:, 4 * b + 1:4 * b + 2], AF.Ln,
                                      bias=C["fconst"][:, 0:1])
                        sc.drain()
                        EA.inc(sc.activation(st1[:, 4 * b + 1:4 * b + 2],
                                             st1[:, 4 * b + 1:4 * b + 2], AF.Exp,
                                             scale=C["fconst"][:, 1:2]))

                @blk.tensor
                def _(t_):
                    for t in range(NT):
                        b = t % 2
                        pt = ptA if b == 0 else ptB
                        t_.wait_ge(EV.h, 4 * t + 2)
                        if t >= 2:
                            t_.wait_ge(EV.h, 4 * (t - 2) + 3)
                        for k in range(8):
                            ins = t_.transpose(pt[:, 128 * k:128 * k + 128],
                                               lnb[b][:, 128 * k:128 * k + 128],
                                               C["ident"][:, :])
                        MM.inc(ins)
                        t_.wait_ge(EV.h, 4 * t + 3)
                        for k in range(8):
                            ins = t_.matmul(psL[:, :], lnxT[:, k, :], C["gwTb"][:, k, :],
                                            start=(k == 0), stop=(k == 7))
                        MM.inc(ins)

        with ExitStack() as c2:
            ohb = sbuf(c2, tag + "ohb", [128, NT, 16], BF16)
            locP = sbuf(c2, tag + "locP", [128, NT, 2], F32)
            g01 = sbuf(c2, tag + "g01", [128, NT, 2], F32)
            scr = sbuf(c2, tag + "scr", [128, NT, 8], F32)
            scr2 = sbuf(c2, tag + "scr2", [128, NT, 8], F32)
            msk = sbuf(c2, tag + "msk", [128, NT, 8], U8)
            sm1 = sbuf(c2, tag + "sm1", [128, NT, 1], F32)
            sm2 = sbuf(c2, tag + "sm2", [128, NT, 1], F32)
            si1 = sbuf(c2, tag + "si1", [128, NT, 1], F32)
            si2 = sbuf(c2, tag + "si2", [128, NT, 1], F32)
            Pp = sbuf(c2, tag + "Pp", [128, NT, 16], F32)
            Pf = sbuf(c2, tag + "Pf", [128, NT, 16], F32)
            Tot2 = sbuf(c2, tag + "Tot2", [128, 16, NT], F32)
            Incl = sbuf(c2, tag + "Incl", [128, 16, NT], F32)
            onv = sbuf(c2, tag + "onv", [128, NT], F32)
            tot = sbuf(c2, tag + "tot", [1, 16], F32)
            totbc = sbuf(c2, tag + "totbc", [128, 16], F32)
            tmp1 = sbuf(c2, tag + "tmp1", [128, NT, 1], F32)
            cnt_sb = sbuf(c2, tag + "cnt", [8, 16], F32)
            prefbc = sbuf(c2, tag + "prefbc", [128, 16], F32)
            totA = sbuf(c2, tag + "totA", [128, 16], F32)
            pq = sbuf(c2, tag + "pq", [1, 32], F32)
            psP = pst(c2, tag + "psP", [128, 16], F32)
            psT = pst(c2, tag + "psT", [128, 16], F32)
            EVr = Sem(nc, c2, tag + "p2ev")
            MMr = Sem(nc, c2, tag + "p2mm")
            STr = Sem(nc, c2, tag + "p2st")
            LDr = Sem(nc, c2, tag + "p2ldr")
            EAr = Sem(nc, c2, tag + "p2ea")
            cc1 = Sem(nc, c2, tag + "cc1")
            GID = [Sem(nc, c2, tag + "gid%d" % i) for i in range(2)]
            LDD = [Sem(nc, c2, tag + "ldd%d" % i) for i in range(2)]
            LD4 = Sem(nc, c2, tag + "ld4")

            def D_(v):
                v.drain()

            with nc.Block() as blk:
                @blk.vector
                def _(v):
                    io = C["iotaE"]
                    L = logits
                    v.tensor_reduce(sm1[...], L[...], AX.X, ALU.max); D_(v)
                    v.tensor_tensor(msk[...], L[...], sm1[...].to_broadcast([128, NT, 8]), ALU.is_equal); D_(v)
                    v.memset(scr[...], BIGF); D_(v)
                    v.copy_predicated(scr[...], msk[...], io[...]); D_(v)
                    v.tensor_reduce(si1[...], scr[...], AX.X, ALU.min); D_(v)
                    v.tensor_tensor(msk[...], io[...], si1[...].to_broadcast([128, NT, 8]), ALU.is_equal); D_(v)
                    v.tensor_copy(ohb[:, :, 0:8], msk[...]); D_(v)
                    v.scalar_tensor_tensor(scr2[...], ohb[:, :, 0:8], -BIGF, L[...],
                                           op0=ALU.mult, op1=ALU.add); D_(v)
                    v.tensor_reduce(sm2[...], scr2[...], AX.X, ALU.max); D_(v)
                    v.tensor_tensor(msk[...], scr2[...], sm2[...].to_broadcast([128, NT, 8]), ALU.is_equal); D_(v)
                    v.memset(scr[...], BIGF); D_(v)
                    v.copy_predicated(scr[...], msk[...], io[...]); D_(v)
                    v.tensor_reduce(si2[...], scr[...], AX.X, ALU.min); D_(v)
                    v.tensor_tensor(msk[...], io[...], si2[...].to_broadcast([128, NT, 8]), ALU.is_equal); D_(v)
                    EV0 = EVr.inc(v.tensor_copy(ohb[:, :, 8:16], msk[...]))
                    v.drain()
                    v.tensor_tensor(sm2[...], sm2[...], sm1[...], ALU.subtract); D_(v)
                    EVr.inc(v.tensor_copy(sm2[...], sm2[...]))
                    v.memset(onv[:, :], 1.0)
                    v.drain()
                    for t in range(NT):
                        v.wait_ge(MMr.h, 2 * t + 1)
                        EVr.inc(v.tensor_copy(Pp[:, t, :], psP[:, :]))
                        v.wait_ge(MMr.h, 2 * t + 2)
                        EVr.inc(v.tensor_copy(Tot2[:, :, t], psT[:, :]))
                    v.drain()
                    for e8 in range(16):
                        v.tensor_tensor_scan(Incl[:, e8, :], onv[:, :], Tot2[:, e8, :],
                                             0.0, ALU.mult, ALU.add)
                    v.drain()
                    v.tensor_tensor(Incl[:, :, :], Incl[:, :, :], Tot2[:, :, :], ALU.subtract)
                    v.drain()
                    v.tensor_tensor(Pf[:, :, :], Pp[:, :, :],
                                    Incl[:, :, :].rearrange("p e t -> p t e"), ALU.add)
                    v.drain()
                    EVr.inc(v.tensor_tensor(tot[0:1, :], r2(Incl[0:1, :, NT - 1:NT]),
                                             r2(Tot2[0:1, :, NT - 1:NT]), ALU.add))
                    v.drain()
                    v.tensor_tensor(scr[...], ohb[:, :, 0:8], Pf[:, :, 0:8], ALU.mult); D_(v)
                    v.tensor_reduce(locP[:, :, 0:1], scr[...], AX.X, ALU.add); D_(v)
                    v.tensor_tensor(scr[...], ohb[:, :, 8:16], Pf[:, :, 8:16], ALU.mult); D_(v)
                    v.tensor_reduce(locP[:, :, 1:2], scr[...], AX.X, ALU.add); D_(v)
                    v.wait_ge(EAr.h, 1)
                    v.tensor_scalar(sm1[...], sm2[...], 1.0, None, op0=ALU.add); D_(v)
                    v.reciprocal(sm1[...], sm1[...]); D_(v)
                    v.tensor_scalar(g01[:, :, 0:1], sm1[...], 0.5, None, op0=ALU.mult); D_(v)
                    v.tensor_tensor(tmp1[...], sm2[...], sm1[...], ALU.mult); D_(v)
                    v.tensor_scalar(g01[:, :, 1:2], tmp1[...], 0.5, None, op0=ALU.mult); D_(v)
                    v.tensor_tensor(totbc[:, :], r2(Incl[:, :, NT - 1:NT]),
                                    r2(Tot2[:, :, NT - 1:NT]), ALU.add)
                    v.drain()
                    v.tensor_tensor(scr[...], ohb[:, :, 8:16],
                                    totbc[:, 0:8].rearrange("p (a e) -> p a e", a=1).to_broadcast([128, NT, 8]),
                                    ALU.mult); D_(v)
                    v.tensor_reduce(tmp1[...], scr[...], AX.X, ALU.add); D_(v)
                    v.tensor_tensor(locP[:, :, 1:2], locP[:, :, 1:2], tmp1[...], ALU.add); D_(v)
                    v.tensor_scalar(sm1[...], locP[:, :, 0:1], -1.0, None, op0=ALU.add); D_(v)
                    v.tensor_scalar(sm2[...], locP[:, :, 1:2], -1.0, None, op0=ALU.add); D_(v)
                    v.tensor_scalar(scr[:, :, 0:1], sm1[...], float(SHARD - 8), None, op0=ALU.min); D_(v)
                    v.tensor_scalar(scr[:, :, 1:2], sm2[...], float(SHARD - 8), None, op0=ALU.min); D_(v)
                    v.scalar_tensor_tensor(scr[:, :, 2:3], si1[...], float(SHARD),
                                           scr[:, :, 0:1], op0=ALU.mult, op1=ALU.add); D_(v)
                    v.scalar_tensor_tensor(scr[:, :, 3:4], si2[...], float(SHARD),
                                           scr[:, :, 1:2], op0=ALU.mult, op1=ALU.add); D_(v)
                    v.tensor_copy(sidx[:, 0:32], r2(scr[:, :, 2:3])); D_(v)
                    v.tensor_copy(sidx[:, 32:64], r2(scr[:, :, 3:4])); D_(v)
                    v.tensor_scalar(scr[:, :, 0:1], sm1[...], float(CAPL - 1), None, op0=ALU.min); D_(v)
                    v.tensor_scalar(scr[:, :, 1:2], sm2[...], float(CAPL - 1), None, op0=ALU.min); D_(v)
                    v.scalar_tensor_tensor(scr[:, :, 2:3], si1[...], float(CAPL),
                                           scr[:, :, 0:1], op0=ALU.mult, op1=ALU.add); D_(v)
                    v.scalar_tensor_tensor(scr[:, :, 3:4], si2[...], float(CAPL),
                                           scr[:, :, 1:2], op0=ALU.mult, op1=ALU.add); D_(v)
                    v.tensor_copy(cidx[:, 0:32], r2(scr[:, :, 2:3])); D_(v)
                    if _DEBUG:
                        v.tensor_copy(dbg_oh_sb[:, :, :], ohb[:, :, :]); D_(v)
                        v.tensor_copy(dbg_pp_sb[:, :, :], Pp[:, :, :]); D_(v)
                    EVr.inc(v.tensor_copy(cidx[:, 32:64], r2(scr[:, :, 3:4])))

                @blk.scalar
                def _(sc):
                    sc.wait_ge(EVr.h, 2)
                    EAr.inc(sc.activation(sm2[...], sm2[...], AF.Exp))

                @blk.tensor
                def _(t_):
                    t_.wait_ge(EVr.h, 1)
                    for t in range(NT):
                        if t > 0:
                            t_.wait_ge(EVr.h, 2 + 2 * (t - 1) + 1)
                        t_.matmul(psP[:, 0:8], C["utri"][:, :], ohb[:, t, 0:8],
                                  start=True, stop=True)
                        ins = t_.matmul(psP[:, 8:16], C["utri"][:, :], ohb[:, t, 8:16],
                                        start=True, stop=True)
                        MMr.inc(ins)
                        if t > 0:
                            t_.wait_ge(EVr.h, 2 + 2 * (t - 1) + 2)
                        t_.matmul(psT[:, 0:8], C["onesm"][:, :], ohb[:, t, 0:8],
                                  start=True, stop=True)
                        ins = t_.matmul(psT[:, 8:16], C["onesm"][:, :], ohb[:, t, 8:16],
                                        start=True, stop=True)
                        MMr.inc(ins)

                @blk.sync
                def _(s):
                    s.wait_ge(EVr.h, 2 + 2 * NT + 1)
                    STr.inc(s.dma_start(out=ap["totd"], in_=tot[:, :]), 16)

            EVr_done = EVr.n
            with ExitStack() as c3:
                lbuf = [sbuf(c3, tag + "lb%d" % i, [128, D], BF16) for i in range(2)]
                with nc.Block() as blk:
                    @blk.gpsimd
                    def _(g):
                        g.wait_ge(ld0.h, P0_LD)
                        g.wait_ge(ldg.h, 80)
                        g.wait_ge(lds.h, 16)
                        g.wait_ge(ldz.h, 144)
                        g.wait_ge(STr.h, 16)
                        cc1.inc(g.collective_compute(
                            "AllGather", ALU.bypass,
                            replica_groups=[list(range(NCORE))],
                            ins=[ap["totd"]], outs=[ap["cnt_all"]]))
                        g.wait_ge(EVr.h, EVr_done)
                        rb = g.to_reg(E * SHARD - 1)
                        for t in range(NT):
                            b = t % 2
                            g.wait_ge(LDD[b].h, 16 * (t // 2 + 1))
                            GID[b].inc(g.indirect_dma_start(
                                out=ap["ldisp"],
                                out_offset=bass.IndirectOffsetOnAxis(sidx[:, t:t + 1], 0),
                                in_=lbuf[b][:, :], in_offset=None,
                                bounds_check=rb, oob_is_err=False), 16)
                            GID[b].inc(g.indirect_dma_start(
                                out=ap["ldisp"],
                                out_offset=bass.IndirectOffsetOnAxis(sidx[:, 32 + t:33 + t], 0),
                                in_=lbuf[b][:, :], in_offset=None,
                                bounds_check=rb, oob_is_err=False), 16)

                    @blk.sync
                    def _(s):
                        for t in range(NT):
                            b = t % 2
                            if t >= 2:
                                s.wait_ge(GID[b].h, 32 * (t // 2))
                            LDD[b].inc(s.dma_start(out=lbuf[b][:, :],
                                                   in_=ap["lnx_my"][128 * t:128 * t + 128, :]), 16)

                with nc.Block() as blk:
                    @blk.gpsimd
                    def _(g):
                        g.wait_ge(GID[0].h, 32 * 16)
                        g.wait_ge(GID[1].h, 32 * 16)
                        with allow_ici_all_to_all():
                            cc2.inc(g.collective_compute(
                                "AllToAll", ALU.bypass,
                                replica_groups=[list(range(NCORE))],
                                ins=[ap["ldisp"]], outs=[ap["ddisp"]]))

                    @blk.sync
                    def _(s):
                        s.wait_ge(cc1.h, 1)
                        LD4.inc(s.dma_start(out=cnt_sb[:, :], in_=ap["cnt_all"]), 16)

            with ExitStack() as c5:
                psq = pst(c5, tag + "psq", [1, 16], F32)
                psq2 = pst(c5, tag + "psq2", [1, 16], F32)
                MM5 = Sem(nc, c5, tag + "mm5")
                EV5 = Sem(nc, c5, tag + "ev5")
                ST5 = Sem(nc, c5, tag + "st5")
                LD5 = Sem(nc, c5, tag + "ld5")
                with nc.Block() as blk:
                    @blk.tensor
                    def _(t_):
                        t_.wait_ge(LD4.h, 16)
                        t_.matmul(psq[:, :], C["prefv"][:, :], cnt_sb[:, :],
                                  start=True, stop=True)
                        MM5.inc(t_.matmul(psq2[:, :], C["ones8"][:, :], cnt_sb[:, :],
                                          start=True, stop=True))

                    @blk.vector
                    def _(v):
                        v.wait_ge(MM5.h, 1)
                        v.tensor_copy(pq[0:1, 0:16], psq[:, :]); D_(v)
                        EV5.inc(v.tensor_copy(pq[0:1, 16:32], psq2[:, :]))
                        v.wait_ge(LD5.h, 32)
                        v.tensor_tensor(scr[...], ohb[:, :, 0:8],
                                        prefbc[:, 0:8].rearrange("p (a e) -> p a e", a=1).to_broadcast([128, NT, 8]),
                                        ALU.mult); D_(v)
                        v.tensor_reduce(tmp1[...], scr[...], AX.X, ALU.add); D_(v)
                        v.tensor_tensor(tmp1[...], tmp1[...], locP[:, :, 0:1], ALU.add); D_(v)
                        v.tensor_scalar(tmp1[...], tmp1[...], -1.0, None, op0=ALU.add); D_(v)
                        v.tensor_scalar(msk[:, :, 0:1], tmp1[...], 8192.0, None, op0=ALU.is_lt); D_(v)
                        v.tensor_copy(tmp1[...], msk[:, :, 0:1]); D_(v)
                        v.tensor_tensor(geff[:, :, 0:1], g01[:, :, 0:1], tmp1[...], ALU.mult); D_(v)
                        v.tensor_tensor(scr[...], ohb[:, :, 8:16],
                                        prefbc[:, 8:16].rearrange("p (a e) -> p a e", a=1).to_broadcast([128, NT, 8]),
                                        ALU.mult); D_(v)
                        v.tensor_reduce(tmp1[...], scr[...], AX.X, ALU.add); D_(v)
                        v.tensor_tensor(scr[...], ohb[:, :, 8:16],
                                        totA[:, 0:8].rearrange("p (a e) -> p a e", a=1).to_broadcast([128, NT, 8]),
                                        ALU.mult); D_(v)
                        v.tensor_reduce(sm1[...], scr[...], AX.X, ALU.add); D_(v)
                        v.tensor_tensor(tmp1[...], tmp1[...], sm1[...], ALU.add); D_(v)
                        v.tensor_tensor(scr[...], ohb[:, :, 8:16],
                                        totbc[:, 0:8].rearrange("p (a e) -> p a e", a=1).to_broadcast([128, NT, 8]),
                                        ALU.mult); D_(v)
                        v.tensor_reduce(sm1[...], scr[...], AX.X, ALU.add); D_(v)
                        v.tensor_tensor(tmp1[...], tmp1[...], sm1[...], ALU.subtract); D_(v)
                        v.tensor_tensor(tmp1[...], tmp1[...], locP[:, :, 1:2], ALU.add); D_(v)
                        v.tensor_scalar(tmp1[...], tmp1[...], -1.0, None, op0=ALU.add); D_(v)
                        v.tensor_scalar(msk[:, :, 0:1], tmp1[...], 8192.0, None, op0=ALU.is_lt); D_(v)
                        v.tensor_copy(tmp1[...], msk[:, :, 0:1]); D_(v)
                        v.tensor_tensor(geff[:, :, 1:2], g01[:, :, 1:2], tmp1[...], ALU.mult)
                        v.drain()
                        EV5.inc(v.tensor_copy(geff[:, 0:1, 0:1], geff[:, 0:1, 0:1]))
                        grdy.inc(v.tensor_copy(sm1[:, 0:1, 0:1], sm1[:, 0:1, 0:1]))

                    @blk.sync
                    def _(s):
                        s.wait_ge(EV5.h, 1)
                        ST5.inc(s.dma_start(out=ap["pq_d"], in_=pq[:, :]), 16)

                    @blk.scalar
                    def _(sc):
                        sc.wait_ge(ST5.h, 16)
                        b0 = bass.AP(_th["pq_d"].ap().tensor, 0, [[0, 128], [1, 16]])
                        b1 = bass.AP(_th["pq_d"].ap().tensor, 16, [[0, 128], [1, 16]])
                        LD5.inc(sc.dma_start(out=prefbc[:, :], in_=b0), 16)
                        LD5.inc(sc.dma_start(out=totA[:, :], in_=b1), 16)

        if _DEBUG and tag == "f1":
            dbs = Sem(nc, fctx, tag + "dbs")
            with nc.Block() as blk:
                @blk.sync
                def _(s):
                    dbs.inc(s.dma_start(out=ap["dbg_rt"][:, 0:32, 0:1].rearrange("p a b -> p (a b)"), in_=r2(geff[:, :, 0:1]), allow_slow_non_contiguous=True), 16)
                    dbs.inc(s.dma_start(out=ap["dbg_rt"][:, 0:32, 1:2].rearrange("p a b -> p (a b)"), in_=r2(geff[:, :, 1:2]), allow_slow_non_contiguous=True), 16)
                    dbs.inc(s.dma_start(out=ap["dbg_rt"][:, 0:64, 2:3].rearrange("p a b -> p (a b)"), in_=sidx[:, :].bitcast(F32), allow_slow_non_contiguous=True), 16)
                    dbs.inc(s.dma_start(out=ap["dbg_rt"][:, 0:64, 3:4].rearrange("p a b -> p (a b)"), in_=cidx[:, :].bitcast(F32), allow_slow_non_contiguous=True), 16)
                    dbs.inc(s.dma_start(out=ap["dbg_rt"][:, 0:32, 4:5].rearrange("p a b -> p (a b)"), in_=r2(logits[:, :, 0:1]), allow_slow_non_contiguous=True), 16)
                    dbs.inc(s.dma_start(out=ap["dbg_lg"], in_=logits[:, :, :]), 16)
                    dbs.inc(s.dma_start(out=ap["dbg_oh"], in_=dbg_oh_sb[:, :, :]), 16)
                    dbs.inc(s.dma_start(out=ap["dbg_pp"], in_=dbg_pp_sb[:, :, :]), 16)
                    s.wait_ge(dbs.h, dbs.n)
        emit_expert(tag, wa, wb, cc2)
        with nc.Block() as blk:
            @blk.gpsimd
            def _(g):
                with allow_ici_all_to_all():
                    cc3.inc(g.collective_compute(
                        "AllToAll", ALU.bypass,
                        replica_groups=[list(range(NCORE))],
                        ins=[ap["eo_my"]], outs=[ap["eo_ret"]]))
        emit_combine(tag, src_x, dst_x, cidx, geff, cc3, grdy, 1, done)
        fctx.close()

    def emit_expert(tag, wa, wb, ccdep):
        with ExitStack() as ce:
            dbuf = [[sbuf(ce, tag + "db%d_%d" % (i, j), [128, D], BF16)
                     for j in range(4)] for i in range(2)]
            xT = sbuf(ce, tag + "xT", [128, 8, 512], BF16)
            hT = sbuf(ce, tag + "hT", [128, 32, 512], BF16)
            obuf = sbuf(ce, tag + "obuf", [128, 4, D], BF16)
            ptT = [pst(ce, tag + "ptT%d" % i, [128, 1024], BF16) for i in range(2)]
            ph = [pst(ce, tag + "ph%d" % i, [128, 512], F32) for i in range(2)]
            po = [pst(ce, tag + "po%d" % i, [128, 512], F32) for i in range(2)]
            LDE = [[Sem(nc, ce, tag + "lde%d_%d" % (i, j)) for j in range(4)]
                   for i in range(2)]
            STE = [Sem(nc, ce, tag + "ste%d" % r) for r in range(4)]
            MME = Sem(nc, ce, tag + "mme")
            EVE = Sem(nc, ce, tag + "eve")
            EAE = Sem(nc, ce, tag + "eae")

            def dd_row(t):
                return (t // 10) * SHARD + (t % 10) * 128

            with nc.Block() as blk:
                @blk.sync
                def _(s):
                    s.wait_ge(ccdep.h, 1)
                    for st in range(NST):
                        i = st % 2
                        for j in range(4):
                            t = 4 * st + j
                            if st >= 2:
                                s.wait_ge(MME.h, 44 * (st - 2) + j + 1)
                            r = dd_row(t)
                            LDE[i][j].inc(s.dma_start(out=dbuf[i][j][:, :],
                                                      in_=ap["ddisp"][r:r + 128, :]), 16)

                @blk.tensor
                def _(t_):
                    for st in range(NST):
                        i = st % 2
                        for j in range(4):
                            t_.wait_ge(LDE[i][j].h, 16 * (st // 2 + 1))
                            if 4 * st + j >= 2:
                                prev = 4 * st + j - 2
                                t_.wait_ge(EVE.h, 12 * (prev // 4) + (prev % 4) + 1)
                            pt = ptT[j % 2]
                            for k in range(8):
                                ins = t_.transpose(pt[:, 128 * k:128 * k + 128],
                                                   dbuf[i][j][:, 128 * k:128 * k + 128],
                                                   C["ident"][:, :])
                            MME.inc(ins)
                        t_.wait_ge(EVE.h, 12 * st + 4)
                        for hg in range(32):
                            gi = 32 * st + hg
                            if gi >= 2:
                                t_.wait_ge(EAE.h, gi - 1)
                            p = ph[hg % 2]
                            for k in range(8):
                                ins = t_.matmul(p[:, :], wa[:, k, 128 * hg:128 * hg + 128],
                                                xT[:, k, :], start=(k == 0), stop=(k == 7))
                            MME.inc(ins)
                        t_.wait_ge(EAE.h, 32 * (st + 1))
                        for rs in range(4):
                            for dh in range(2):
                                m = 8 * st + rs * 2 + dh
                                if m >= 2:
                                    pm = m - 2
                                    t_.wait_ge(EVE.h, 12 * (pm // 8) + 4 + (pm % 8) + 1)
                                p = po[dh]
                                for hg in range(32):
                                    ins = t_.matmul(p[:, :], hT[:, hg, 128 * rs:128 * rs + 128],
                                                    wb[:, hg, 512 * dh:512 * dh + 512],
                                                    start=(hg == 0), stop=(hg == 31))
                                MME.inc(ins)

                @blk.vector
                def _(v):
                    for st in range(NST):
                        for j in range(4):
                            v.wait_ge(MME.h, 44 * st + j + 1)
                            pt = ptT[j % 2]
                            EVE.inc(v.tensor_copy(xT[:, :, 128 * j:128 * j + 128],
                                                  pt[:, :].rearrange("p (k q) -> p k q", q=128)))
                        for rs in range(4):
                            for dh in range(2):
                                m = rs * 2 + dh
                                v.wait_ge(MME.h, 44 * st + 36 + m + 1)
                                if st >= 1:
                                    v.wait_ge(STE[rs].h, 16 * st)
                                EVE.inc(v.tensor_tensor(obuf[:, rs, 512 * dh:512 * dh + 512],
                                                        po[dh][:, :],
                                                        C["b2bc"][:, 512 * dh:512 * dh + 512],
                                                        ALU.add))

                @blk.scalar
                def _(sc):
                    for st in range(NST):
                        for hg in range(32):
                            sc.wait_ge(MME.h, 44 * st + 4 + hg + 1)
                            EAE.inc(sc.activation(hT[:, hg, :], ph[hg % 2][:, :], AF.Relu,
                                                  bias=C["b1t"][:, hg:hg + 1]))

                @blk.gpsimd
                def _(g):
                    for st in range(NST):
                        for rs in range(4):
                            g.wait_ge(EVE.h, 12 * st + 4 + rs * 2 + 2)
                            STE[rs].inc(g.dma_start(
                                out=ap["eo_my"][512 * st + 128 * rs:512 * st + 128 * rs + 128, :],
                                in_=obuf[:, rs, :]), 16)

            with nc.Block() as blk:
                @blk.gpsimd
                def _(g):
                    for rs in range(4):
                        g.wait_ge(STE[rs].h, 16 * NST)

    def emit_combine(tag, src_x, dst_x, cidx, geff, ccdep, gsem, gval, done):
        with ExitStack() as cc_:
            ga = [[sbuf(cc_, tag + "ga%d_%d" % (i, j), [128, D], BF16)
                   for j in range(2)] for i in range(2)]
            xc = [sbuf(cc_, tag + "xc%d" % i, [128, D], F32) for i in range(2)]
            xo = [sbuf(cc_, tag + "xo%d" % i, [128, D], F32) for i in range(2)]
            tmpf = [sbuf(cc_, tag + "ctmp%d" % i, [128, D], F32) for i in range(2)]
            mk1 = sbuf(cc_, tag + "cmk", [1, 2], F32)
            GIC = [Sem(nc, cc_, tag + "gic%d" % i) for i in range(2)]
            LDC = [Sem(nc, cc_, tag + "ldc%d" % i) for i in range(2)]
            EVC = Sem(nc, cc_, tag + "evc")
            STC = [Sem(nc, cc_, tag + "stc%d" % i) for i in range(2)]
            with nc.Block() as blk:
                @blk.gpsimd
                def _(g):
                    g.wait_ge(ccdep.h, 1)
                    rb = g.to_reg(E * CAPL - 1)
                    for t in range(NT):
                        b = t % 2
                        if t >= 2:
                            g.wait_ge(EVC.h, 3 * (t - 2) + 2)
                        GIC[b].inc(g.indirect_dma_start(
                            out=ga[b][0][:, :], out_offset=None,
                            in_=ap["eo_ret"],
                            in_offset=bass.IndirectOffsetOnAxis(cidx[:, t:t + 1], 0),
                            bounds_check=rb, oob_is_err=False), 16)
                        GIC[b].inc(g.indirect_dma_start(
                            out=ga[b][1][:, :], out_offset=None,
                            in_=ap["eo_ret"],
                            in_offset=bass.IndirectOffsetOnAxis(cidx[:, 32 + t:33 + t], 0),
                            bounds_check=rb, oob_is_err=False), 16)

                @blk.sync
                def _(s):
                    for t in range(NT):
                        b = t % 2
                        if t >= 2:
                            s.wait_ge(EVC.h, 3 * (t - 2) + 3)
                        LDC[b].inc(s.dma_start(out=xc[b][:, :],
                                               in_=src_x[128 * t:128 * t + 128, :]), 16)

                @blk.vector
                def _(v):
                    v.wait_ge(gsem.h, gval)
                    for t in range(NT):
                        b = t % 2
                        v.wait_ge(GIC[b].h, 32 * (t // 2 + 1))
                        v.wait_ge(LDC[b].h, 16 * (t // 2 + 1))
                        if t >= 2:
                            v.wait_ge(STC[b].h, 16 * ((t - 2) // 2 + 1))
                        v.tensor_scalar(tmpf[b][:, :], ga[b][0][:, :],
                                        r2(geff[:, t:t + 1, 0:1]), None, op0=ALU.mult)
                        v.drain()
                        EVC.inc(v.scalar_tensor_tensor(tmpf[b][:, :], ga[b][1][:, :],
                                                       r2(geff[:, t:t + 1, 1:2]), tmpf[b][:, :],
                                                       op0=ALU.mult, op1=ALU.add))
                        v.drain()
                        EVC.inc(v.tensor_copy(mk1[0:1, 0:1], xc[b][0:1, 0:1]))
                        EVC.inc(v.tensor_tensor(xo[b][:, :], xc[b][:, :], tmpf[b][:, :], ALU.add))

                @blk.scalar
                def _(sc):
                    for t in range(NT):
                        b = t % 2
                        sc.wait_ge(EVC.h, 3 * t + 3)
                        STC[b].inc(sc.dma_start(out=dst_x[128 * t:128 * t + 128, :],
                                                in_=xo[b][:, :]), 16)
                    sc.wait_ge(STC[0].h, 16 * (NT // 2))
                    sc.wait_ge(STC[1].h, 16 * (NT // 2))
                    done.inc(sc.activation(tmpf[0][0:1, 0:1], tmpf[0][0:1, 0:1], AF.Copy))

    def emit_ln2(tag, src_x, dst_ln, dep, depval, done):
        with ExitStack() as cl:
            xin = [sbuf(cl, tag + "xi%d" % i, [128, D], F32) for i in range(2)]
            lnb = [sbuf(cl, tag + "lo%d" % i, [128, D], BF16) for i in range(2)]
            sqs = sbuf(cl, tag + "sq", [128, D], F32)
            st1 = sbuf(cl, tag + "st", [128, 8], F32)
            LD = [Sem(nc, cl, tag + "ld%d" % i) for i in range(2)]
            ST = [Sem(nc, cl, tag + "st%d" % i) for i in range(2)]
            EV = Sem(nc, cl, tag + "ev")
            EA = Sem(nc, cl, tag + "ea")
            with nc.Block() as blk:
                @blk.sync
                def _(s):
                    s.wait_ge(dep.h, depval)
                    for t in range(NT):
                        b = t % 2
                        if t >= 2:
                            s.wait_ge(EV.h, 2 * (t - 2) + 2)
                        LD[b].inc(s.dma_start(out=xin[b][:, :],
                                              in_=src_x[128 * t:128 * t + 128, :]), 16)
                        if t >= 1:
                            s.wait_ge(EV.h, 2 * (t - 1) + 2)
                            ST[(t - 1) % 2].inc(s.dma_start(
                                out=dst_ln[128 * (t - 1):128 * t, :],
                                in_=lnb[(t - 1) % 2][:, :]), 16)
                    s.wait_ge(EV.h, 2 * (NT - 1) + 2)
                    ST[(NT - 1) % 2].inc(s.dma_start(
                        out=dst_ln[128 * (NT - 1):128 * NT, :],
                        in_=lnb[(NT - 1) % 2][:, :]), 16)

                @blk.vector
                def _(v):
                    for t in range(NT):
                        b = t % 2
                        v.wait_ge(LD[b].h, 16 * (t // 2 + 1))
                        v.tensor_reduce(st1[:, 4 * b:4 * b + 1], xin[b][:, :], AX.X, ALU.add)
                        v.drain()
                        v.wait_ge(EA.h, 2 * t + 1)
                        v.tensor_scalar(st1[:, 4 * b:4 * b + 1], st1[:, 4 * b:4 * b + 1],
                                        1.0 / D, None, op0=ALU.mult)
                        v.drain()
                        v.tensor_tensor(st1[:, 4 * b + 2:4 * b + 3], st1[:, 4 * b:4 * b + 1],
                                        st1[:, 4 * b:4 * b + 1], ALU.mult)
                        v.drain()
                        EV.inc(v.scalar_tensor_tensor(st1[:, 4 * b + 1:4 * b + 2],
                                                      st1[:, 4 * b + 1:4 * b + 2], 1.0 / D,
                                                      st1[:, 4 * b + 2:4 * b + 3],
                                                      op0=ALU.mult, op1=ALU.subtract))
                        v.wait_ge(EA.h, 2 * t + 2)
                        if t >= 2:
                            v.wait_ge(ST[b].h, 16 * ((t - 2) // 2 + 1))
                        EV.inc(v.tensor_scalar(lnb[b][:, :], xin[b][:, :],
                                               st1[:, 4 * b:4 * b + 1],
                                               st1[:, 4 * b + 1:4 * b + 2],
                                               op0=ALU.subtract, op1=ALU.mult))

                @blk.scalar
                def _(sc):
                    for t in range(NT):
                        b = t % 2
                        sc.wait_ge(LD[b].h, 16 * (t // 2 + 1))
                        if t >= 2:
                            sc.wait_ge(EV.h, 2 * (t - 2) + 2)
                        EA.inc(sc.activation(sqs[:, :], xin[b][:, :], AF.Square,
                                             accum_out=st1[:, 4 * b + 1:4 * b + 2]))
                        sc.wait_ge(EV.h, 2 * t + 1)
                        sc.activation(st1[:, 4 * b + 1:4 * b + 2],
                                      st1[:, 4 * b + 1:4 * b + 2], AF.Ln,
                                      bias=C["fconst"][:, 0:1])
                        sc.drain()
                        EA.inc(sc.activation(st1[:, 4 * b + 1:4 * b + 2],
                                             st1[:, 4 * b + 1:4 * b + 2], AF.Exp,
                                             scale=C["fconst"][:, 1:2]))
                    sc.wait_ge(ST[0].h, 16 * (NT // 2))
                    sc.wait_ge(ST[1].h, 16 * (NT // 2))
                    done.inc(sc.activation(st1[0:1, 0:1], st1[0:1, 0:1], AF.Copy))

    def emit_mamba(mdone):
        PAIRS = [[0, 1], [2, 3], [4, 5], [6, 7]]
        mctx = ExitStack()
        woutb = sbuf(mctx, "woutb", [128, 8, D], BF16)
        wxpb = sbuf(mctx, "wxpb", [128, 4, 96], BF16)
        wdtcb = sbuf(mctx, "wdtcb", [64, 512], BF16)
        dtBC = sbuf(mctx, "dtBC", [128, SEQ], F32)
        ldm = Sem(nc, mctx, "ldm")
        lndone = Sem(nc, mctx, "lndone")
        ccM2 = Sem(nc, mctx, "ccM2")
        ccM3 = Sem(nc, mctx, "ccM3")
        ST13 = Sem(nc, mctx, "st13")

        with nc.Block() as blk:
            @blk.gpsimd
            def _(g):
                g.wait_ge(f1done.h, 1)
                ldm.inc(g.dma_start(out=wxpb[:, :, :],
                        in_=ap["wxp"].rearrange("(k p) c -> p k c", p=128)), 16)
                ldm.inc(g.dma_start(out=wdtcb[:, :], in_=ap["wdtc"]), 16)
                ldm.inc(g.dma_start(out=woutb[:, :, :],
                        in_=ap["woutr"].rearrange("(k p) d -> p k d", p=128)), 16)

        emit_ln2("m9", ap["x1_d"], ap["lnx_my"], f1done, 1, lndone)

        with ExitStack() as c11:
            wincb = sbuf(c11, "wincb", [128, 8, D], BF16)
            rT = [sbuf(c11, "m11r%d" % i, [128, 8, 512], BF16) for i in range(2)]
            xzs = [sbuf(c11, "m11x%d" % i, [128, 8, 512], BF16) for i in range(2)]
            pz = [pst(c11, "m11p%d" % i, [128, 512], F32) for i in range(2)]
            LTI = [Sem(nc, c11, "m11lt%d" % i) for i in range(2)]
            STI = [Sem(nc, c11, "m11st%d" % i) for i in range(2)]
            EVI = Sem(nc, c11, "m11ev")
            MMI = Sem(nc, c11, "m11mm")
            ccM1 = Sem(nc, c11, "ccM1")
            with nc.Block() as blk:
                @blk.gpsimd
                def _(g):
                    g.wait_ge(f1done.h, 1)
                    ldm.inc(g.dma_start(out=wincb[:, :, :],
                            in_=ap["winc"].rearrange("(k p) i -> p k i", p=128)), 16)
                    g.wait_ge(lndone.h, 1)
                    ccM1.inc(g.collective_compute(
                        "AllGather", ALU.bypass, replica_groups=PAIRS,
                        ins=[ap["lnx_my"]], outs=[ap["ln2pair"]]))

            with nc.Block() as blk:
                @blk.scalar
                def _(sc):
                    sc.wait_ge(ccM1.h, 1)
                    for tc in range(NTC):
                        b = tc % 2
                        if tc >= 2:
                            sc.wait_ge(MMI.h, 8 * (tc - 1))
                        for k in range(8):
                            LTI[b].inc(sc.dma_start(
                                out=rT[b][:, k, :],
                                in_=ap["ln2pair"][512 * tc:512 * tc + 512,
                                                  128 * k:128 * k + 128],
                                transpose=True), 16)

                @blk.tensor
                def _(t_):
                    t_.wait_ge(ldm.h, 64)
                    for tc in range(NTC):
                        b = tc % 2
                        t_.wait_ge(LTI[b].h, 128 * (tc // 2 + 1))
                        for m in range(8):
                            gidx = 8 * tc + m
                            if gidx >= 2:
                                t_.wait_ge(EVI.h, gidx - 1)
                            for k in range(8):
                                ins = t_.matmul(pz[m % 2][:, :],
                                                wincb[:, k, 128 * m:128 * m + 128],
                                                rT[b][:, k, :],
                                                start=(k == 0), stop=(k == 7))
                            MMI.inc(ins)

                @blk.vector
                def _(v):
                    for tc in range(NTC):
                        b2 = tc % 2
                        for m in range(8):
                            gidx = 8 * tc + m
                            v.wait_ge(MMI.h, gidx + 1)
                            if tc >= 2 and m == 0:
                                v.wait_ge(STI[b2].h, 16 * ((tc - 2) // 2 + 1))
                            EVI.inc(v.tensor_copy(xzs[b2][:, m, :], pz[m % 2][:, :]))

                @blk.sync
                def _(s):
                    for tc in range(NTC):
                        b2 = tc % 2
                        s.wait_ge(EVI.h, 8 * (tc + 1))
                        STI[b2].inc(s.dma_start(
                            out=ap["xz_d"].rearrange("(m p) t -> p m t", p=128)[
                                :, :, 512 * tc:512 * tc + 512],
                            in_=xzs[b2][:, :, :]), 16)
                    s.wait_ge(STI[0].h, 16 * 8)
                    s.wait_ge(STI[1].h, 16 * 8)

        with ExitStack() as c12:
            czin = [sbuf(c12, "m12c%d" % i, [128, 8, 516], BF16) for i in range(2)]
            cf = sbuf(c12, "m12f", [128, 8, 512], F32)
            xhzs = [sbuf(c12, "m12h%d" % i, [128, 8, 512], BF16) for i in range(2)]
            stg = [sbuf(c12, "m12s%d" % i, [128, 512], F32) for i in range(2)]
            px = pst(c12, "m12p", [128, 512], F32)
            LCZ = [Sem(nc, c12, "m12l%d" % i) for i in range(2)]
            STC2 = [Sem(nc, c12, "m12t%d" % i) for i in range(2)]
            STX = [Sem(nc, c12, "m12x%d" % i) for i in range(2)]
            EVC = Sem(nc, c12, "m12ev")
            EAC = Sem(nc, c12, "m12ea")
            EAX = Sem(nc, c12, "m12ax")
            MMX = Sem(nc, c12, "m12mm")
            with nc.Block() as blk:
                @blk.sync
                def _(s):
                    for tc in range(NTC):
                        b = tc % 2
                        if tc >= 2:
                            s.wait_ge(EVC.h, 32 * (tc - 1))
                        xzv = ap["xz_d"].rearrange("(g p) t -> p g t", p=128)
                        if tc == 0:
                            LCZ[b].inc(s.dma_start(out=czin[b][:, :, 1:516],
                                                   in_=xzv[:, :, 0:515]), 16)
                        elif tc == NTC - 1:
                            LCZ[b].inc(s.dma_start(out=czin[b][:, :, 0:513],
                                                   in_=xzv[:, :, 512 * tc - 1:SEQ]), 16)
                        else:
                            LCZ[b].inc(s.dma_start(
                                out=czin[b][:, :, 0:515],
                                in_=xzv[:, :, 512 * tc - 1:512 * tc + 514]), 16)
                        if tc >= 1:
                            bp = (tc - 1) % 2
                            s.wait_ge(EAC.h, 8 * (tc - 1) + 8)
                            STC2[bp].inc(s.dma_start(
                                out=ap["xhc_d"].rearrange("(g p) t -> p g t", p=128)[
                                    :, :, 512 * (tc - 1):512 * tc],
                                in_=xhzs[bp][:, 0:4, :]), 16)
                            STC2[bp].inc(s.dma_start(
                                out=ap["zc_d"].rearrange("(g p) t -> p g t", p=128)[
                                    :, :, 512 * (tc - 1):512 * tc],
                                in_=xhzs[bp][:, 4:8, :]), 16)
                            s.wait_ge(EAX.h, tc)
                            STX[bp].inc(s.dma_start(
                                out=ap["xdbl_my"][0:96, 512 * (tc - 1):512 * tc],
                                in_=stg[bp][0:96, :]), 16)
                    bp = (NTC - 1) % 2
                    s.wait_ge(EAC.h, 8 * NTC)
                    STC2[bp].inc(s.dma_start(
                        out=ap["xhc_d"].rearrange("(g p) t -> p g t", p=128)[
                            :, :, 512 * (NTC - 1):SEQ],
                        in_=xhzs[bp][:, 0:4, :]), 16)
                    STC2[bp].inc(s.dma_start(
                        out=ap["zc_d"].rearrange("(g p) t -> p g t", p=128)[
                            :, :, 512 * (NTC - 1):SEQ],
                        in_=xhzs[bp][:, 4:8, :]), 16)
                    s.wait_ge(EAX.h, NTC)
                    STX[bp].inc(s.dma_start(
                        out=ap["xdbl_my"][0:96, 512 * (NTC - 1):SEQ],
                        in_=stg[bp][0:96, :]), 16)
                    s.wait_ge(STC2[0].h, 16 * 16)
                    s.wait_ge(STC2[1].h, 16 * 16)
                    s.wait_ge(STX[0].h, 16 * 8)
                    s.wait_ge(STX[1].h, 16 * 8)

                @blk.vector
                def _(v):
                    v.memset(czin[0][:, :, 0:1], 0.0)
                    v.memset(czin[1][:, :, 0:1], 0.0)
                    v.memset(stg[0][0:1, 0:2], 0.0)
                    for tc in range(NTC):
                        b = tc % 2
                        v.wait_ge(LCZ[b].h, 16 * (tc // 2 + 1))
                        if tc == NTC - 1:
                            v.memset(czin[b][:, :, 513:515], 0.0)
                        for j in range(4):
                            for gch in range(8):
                                if j == 0:
                                    if tc >= 1:
                                        v.wait_ge(EAC.h, 8 * (tc - 1) + gch + 1)
                                    EVC.inc(v.tensor_scalar(
                                        cf[:, gch, :], czin[b][:, gch, 0:512],
                                        C["ck"][:, gch, 0:1], None, op0=ALU.mult))
                                else:
                                    EVC.inc(v.scalar_tensor_tensor(
                                        cf[:, gch, :], czin[b][:, gch, j:j + 512],
                                        C["ck"][:, gch, j:j + 1], cf[:, gch, :],
                                        op0=ALU.mult, op1=ALU.add))

                @blk.scalar
                def _(sc):
                    for tc in range(NTC):
                        b = tc % 2
                        for gch in range(8):
                            sc.wait_ge(EVC.h, 32 * tc + 25 + gch)
                            if gch == 0 and tc >= 2:
                                sc.wait_ge(STC2[b].h, 32 * ((tc - 2) // 2 + 1))
                                sc.wait_ge(MMX.h, tc - 1)
                            EAC.inc(sc.activation(xhzs[b][:, gch, :], cf[:, gch, :],
                                                  AF.Silu))
                        sc.wait_ge(MMX.h, tc + 1)
                        if tc >= 2:
                            sc.wait_ge(STX[b].h, 16 * ((tc - 2) // 2 + 1))
                        EAX.inc(sc.activation(stg[b][0:96, :], px[0:96, :], AF.Copy))

                @blk.tensor
                def _(t_):
                    t_.wait_ge(ldm.h, 64)
                    for tc in range(NTC):
                        b = tc % 2
                        t_.wait_ge(EAC.h, 8 * tc + 4)
                        if tc >= 1:
                            t_.wait_ge(EAX.h, tc)
                        for k in range(4):
                            ins = t_.matmul(px[0:96, :], wxpb[:, k, :],
                                            xhzs[b][:, k, :],
                                            start=(k == 0), stop=(k == 3))
                        MMX.inc(ins)

            with nc.Block() as blk:
                @blk.gpsimd
                def _(g):
                    g.wait_ge(STX[0].h, 16 * 8)
                    g.wait_ge(STX[1].h, 16 * 8)
                    ccM2.inc(g.collective_compute(
                        "AllGather", ALU.bypass, replica_groups=PAIRS,
                        ins=[ap["xdbl_my"]], outs=[ap["xdbl_pair"]]))

        with ExitStack() as c13:
            L0 = sbuf(c13, "m13a", [128, SEQ], F32)
            L1 = sbuf(c13, "m13b", [128, SEQ], F32)
            LD13 = Sem(nc, c13, "m13l")
            EV13 = Sem(nc, c13, "m13e")
            with nc.Block() as blk:
                @blk.sync
                def _(s):
                    s.wait_ge(ccM2.h, 1)
                    LD13.inc(s.dma_start(out=L0[0:96, :], in_=ap["xdbl_pair"][0:96, :]), 16)
                    LD13.inc(s.dma_start(out=L1[0:96, :], in_=ap["xdbl_pair"][96:192, :]), 16)

                @blk.vector
                def _(v):
                    v.wait_ge(LD13.h, 32)
                    EV13.inc(v.tensor_tensor(dtBC[0:96, :], L0[0:96, :], L1[0:96, :],
                                             ALU.add))

                @blk.gpsimd
                def _(g):
                    g.wait_ge(EV13.h, 1)
                    ST13.inc(g.dma_start(out=ap["bd_d"], in_=dtBC[64:80, :]), 16)
                    ST13.inc(g.dma_start(out=ap["cd_d"], in_=dtBC[80:96, :]), 16)

        VPT, APT, PPT = 117, 80, 80
        ev_dtrb = lambda tc: VPT * tc + 1
        ev_deltaT = lambda tc, m: VPT * tc + 2 + m
        ev_du = lambda tc: VPT * tc + 9
        scan_done = lambda tc, n: VPT * tc + (13 if n == 0 else 6 * n + 11)
        hc_done = lambda tc, n: VPT * tc + (104 if n == 15 else 6 * n + 18)
        ev_ybf = lambda tc: VPT * tc + 109
        ev_evac = lambda tc, m: VPT * tc + 110 + m
        ea_abs = lambda tc, m: APT * tc + 2 * m + 2
        ea_ln = lambda tc, m: APT * tc + 13 + m
        ea_da = lambda tc, n, g: APT * tc + 17 + 4 * n + g
        mm_dt = lambda tc, m: PPT * tc + 1 + m
        mm_yi = lambda tc, g: PPT * tc + 5 + g
        mm_ya = lambda tc, n, g: PPT * tc + 9 + 4 * n + g
        mm_op = lambda tc, m: PPT * tc + 73 + m
        scf_ = lambda fn: scan_done(fn // 16, fn % 16)
        hcf_ = lambda fn: hc_done(fn // 16, fn % 16)
        with ExitStack() as c14:
            xhcb = [sbuf(c14, "m14u%d" % i, [128, 4, 512], BF16) for i in range(2)]
            zcb = [sbuf(c14, "m14z%d" % i, [128, 4, 512], BF16) for i in range(2)]
            dtrb = [sbuf(c14, "m14d%d" % i, [64, 512], BF16) for i in range(2)]
            rbuf = sbuf(c14, "m14r", [128, 4, 512], BF16)
            abuf = sbuf(c14, "m14a", [128, 4, 512], BF16)
            deltaT = sbuf(c14, "m14t", [128, 4, 512], BF16)
            dub = sbuf(c14, "m14w", [128, 4, 512], BF16)
            wb = [sbuf(c14, "m14v%d" % i, [128, 4, 512], F32) for i in range(2)]
            dab = [sbuf(c14, "m14e%d" % i, [128, 4, 512], F32) for i in range(2)]
            hcb = [sbuf(c14, "m14h%d" % i, [128, 4, 512], BF16) for i in range(2)]
            hng = [sbuf(c14, "m14g%d" % i, [128, 4, 512], F32) for i in range(4)]
            Bbc = [sbuf(c14, "m14B%d" % i, [128, 512], BF16) for i in range(2)]
            Cbc = [sbuf(c14, "m14C%d" % i, [128, 512], BF16) for i in range(2)]
            ybfb = [sbuf(c14, "m14q%d" % i, [128, 4, 512], BF16) for i in range(2)]
            ost = [sbuf(c14, "m14o%d" % i, [128, 8, 512], BF16) for i in range(2)]
            carry = sbuf(c14, "m14k", [128, 64], F32)
            psd2 = [pst(c14, "m14pd%d" % i, [128, 512], F32) for i in range(2)]
            pso = [pst(c14, "m14po%d" % i, [128, 512], F32) for i in range(2)]
            psy = [pst(c14, "m14py%d" % i, [128, 512], F32) for i in range(4)]
            LXH = [Sem(nc, c14, "m14lx%d" % i) for i in range(2)]
            LZC = [Sem(nc, c14, "m14lz%d" % i) for i in range(2)]
            LBB = [Sem(nc, c14, "m14lb%d" % i) for i in range(2)]
            LCC = [Sem(nc, c14, "m14lc%d" % i) for i in range(2)]
            STO = [Sem(nc, c14, "m14so%d" % i) for i in range(2)]
            EVS = Sem(nc, c14, "m14ev")
            EAS = Sem(nc, c14, "m14ea")
            MMP = Sem(nc, c14, "m14mm")
            GPW = Sem(nc, c14, "m14gw")
            with nc.Block() as blk:
                @blk.sync
                def _(s):
                    s.wait_ge(ST13.h, 32)
                    for tc in range(NTC):
                        b = tc % 2
                        if tc >= 2:
                            s.wait_ge(EVS.h, ev_du(tc - 2))
                            s.wait_ge(MMP.h, mm_yi(tc - 2, 3))
                        LXH[b].inc(s.dma_start(
                            out=xhcb[b][:, :, :],
                            in_=ap["xhc_d"].rearrange("(g p) t -> p g t", p=128)[
                                :, :, 512 * tc:512 * tc + 512]), 16)
                        if tc >= 2:
                            s.wait_ge(MMP.h, PPT * (tc - 1))
                        LZC[b].inc(s.dma_start(
                            out=zcb[b][:, :, :],
                            in_=ap["zc_d"].rearrange("(g p) t -> p g t", p=128)[
                                :, :, 512 * tc:512 * tc + 512]), 16)
                        for n in range(16):
                            fn = 16 * tc + n
                            if fn >= 2:
                                s.wait_ge(GPW.h, fn - 1)
                            src = bass.AP(_th["bd_d"].ap().tensor,
                                          n * SEQ + 512 * tc, [[0, 128], [1, 512]])
                            LBB[n % 2].inc(s.dma_start(out=Bbc[n % 2][:, :], in_=src), 16)

                @blk.gpsimd
                def _(g):
                    for tc in range(NTC):
                        if tc == 9:
                            g.wait_ge(STO[0].h, 16 * 4)
                            g.wait_ge(STO[1].h, 16 * 4)
                            ccM3.inc(g.collective_compute(
                                "AllGather", ALU.bypass, replica_groups=PAIRS,
                                ins=[ap["op_send"][0:1024, :]],
                                outs=[ap["op_pag"][0:2048, :]]))
                        for n in range(16):
                            fn = 16 * tc + n
                            if n == 0:
                                g.wait_ge(EVS.h, ev_du(tc))
                            g.wait_ge(LBB[n % 2].h, 16 * (8 * tc + n // 2 + 1))
                            if fn >= 2:
                                g.wait_ge(EVS.h, scf_(fn - 2))
                            GPW.inc(g.tensor_tensor(
                                wb[n % 2][:, :, :], dub[:, :, :],
                                Bbc[n % 2][:, :].rearrange(
                                    "p (a t) -> p a t", a=1).to_broadcast([128, 4, 512]),
                                ALU.mult))

                @blk.scalar
                def _(sc):
                    sc.wait_ge(ST13.h, 32)
                    for tc in range(NTC):
                        b = tc % 2
                        if tc >= 1:
                            sc.wait_ge(EVS.h, ev_evac(tc - 1, 7))
                            STO[(tc - 1) % 2].inc(sc.dma_start(
                                out=ap["op_send"].rearrange("(m p) t -> p m t", p=128)[
                                    :, 8 * ((tc - 1) // 8):8 * ((tc - 1) // 8) + 8,
                                    (512 * (tc - 1)) % 4096:(512 * (tc - 1)) % 4096 + 512],
                                in_=ost[(tc - 1) % 2][:, :, :]), 16)
                        for n in range(2):
                            fn = 16 * tc + n
                            if fn >= 2:
                                sc.wait_ge(EVS.h, hcf_(fn - 2))
                            src = bass.AP(_th["cd_d"].ap().tensor,
                                          n * SEQ + 512 * tc, [[0, 128], [1, 512]])
                            LCC[n % 2].inc(sc.dma_start(out=Cbc[n % 2][:, :], in_=src), 16)
                        for m in range(4):
                            sc.wait_ge(MMP.h, mm_dt(tc, m))
                            if m == 0 and tc >= 1:
                                sc.wait_ge(EVS.h, ev_deltaT(tc - 1, 3))
                            EAS.inc(sc.activation(rbuf[:, m, :], psd2[m % 2][:, :],
                                                  AF.Relu, bias=C["bdt2"][:, m:m + 1]))
                            EAS.inc(sc.activation(abuf[:, m, :], psd2[m % 2][:, :],
                                                  AF.Abs, bias=C["bdt2"][:, m:m + 1]))
                        for m in range(4):
                            EAS.inc(sc.activation(abuf[:, m, :], abuf[:, m, :],
                                                  AF.Exp, scale=C["fconst"][:, 2:3]))
                        for m in range(4):
                            EAS.inc(sc.activation(abuf[:, m, :], abuf[:, m, :],
                                                  AF.Ln, bias=C["fconst"][:, 3:4]))
                        for n in range(16):
                            fn = 16 * tc + n
                            for g in range(4):
                                if n == 0:
                                    sc.wait_ge(EVS.h, ev_deltaT(tc, g))
                                if g == 0 and fn >= 2:
                                    sc.wait_ge(EVS.h, scf_(fn - 2))
                                EAS.inc(sc.activation(dab[n % 2][:, g, :],
                                                      deltaT[:, g, :], AF.Exp,
                                                      scale=C["A"][:, g, n:n + 1]))
                            if 1 <= n < 15:
                                sc.wait_ge(EVS.h, hcf_(16 * tc + n - 1))
                                src = bass.AP(_th["cd_d"].ap().tensor,
                                              (n + 1) * SEQ + 512 * tc,
                                              [[0, 128], [1, 512]])
                                LCC[(n + 1) % 2].inc(sc.dma_start(out=Cbc[(n + 1) % 2][:, :],
                                                                  in_=src), 16)
                    sc.wait_ge(EVS.h, ev_evac(NTC - 1, 7))
                    STO[(NTC - 1) % 2].inc(sc.dma_start(
                        out=ap["op_send"].rearrange("(m p) t -> p m t", p=128)[
                            :, 8 * ((NTC - 1) // 8):8 * ((NTC - 1) // 8) + 8,
                            (512 * (NTC - 1)) % 4096:(512 * (NTC - 1)) % 4096 + 512],
                        in_=ost[(NTC - 1) % 2][:, :, :]), 16)
                    sc.wait_ge(STO[0].h, 16 * 8)
                    sc.wait_ge(STO[1].h, 16 * 8)

                @blk.tensor
                def _(t_):
                    t_.wait_ge(ldm.h, 64)
                    for tc in range(NTC):
                        b = tc % 2
                        for m in range(4):
                            q = 4 * tc + m
                            if m == 0:
                                t_.wait_ge(EVS.h, ev_dtrb(tc))
                            if q >= 2:
                                t_.wait_ge(EAS.h, ea_abs((q - 2) // 4, (q - 2) % 4))
                            MMP.inc(t_.matmul(psd2[m % 2][:, :],
                                              wdtcb[0:64, 128 * m:128 * m + 128],
                                              dtrb[b][0:64, :],
                                              start=True, stop=True))
                        for g in range(4):
                            if g == 0:
                                t_.wait_ge(LXH[b].h, 16 * (tc // 2 + 1))
                                if tc >= 1:
                                    t_.wait_ge(EVS.h, ev_ybf(tc - 1))
                            MMP.inc(t_.matmul(psy[g][:, :], C["diagd"][:, g, :],
                                              xhcb[b][:, g, :],
                                              start=True, stop=False))
                        for n in range(16):
                            for g in range(4):
                                if g == 0:
                                    t_.wait_ge(EVS.h, hc_done(tc, n))
                                MMP.inc(t_.matmul(psy[g][:, :], C["ident"][:, :],
                                                  hcb[n % 2][:, g, :],
                                                  start=False, stop=(n == 15)))
                        for m8 in range(8):
                            r = 8 * tc + m8
                            if m8 == 0:
                                t_.wait_ge(EVS.h, ev_ybf(tc))
                                t_.wait_ge(LZC[b].h, 16 * (tc // 2 + 1))
                            if r >= 2:
                                t_.wait_ge(EVS.h, ev_evac((r - 2) // 8, (r - 2) % 8))
                            for k in range(8):
                                rhs = ybfb[b][:, k, :] if k < 4 else zcb[b][:, k - 4, :]
                                ins = t_.matmul(pso[m8 % 2][:, :],
                                                woutb[:, k, 128 * m8:128 * m8 + 128],
                                                rhs, start=(k == 0), stop=(k == 7))
                            MMP.inc(ins)

                @blk.vector
                def _(v):
                    for tc in range(NTC):
                        b = tc % 2
                        if tc >= 2:
                            v.wait_ge(MMP.h, mm_dt(tc - 2, 3))
                        EVS.inc(v.tensor_copy(dtrb[b][0:64, :],
                                              dtBC[0:64, 512 * tc:512 * tc + 512]))
                        for m in range(4):
                            v.wait_ge(EAS.h, ea_ln(tc, m))
                            EVS.inc(v.tensor_tensor(deltaT[:, m, :], rbuf[:, m, :],
                                                    abuf[:, m, :], ALU.add))
                        for g in range(4):
                            if g == 0:
                                v.wait_ge(LXH[b].h, 16 * (tc // 2 + 1))
                            EVS.inc(v.tensor_tensor(dub[:, g, :], deltaT[:, g, :],
                                                    xhcb[b][:, g, :], ALU.mult))
                        for n in range(16):
                            fn = 16 * tc + n
                            for g in range(4):
                                if g == 0:
                                    v.wait_ge(GPW.h, fn + 1)
                                    v.wait_ge(EAS.h, ea_da(tc, n, 3))
                                EVS.inc(v.tensor_tensor_scan(
                                    hng[n % 4][:, g, :], dab[n % 2][:, g, :],
                                    wb[n % 2][:, g, :],
                                    0.0 if tc == 0 else carry[:, 4 * n + g:4 * n + g + 1],
                                    ALU.mult, ALU.add))
                            if n >= 1:
                                j = n - 1
                                fj = 16 * tc + j
                                v.wait_ge(LCC[j % 2].h, 16 * (8 * tc + j // 2 + 1))
                                if fj >= 2:
                                    v.wait_ge(MMP.h,
                                              mm_ya((fj - 2) // 16, (fj - 2) % 16, 3))
                                EVS.inc(v.tensor_tensor(
                                    hcb[j % 2][:, :, :], hng[j % 4][:, :, :],
                                    Cbc[j % 2][:, :].rearrange(
                                        "p (a t) -> p a t", a=1).to_broadcast([128, 4, 512]),
                                    ALU.mult))
                                EVS.inc(v.tensor_copy(
                                    carry[:, 4 * j:4 * j + 4],
                                    r2(hng[j % 4][:, :, 511:512])))
                        v.wait_ge(LCC[15 % 2].h, 16 * (8 * tc + 8))
                        v.wait_ge(MMP.h, mm_ya(tc, 13, 3))
                        EVS.inc(v.tensor_tensor(
                            hcb[15 % 2][:, :, :], hng[15 % 4][:, :, :],
                            Cbc[15 % 2][:, :].rearrange(
                                "p (a t) -> p a t", a=1).to_broadcast([128, 4, 512]),
                            ALU.mult))
                        EVS.inc(v.tensor_copy(carry[:, 60:64],
                                              r2(hng[15 % 4][:, :, 511:512])))
                        for g in range(4):
                            if g == 0:
                                v.wait_ge(MMP.h, mm_ya(tc, 15, 3))
                            EVS.inc(v.tensor_copy(ybfb[b][:, g, :], psy[g][:, :]))
                        for m8 in range(8):
                            v.wait_ge(MMP.h, mm_op(tc, m8))
                            if m8 == 0 and tc >= 2:
                                v.wait_ge(STO[b].h, 16 * ((tc - 2) // 2 + 1))
                            EVS.inc(v.tensor_copy(ost[b][:, m8, :], pso[m8 % 2][:, :]))


            with nc.Block() as blk:
                @blk.gpsimd
                def _(g):
                    g.wait_ge(STO[0].h, 16 * 8)
                    g.wait_ge(STO[1].h, 16 * 8)
                    ccM3.inc(g.collective_compute(
                        "AllGather", ALU.bypass, replica_groups=PAIRS,
                        ins=[ap["op_send"][1024:2048, :]],
                        outs=[ap["op_pag"][2048:4096, :]]))

        with ExitStack() as c15:
            la = [sbuf(c15, "m15a%d" % i, [128, TCOR], BF16) for i in range(2)]
            lb = [sbuf(c15, "m15b%d" % i, [128, TCOR], BF16) for i in range(2)]
            mos = [sbuf(c15, "m15m%d" % i, [128, TCOR], BF16) for i in range(2)]
            GI15 = [Sem(nc, c15, "m15g%d" % i) for i in range(2)]
            ST15 = [Sem(nc, c15, "m15s%d" % i) for i in range(2)]
            EV15 = Sem(nc, c15, "m15e")
            with nc.Block() as blk:
                @blk.gpsimd
                def _(g):
                    g.wait_ge(ccM3.h, 2)
                    rb = g.to_reg(4 * D - 1)
                    for m in range(8):
                        b = m % 2
                        if m >= 2:
                            g.wait_ge(EV15.h, m - 1)
                        GI15[b].inc(g.indirect_dma_start(
                            out=la[b][:, :], out_offset=None, in_=ap["op_pag"],
                            in_offset=bass.IndirectOffsetOnAxis(
                                C["mo_rows"][:, 2 * m:2 * m + 1], 0),
                            bounds_check=rb, oob_is_err=False), 16)
                        GI15[b].inc(g.indirect_dma_start(
                            out=lb[b][:, :], out_offset=None, in_=ap["op_pag"],
                            in_offset=bass.IndirectOffsetOnAxis(
                                C["mo_rows"][:, 2 * m + 1:2 * m + 2], 0),
                            bounds_check=rb, oob_is_err=False), 16)

                @blk.vector
                def _(v):
                    for m in range(8):
                        b = m % 2
                        v.wait_ge(GI15[b].h, 32 * (m // 2 + 1))
                        if m >= 2:
                            v.wait_ge(ST15[b].h, 16 * (m // 2))
                        EV15.inc(v.scalar_tensor_tensor(
                            mos[b][:, :], la[b][:, :], C["bout"][:, m:m + 1],
                            lb[b][:, :], op0=ALU.add, op1=ALU.add))

                @blk.sync
                def _(s):
                    for m in range(8):
                        s.wait_ge(EV15.h, m + 1)
                        ST15[m % 2].inc(s.dma_start(
                            out=ap["mo_d"][128 * m:128 * m + 128, :],
                            in_=mos[m % 2][:, :]), 16)
                    s.wait_ge(ST15[0].h, 16 * 4)
                    s.wait_ge(ST15[1].h, 16 * 4)

        with ExitStack() as c16:
            mt = [sbuf(c16, "m16m%d" % i, [128, D], BF16) for i in range(2)]
            x1b = [sbuf(c16, "m16a%d" % i, [128, D], F32) for i in range(2)]
            x2b = [sbuf(c16, "m16b%d" % i, [128, D], F32) for i in range(2)]
            LT16 = [Sem(nc, c16, "m16l%d" % i) for i in range(2)]
            LX16 = [Sem(nc, c16, "m16x%d" % i) for i in range(2)]
            ST16 = [Sem(nc, c16, "m16s%d" % i) for i in range(2)]
            EV16 = Sem(nc, c16, "m16e")
            with nc.Block() as blk:
                @blk.sync
                def _(s):
                    for t in range(NT):
                        b = t % 2
                        if t >= 2:
                            s.wait_ge(EV16.h, t - 1)
                        LT16[b].inc(s.dma_start(
                            out=mt[b][:, :],
                            in_=ap["mo_d"][0:D, 128 * t:128 * t + 128],
                            transpose=True), 16)
                        LX16[b].inc(s.dma_start(
                            out=x1b[b][:, :],
                            in_=ap["x1_d"][128 * t:128 * t + 128, :]), 16)

                @blk.vector
                def _(v):
                    for t in range(NT):
                        b = t % 2
                        v.wait_ge(LT16[b].h, 16 * (t // 2 + 1))
                        v.wait_ge(LX16[b].h, 16 * (t // 2 + 1))
                        if t >= 2:
                            v.wait_ge(ST16[b].h, 16 * ((t - 2) // 2 + 1))
                        EV16.inc(v.tensor_tensor(x2b[b][:, :], x1b[b][:, :],
                                                 mt[b][:, :], ALU.add))

                @blk.scalar
                def _(sc):
                    for t in range(NT):
                        sc.wait_ge(EV16.h, t + 1)
                        ST16[t % 2].inc(sc.dma_start(
                            out=ap["x2_d"][128 * t:128 * t + 128, :],
                            in_=x2b[t % 2][:, :]), 16)
                    sc.wait_ge(ST16[0].h, 16 * (NT // 2))
                    sc.wait_ge(ST16[1].h, 16 * (NT // 2))
                    mdone.inc(sc.activation(x2b[0][0:1, 0:1], x2b[0][0:1, 0:1],
                                            AF.Copy))
        mctx.close()

    f1done = Sem(nc, pctx, "f1done")
    emit_ffn("f1", ap["x_my"], ap["x1_d"], w1s, w2s, f1done)
    f1ctx.close()

    if _UPTO >= 9:
        x2done = Sem(nc, pctx, "x2done")
        emit_mamba(x2done)
        f2ctx = ExitStack()
        w1s2 = sbuf(f2ctx, "w1s2", [128, 8, H], BF16)
        w2s2 = sbuf(f2ctx, "w2s2", [128, 32, D], BF16)
        ldw = Sem(nc, f2ctx, "ldw2")
        with nc.Block() as blk:
            @blk.gpsimd
            def _(g):
                g.wait_ge(x2done.h, 1)
                ldw.inc(g.dma_start(out=w1s2[:, :, :], in_=ap["wbf1_d"]), 16)
                ldw.inc(g.dma_start(out=w2s2[:, :, :], in_=ap["wbf2_d"]), 16)
                g.wait_ge(ldw.h, 32)
        f2done = Sem(nc, pctx, "f2done")
        emit_ffn("f2", ap["x2_d"], ap["y_out"], w1s2, w2s2, f2done, xdep=(x2done, 1))
        f2ctx.close()
    if _DEBUG:
        dbf = Sem(nc, pctx, "dbf")
        with nc.Block() as blk:
            @blk.sync
            def _(s):
                s.wait_ge(f1done.h, 1)
                for src, dst in [("lnx_my", "dbg_lnx"), ("totd", "dbg_tot"),
                                 ("cnt_all", "dbg_cnt"), ("pq_d", "dbg_pq"),
                                 ("ldisp", "dbg_ldisp"), ("ddisp", "dbg_ddisp"),
                                 ("eo_my", "dbg_eo"), ("eo_ret", "dbg_eor")]:
                    dbf.inc(s.dma_start(out=ap[dst], in_=ap[src]), 16)
                s.wait_ge(dbf.h, dbf.n)
    if _UPTO >= 9:
        pass
    else:
        with ExitStack() as cd:
            tb = sbuf(cd, "devcp", [128, D], F32)
            LDz = Sem(nc, cd, "ldz")
            STz = Sem(nc, cd, "stz")
            with nc.Block() as blk:
                @blk.sync
                def _(s):
                    s.wait_ge(f1done.h, 1)
                    for t in range(NT):
                        if t >= 1:
                            s.wait_ge(STz.h, 16 * t)
                        LDz.inc(s.dma_start(out=tb[:, :],
                                            in_=ap["x1_d"][128 * t:128 * t + 128, :]), 16)
                        s.wait_ge(LDz.h, 16 * (t + 1))
                        STz.inc(s.dma_start(out=ap["y_out"][128 * t:128 * t + 128, :],
                                            in_=tb[:, :]), 16)
                    s.wait_ge(STz.h, 16 * NT)

    pctx.close()
    return nc



def _prep_core(c, I):
    b, k = c // 2, c % 2
    xh = np.arange(512 * k, 512 * k + 512)
    f32 = np.float32
    bf = ml_dtypes.bfloat16
    d = {}
    X = np.asarray(I["L0_attn_x"], f32).reshape(32768, D)
    d["x_my"] = np.ascontiguousarray(X[4096 * c:4096 * (c + 1)])
    d["fc1wT"] = np.ascontiguousarray(np.asarray(I["L0_ffn_fc1_w"], f32)[c].T)
    d["fc2w"] = np.ascontiguousarray(np.asarray(I["L0_ffn_fc2_w"], f32)[c])
    d["b1t"] = np.ascontiguousarray(np.asarray(I["L0_ffn_fc1_b"], f32)[c, 0].reshape(32, 128).T)
    d["b2"] = np.ascontiguousarray(np.asarray(I["L0_ffn_fc2_b"], f32)[c, 0].reshape(1, D))
    d["gwT"] = np.ascontiguousarray(np.asarray(I["L0_ffn_gate_w"], f32).T)
    cols = np.concatenate([xh, 1024 + xh])
    d["winc"] = np.ascontiguousarray(np.asarray(I["L0_attn_W_in"], f32)[:, cols])
    d["woutr"] = np.ascontiguousarray(np.asarray(I["L0_attn_W_out"], f32)[cols, :])
    d["wxp"] = np.ascontiguousarray(np.asarray(I["L0_attn_W_xproj"], f32)[xh, :])
    d["wdtc"] = np.ascontiguousarray(np.asarray(I["L0_attn_W_dt"], f32)[:, xh])
    d["bdt"] = np.ascontiguousarray(np.asarray(I["L0_attn_b_dt"], f32)[xh].reshape(4, 128).T)
    d["alog"] = np.ascontiguousarray(
        np.asarray(I["L0_attn_A_log"], f32)[xh].reshape(4, 128, 16).transpose(1, 0, 2))
    d["dvec"] = np.ascontiguousarray(np.asarray(I["L0_attn_D"], f32)[xh].reshape(4, 128).T)
    ckx = np.asarray(I["L0_attn_conv_x_k"], f32)[:, 0, :][:, xh]
    ckz = np.asarray(I["L0_attn_conv_z_k"], f32)[:, 0, :][:, xh]
    ck = np.zeros((128, 8, 4), f32)
    ck[:, 0:4, :] = ckx.T.reshape(4, 128, 4).transpose(1, 0, 2)
    ck[:, 4:8, :] = ckz.T.reshape(4, 128, 4).transpose(1, 0, 2)
    d["ck"] = ck
    d["bout"] = np.ascontiguousarray(np.asarray(I["L0_attn_b_out"], f32).reshape(8, 128).T)
    d["ident"] = np.eye(128, dtype=bf)
    d["utri"] = np.triu(np.ones((128, 128), bf))
    d["ltri"] = np.tril(np.ones((128, 128), bf), -1)
    d["onesm"] = np.ones((128, 128), bf)
    d["iotaE"] = np.broadcast_to(np.arange(E, dtype=f32), (128, NT, E)).copy()
    d["prefv"] = (np.arange(8) < c).astype(f32).reshape(8, 1)
    d["ones8"] = np.ones((8, 1), f32)
    p = np.arange(128)
    d["mam_rows"] = (b * 8192 + (np.arange(64) * 128)[None, :] + p[:, None]).astype(np.int32)
    xr = np.zeros((128, 4), np.int32)
    xr[:, 0] = (2 * b) * 256 + p
    xr[:, 1] = (2 * b + 1) * 256 + p
    xr[:, 2] = (2 * b) * 256 + 128 + p
    xr[:, 3] = (2 * b + 1) * 256 + 128 + p
    d["xdb_rows"] = xr
    mo = np.zeros((128, 16), np.int32)
    for m in range(8):
        for src in range(2):
            mo[:, 2 * m + src] = 2048 * k + 1024 * src + 128 * m + p
    d["mo_rows"] = mo
    dg = np.zeros((128, 4, 128), bf)
    for g_ in range(4):
        dg[np.arange(128), g_, np.arange(128)] = d["dvec"][:, g_].astype(bf)
    d["diagd"] = dg
    m01 = np.zeros((128, 2), f32)
    m01[:, 0] = 1.0 if k == 0 else 0.0
    m01[:, 1] = 1.0 if k == 1 else 0.0
    d["m01"] = m01
    fc = np.zeros((128, 8), f32)
    fc[:, 0] = 1e-5
    fc[:, 1] = -0.5
    fc[:, 2] = -1.0
    fc[:, 3] = 1.0
    fc[:, 4] = 0.5
    fc[:, 6] = 8192.0
    d["fconst"] = fc
    return d


_last_results = None


def kernel(**inputs):
    global _last_results
    nc = build_nc()
    in_maps = [_prep_core(c, inputs) for c in range(NCORE)]
    res = run_bass_kernel_spmd(nc, in_maps, core_ids=list(range(NCORE)))
    _last_results = res.results
    y = np.concatenate([np.asarray(res.results[c]["y_out"]) for c in range(NCORE)], axis=0)
    return y.reshape(4, 8192, 1024).astype(np.float32)


# version 05
# speedup vs baseline: 1.0266x; 1.0018x over previous
import os
os.environ.setdefault("TRN_ALLOW_ICI_ALL_TO_ALL", "1")

import numpy as np
import ml_dtypes
from contextlib import ExitStack

import concourse.bass as bass
import concourse.mybir as mybir
from concourse.bass_utils import run_bass_kernel_spmd
from concourse.aot_env import allow_ici_all_to_all

F32 = mybir.dt.float32
BF16 = mybir.dt.bfloat16
I32 = mybir.dt.int32
U8 = mybir.dt.uint8
AF = mybir.ActivationFunctionType
ALU = mybir.AluOpType
AX = mybir.AxisListType

NCORE = 8
TCOR = 4096
NT = 32
D = 1024
H = 4096
E = 8
CAPL = 1280
SHARD = 1296
NST = 20
SEQ = 8192
TCH = 512
NTC = 16
BIGF = 1.0e9

_UPTO = 99
_DEBUG = False


class _SemScope:

    def __init__(self, nc, ctx):
        self.nc = nc
        self.handles = []
        ctx.callback(self._close)

    def _close(self):
        if self.handles:
            self.nc.all_engine_barrier()
            self.nc.clear_and_free_semaphores(self.handles)


class Sem:
    def __init__(self, nc, ctx, name):
        if not hasattr(ctx, "_semscope"):
            ctx._semscope = _SemScope(nc, ctx)
        self.h = nc.alloc_semaphore(name=name)
        ctx._semscope.handles.append(self.h)
        self.n = 0

    def inc(self, ins, k=1):
        ins.then_inc(self.h, k)
        self.n += k
        return self.n


def r2(t):
    return t.rearrange("p a b -> p (a b)")


def build_nc():
    nc = bass.Bass("TRN2", target_bir_lowering=False, debug=False,
                 detect_race_conditions=False)
    _th = {}

    class _ApDict:
        def __getitem__(self, name):
            return _th[name].ap()

    ap = _ApDict()

    def din(name, shape, dt):
        _th[name] = nc.dram_tensor(name, shape, dt, kind="ExternalInput")

    def dint(name, shape, dt):
        _th[name] = nc.dram_tensor(name, shape, dt)

    din("x_my", [TCOR, D], F32)
    din("fc1wT", [D, H], F32)
    din("fc2w", [H, D], F32)
    din("b1t", [128, 32], F32)
    din("b2", [1, D], F32)
    din("gwT", [D, E], F32)
    din("winc", [D, D], F32)
    din("woutr", [D, D], F32)
    din("wxp", [512, 96], F32)
    din("wdtc", [64, 512], F32)
    din("bdt", [128, 4], F32)
    din("alog", [128, 4, 16], F32)
    din("dvec", [128, 4], F32)
    din("ck", [128, 8, 4], F32)
    din("bout", [128, 8], F32)
    din("ident", [128, 128], BF16)
    din("utri", [128, 128], BF16)
    din("ltri", [128, 128], BF16)
    din("onesm", [128, 128], BF16)
    din("iotaE", [128, NT, E], F32)
    din("prefv", [8, 1], F32)
    din("ones8", [8, 1], F32)
    din("mam_rows", [128, 64], I32)
    din("xdb_rows", [128, 4], I32)
    din("mo_rows", [128, 16], I32)
    din("m01", [128, 2], F32)
    din("diagd", [128, 4, 128], BF16)
    din("fconst", [128, 8], F32)

    _th["y_out"] = nc.dram_tensor("y_out", [TCOR, D], F32, kind="ExternalOutput")

    dint("lnx_my", [TCOR, D], BF16)
    dint("ldisp", [E * SHARD, D], BF16)
    dint("ddisp", [E * SHARD, D], BF16)
    dint("eo_my", [E * CAPL, D], BF16)
    dint("eo_ret", [E * CAPL, D], BF16)
    dint("totd", [1, 16], F32)
    dint("cnt_all", [8, 16], F32)
    dint("pq_d", [1, 32], F32)
    dint("x1_d", [TCOR, D], F32)
    dint("x2_d", [TCOR, D], F32)
    dint("ln2pair", [2 * TCOR, D], BF16)
    dint("xdbl_pair", [192, SEQ], F32)
    dint("xz_d", [D, SEQ], BF16)
    dint("xhc_d", [512, SEQ], BF16)
    dint("zc_d", [512, SEQ], BF16)
    dint("xdbl_my", [96, SEQ], F32)
    dint("bd_d", [16, SEQ], BF16)
    dint("cd_d", [16, SEQ], BF16)
    dint("op_send", [2 * D, TCOR], BF16)
    dint("op_pag", [4 * D, TCOR], BF16)
    dint("mo_d", [D, TCOR], BF16)
    dint("wbf1_d", [128, 8, H], BF16)
    dint("wbf2_d", [128, 32, D], BF16)
    if _DEBUG:
        for nm, shape, dt in [("dbg_lnx", [TCOR, D], BF16), ("dbg_tot", [1, 16], F32),
                              ("dbg_cnt", [8, 16], F32), ("dbg_pq", [1, 32], F32),
                              ("dbg_ldisp", [E * SHARD, D], BF16),
                              ("dbg_ddisp", [E * SHARD, D], BF16),
                              ("dbg_eo", [E * CAPL, D], BF16),
                              ("dbg_eor", [E * CAPL, D], BF16),
                              ("dbg_rt", [128, 64, 8], F32), ("dbg_lg", [128, NT, 8], F32),
                              ("dbg_oh", [128, NT, 16], F32), ("dbg_pp", [128, NT, 16], F32)]:
            _th[nm] = nc.dram_tensor(nm, shape, dt, kind="ExternalOutput")

    pctx = ExitStack()

    def sbuf(ctx, name, shape, dt):
        return ctx.enter_context(nc.sbuf_tensor(name, shape, dt))

    def pst(ctx, name, shape, dt):
        return ctx.enter_context(nc.psum_tensor(name, shape, dt))

    C = {}
    for nm, shape, dt in [("ident", [128, 128], BF16), ("utri", [128, 128], BF16),
                          ("ltri", [128, 128], BF16), ("onesm", [128, 128], BF16),
                          ("iotaE", [128, NT, E], F32),
                          ("prefv", [8, 1], F32), ("ones8", [8, 1], F32),
                          ("b1t", [128, 32], F32), ("m01", [128, 2], F32),
                          ("bout", [128, 8], F32), ("mam_rows", [128, 64], I32),
                          ("xdb_rows", [128, 4], I32), ("mo_rows", [128, 16], I32),
                          ("bdt", [128, 4], F32), ("alog", [128, 4, 16], F32),
                          ("dvec", [128, 4], F32), ("ck", [128, 8, 4], F32),
                          ("diagd", [128, 4, 128], BF16),
                          ("fconst", [128, 8], F32)]:
        C[nm] = sbuf(pctx, "c_" + nm, shape, dt)
    C["b2bc"] = sbuf(pctx, "c_b2bc", [128, D], F32)
    C["gwTb"] = sbuf(pctx, "c_gwTb", [128, 8, 8], BF16)
    C["A"] = sbuf(pctx, "c_A", [128, 4, 16], F32)
    C["bdt2"] = sbuf(pctx, "c_bdt2", [128, 4], F32)

    ld0 = Sem(nc, pctx, "ld0")
    ldg = Sem(nc, pctx, "ldg")
    lds = Sem(nc, pctx, "lds")
    ldz = Sem(nc, pctx, "ldz")
    cm0 = Sem(nc, pctx, "cm0")

    f1ctx = ExitStack()
    w1s = sbuf(f1ctx, "w1s", [128, 8, H], BF16)
    w2s = sbuf(f1ctx, "w2s", [128, 32, D], BF16)

    with ExitStack() as zctx:
        zt = sbuf(zctx, "zt", [128, 10240], BF16)
        with nc.Block() as blk:
            @blk.sync
            def _(s):
                for nm in ["ident", "utri", "ltri", "iotaE", "prefv", "ones8",
                           "b1t", "m01", "bout", "mam_rows", "xdb_rows",
                           "mo_rows", "bdt", "alog", "dvec", "ck", "fconst",
                           "onesm", "diagd"]:
                    ld0.inc(s.dma_start(out=C[nm][...], in_=ap[nm]), 16)

            @blk.scalar
            def _(sc):
                bap = bass.AP(ap["b2"].tensor, 0, [[0, 128], [1, D]])
                lds.inc(sc.dma_start(out=C["b2bc"][:, :], in_=bap), 16)

            @blk.gpsimd
            def _(g):
                ldg.inc(g.dma_start(out=w1s[:, :, :],
                        in_=ap["fc1wT"].rearrange("(k p) h -> p k h", p=128)), 16)
                ldg.inc(g.dma_start(out=w2s[:, :, :],
                        in_=ap["fc2w"].rearrange("(g p) d -> p g d", p=128)), 16)
                ldg.inc(g.dma_start(out=C["gwTb"][:, :, :],
                        in_=ap["gwT"].rearrange("(k p) e -> p k e", p=128)), 16)

            @blk.vector
            def _(v):
                cm0.inc(v.memset(zt[:, :], 0))

        with nc.Block() as blk:
            mk_ld = ld0.n

            @blk.scalar
            def _(sc):
                sc.wait_ge(ld0.h, mk_ld)
                sc.wait_ge(ldg.h, 48)
                cm0.inc(sc.activation(C["A"][...], C["alog"][...], AF.Exp))

            @blk.vector
            def _(v):
                v.wait_ge(cm0.h, 2)
                cm0.inc(v.tensor_scalar(C["A"][...], C["A"][...], -1.0, None, op0=ALU.mult))
                cm0.inc(v.tensor_scalar(C["bdt2"][...], C["bdt"][...], 2.0, None, op0=ALU.mult))

            @blk.sync
            def _(s):
                s.wait_ge(cm0.h, 1)
                rows = E * SHARD
                r0 = 0
                while r0 < rows:
                    n = min(1280, rows - r0)
                    nb = n // 128
                    ldz.inc(s.dma_start(
                        out=ap["ldisp"][r0:r0 + n, :].rearrange("(a p) d -> p a d", p=128),
                        in_=zt[:, 0:nb * 1024].rearrange("p (a d) -> p a d", d=1024)), 16)
                    r0 += n
                s.wait_ge(ld0.h, mk_ld)

            @blk.gpsimd
            def _(g):
                g.wait_ge(ldg.h, 48)
                ldg.inc(g.dma_start(out=ap["wbf1_d"], in_=w1s[:, :, :]), 16)
                ldg.inc(g.dma_start(out=ap["wbf2_d"], in_=w2s[:, :, :]), 16)

    P0_LD = ld0.n

    def emit_ffn(tag, src_x, dst_x, wa, wb, done, xdep=None):
        fctx = ExitStack()
        logits = sbuf(fctx, tag + "logits", [128, NT, 8], F32)
        if _DEBUG:
            dbg_oh_sb = sbuf(fctx, tag + "dbgoh", [128, NT, 16], F32)
            dbg_pp_sb = sbuf(fctx, tag + "dbgpp", [128, NT, 16], F32)
        geff = sbuf(fctx, tag + "geff", [128, NT, 2], F32)
        sidx = sbuf(fctx, tag + "sidx", [128, 64], I32)
        cidx = sbuf(fctx, tag + "cidx", [128, 64], I32)
        grdy = Sem(nc, fctx, tag + "grdy")
        cc2 = Sem(nc, fctx, tag + "cc2")
        cc3 = Sem(nc, fctx, tag + "cc3")

        with ExitStack() as c1:
            xin = [sbuf(c1, tag + "xin%d" % i, [128, D], F32) for i in range(2)]
            lnb = [sbuf(c1, tag + "lnb%d" % i, [128, D], BF16) for i in range(2)]
            sqs = sbuf(c1, tag + "sqs", [128, D], F32)
            st1 = sbuf(c1, tag + "st1", [128, 8], F32)
            lnxT = sbuf(c1, tag + "lnxT", [128, 8, 128], BF16)
            ptA = pst(c1, tag + "ptA", [128, 1024], BF16)
            ptB = pst(c1, tag + "ptB", [128, 1024], BF16)
            psL = pst(c1, tag + "psL", [128, 8], F32)
            LD = [Sem(nc, c1, tag + "p1ld%d" % i) for i in range(2)]
            ST = [Sem(nc, c1, tag + "p1st%d" % i) for i in range(2)]
            EV = Sem(nc, c1, tag + "p1ev")
            EA = Sem(nc, c1, tag + "p1ea")
            MM = Sem(nc, c1, tag + "p1mm")
            with nc.Block() as blk:
                @blk.sync
                def _(s):
                    s.wait_ge(ld0.h, P0_LD)
                    s.wait_ge(ldg.h, 80)
                    s.wait_ge(lds.h, 16)
                    s.wait_ge(ldz.h, 144)
                    if xdep is not None:
                        s.wait_ge(xdep[0].h, xdep[1])
                    for t in range(NT):
                        b = t % 2
                        if t >= 2:
                            s.wait_ge(EV.h, 4 * (t - 2) + 2)
                        LD[b].inc(s.dma_start(out=xin[b][:, :],
                                              in_=src_x[128 * t:128 * t + 128, :]), 16)
                        if t >= 1:
                            s.wait_ge(EV.h, 4 * (t - 1) + 2)
                            ST[(t - 1) % 2].inc(s.dma_start(
                                out=ap["lnx_my"][128 * (t - 1):128 * t, :],
                                in_=lnb[(t - 1) % 2][:, :]), 16)
                    s.wait_ge(EV.h, 4 * (NT - 1) + 2)
                    ST[(NT - 1) % 2].inc(s.dma_start(
                        out=ap["lnx_my"][128 * (NT - 1):128 * NT, :],
                        in_=lnb[(NT - 1) % 2][:, :]), 16)
                    s.wait_ge(ST[0].h, 16 * (NT // 2))
                    s.wait_ge(ST[1].h, 16 * (NT // 2))

                @blk.vector
                def _(v):
                    for t in range(NT):
                        b = t % 2
                        v.wait_ge(LD[b].h, 16 * (t // 2 + 1))
                        v.tensor_reduce(st1[:, 4 * b:4 * b + 1], xin[b][:, :], AX.X, ALU.add)
                        v.drain()
                        v.wait_ge(EA.h, 2 * t + 1)
                        v.tensor_scalar(st1[:, 4 * b:4 * b + 1], st1[:, 4 * b:4 * b + 1],
                                        1.0 / D, None, op0=ALU.mult)
                        v.drain()
                        v.tensor_tensor(st1[:, 4 * b + 2:4 * b + 3], st1[:, 4 * b:4 * b + 1],
                                        st1[:, 4 * b:4 * b + 1], ALU.mult)
                        v.drain()
                        EV.inc(v.scalar_tensor_tensor(st1[:, 4 * b + 1:4 * b + 2],
                                                      st1[:, 4 * b + 1:4 * b + 2], 1.0 / D,
                                                      st1[:, 4 * b + 2:4 * b + 3],
                                                      op0=ALU.mult, op1=ALU.subtract))
                        v.wait_ge(EA.h, 2 * t + 2)
                        if t >= 2:
                            v.wait_ge(MM.h, 2 * (t - 2) + 1)
                            v.wait_ge(ST[b].h, 16 * ((t - 2) // 2 + 1))
                        EV.inc(v.tensor_scalar(lnb[b][:, :], xin[b][:, :],
                                               st1[:, 4 * b:4 * b + 1],
                                               st1[:, 4 * b + 1:4 * b + 2],
                                               op0=ALU.subtract, op1=ALU.mult))
                        v.wait_ge(MM.h, 2 * t + 1)
                        EV.inc(v.tensor_copy(lnxT[:, :, :],
                                             (ptA if b == 0 else ptB)[:, :].rearrange(
                                                 "p (k q) -> p k q", q=128)))
                        v.wait_ge(MM.h, 2 * t + 2)
                        EV.inc(v.tensor_copy(logits[:, t, :], psL[:, :]))

                @blk.scalar
                def _(sc):
                    for t in range(NT):
                        b = t % 2
                        sc.wait_ge(LD[b].h, 16 * (t // 2 + 1))
                        if t >= 2:
                            sc.wait_ge(EV.h, 4 * (t - 2) + 2)
                        EA.inc(sc.activation(sqs[:, :], xin[b][:, :], AF.Square,
                                             accum_out=st1[:, 4 * b + 1:4 * b + 2]))
                        sc.wait_ge(EV.h, 4 * t + 1)
                        sc.activation(st1[:, 4 * b + 1:4 * b + 2],
                                      st1[:, 4 * b + 1:4 * b + 2], AF.Ln,
                                      bias=C["fconst"][:, 0:1])
                        sc.drain()
                        EA.inc(sc.activation(st1[:, 4 * b + 1:4 * b + 2],
                                             st1[:, 4 * b + 1:4 * b + 2], AF.Exp,
                                             scale=C["fconst"][:, 1:2]))

                @blk.tensor
                def _(t_):
                    for t in range(NT):
                        b = t % 2
                        pt = ptA if b == 0 else ptB
                        t_.wait_ge(EV.h, 4 * t + 2)
                        if t >= 2:
                            t_.wait_ge(EV.h, 4 * (t - 2) + 3)
                        for k in range(8):
                            ins = t_.transpose(pt[:, 128 * k:128 * k + 128],
                                               lnb[b][:, 128 * k:128 * k + 128],
                                               C["ident"][:, :])
                        MM.inc(ins)
                        t_.wait_ge(EV.h, 4 * t + 3)
                        for k in range(8):
                            ins = t_.matmul(psL[:, :], lnxT[:, k, :], C["gwTb"][:, k, :],
                                            start=(k == 0), stop=(k == 7))
                        MM.inc(ins)

        with ExitStack() as c2:
            ohb = sbuf(c2, tag + "ohb", [128, NT, 16], BF16)
            locP = sbuf(c2, tag + "locP", [128, NT, 2], F32)
            g01 = sbuf(c2, tag + "g01", [128, NT, 2], F32)
            scr = sbuf(c2, tag + "scr", [128, NT, 8], F32)
            scr2 = sbuf(c2, tag + "scr2", [128, NT, 8], F32)
            msk = sbuf(c2, tag + "msk", [128, NT, 8], U8)
            sm1 = sbuf(c2, tag + "sm1", [128, NT, 1], F32)
            sm2 = sbuf(c2, tag + "sm2", [128, NT, 1], F32)
            si1 = sbuf(c2, tag + "si1", [128, NT, 1], F32)
            si2 = sbuf(c2, tag + "si2", [128, NT, 1], F32)
            Pp = sbuf(c2, tag + "Pp", [128, NT, 16], F32)
            Pf = sbuf(c2, tag + "Pf", [128, NT, 16], F32)
            Tot2 = sbuf(c2, tag + "Tot2", [128, 16, NT], F32)
            Incl = sbuf(c2, tag + "Incl", [128, 16, NT], F32)
            onv = sbuf(c2, tag + "onv", [128, NT], F32)
            tot = sbuf(c2, tag + "tot", [1, 16], F32)
            totbc = sbuf(c2, tag + "totbc", [128, 16], F32)
            tmp1 = sbuf(c2, tag + "tmp1", [128, NT, 1], F32)
            cnt_sb = sbuf(c2, tag + "cnt", [8, 16], F32)
            prefbc = sbuf(c2, tag + "prefbc", [128, 16], F32)
            totA = sbuf(c2, tag + "totA", [128, 16], F32)
            pq = sbuf(c2, tag + "pq", [1, 32], F32)
            psP = pst(c2, tag + "psP", [128, 16], F32)
            psT = pst(c2, tag + "psT", [128, 16], F32)
            EVr = Sem(nc, c2, tag + "p2ev")
            MMr = Sem(nc, c2, tag + "p2mm")
            STr = Sem(nc, c2, tag + "p2st")
            LDr = Sem(nc, c2, tag + "p2ldr")
            EAr = Sem(nc, c2, tag + "p2ea")
            cc1 = Sem(nc, c2, tag + "cc1")
            GID = [Sem(nc, c2, tag + "gid%d" % i) for i in range(2)]
            LDD = [Sem(nc, c2, tag + "ldd%d" % i) for i in range(2)]
            LD4 = Sem(nc, c2, tag + "ld4")

            def D_(v):
                v.drain()

            with nc.Block() as blk:
                @blk.vector
                def _(v):
                    io = C["iotaE"]
                    L = logits
                    v.tensor_reduce(sm1[...], L[...], AX.X, ALU.max); D_(v)
                    v.tensor_tensor(msk[...], L[...], sm1[...].to_broadcast([128, NT, 8]), ALU.is_equal); D_(v)
                    v.memset(scr[...], BIGF); D_(v)
                    v.copy_predicated(scr[...], msk[...], io[...]); D_(v)
                    v.tensor_reduce(si1[...], scr[...], AX.X, ALU.min); D_(v)
                    v.tensor_tensor(msk[...], io[...], si1[...].to_broadcast([128, NT, 8]), ALU.is_equal); D_(v)
                    v.tensor_copy(ohb[:, :, 0:8], msk[...]); D_(v)
                    v.scalar_tensor_tensor(scr2[...], ohb[:, :, 0:8], -BIGF, L[...],
                                           op0=ALU.mult, op1=ALU.add); D_(v)
                    v.tensor_reduce(sm2[...], scr2[...], AX.X, ALU.max); D_(v)
                    v.tensor_tensor(msk[...], scr2[...], sm2[...].to_broadcast([128, NT, 8]), ALU.is_equal); D_(v)
                    v.memset(scr[...], BIGF); D_(v)
                    v.copy_predicated(scr[...], msk[...], io[...]); D_(v)
                    v.tensor_reduce(si2[...], scr[...], AX.X, ALU.min); D_(v)
                    v.tensor_tensor(msk[...], io[...], si2[...].to_broadcast([128, NT, 8]), ALU.is_equal); D_(v)
                    EV0 = EVr.inc(v.tensor_copy(ohb[:, :, 8:16], msk[...]))
                    v.drain()
                    v.tensor_tensor(sm2[...], sm2[...], sm1[...], ALU.subtract); D_(v)
                    EVr.inc(v.tensor_copy(sm2[...], sm2[...]))
                    v.memset(onv[:, :], 1.0)
                    v.drain()
                    for t in range(NT):
                        v.wait_ge(MMr.h, 2 * t + 1)
                        EVr.inc(v.tensor_copy(Pp[:, t, :], psP[:, :]))
                        v.wait_ge(MMr.h, 2 * t + 2)
                        EVr.inc(v.tensor_copy(Tot2[:, :, t], psT[:, :]))
                    v.drain()
                    for e8 in range(16):
                        v.tensor_tensor_scan(Incl[:, e8, :], onv[:, :], Tot2[:, e8, :],
                                             0.0, ALU.mult, ALU.add)
                    v.drain()
                    v.tensor_tensor(Incl[:, :, :], Incl[:, :, :], Tot2[:, :, :], ALU.subtract)
                    v.drain()
                    v.tensor_tensor(Pf[:, :, :], Pp[:, :, :],
                                    Incl[:, :, :].rearrange("p e t -> p t e"), ALU.add)
                    v.drain()
                    EVr.inc(v.tensor_tensor(tot[0:1, :], r2(Incl[0:1, :, NT - 1:NT]),
                                             r2(Tot2[0:1, :, NT - 1:NT]), ALU.add))
                    v.drain()
                    v.tensor_tensor(scr[...], ohb[:, :, 0:8], Pf[:, :, 0:8], ALU.mult); D_(v)
                    v.tensor_reduce(locP[:, :, 0:1], scr[...], AX.X, ALU.add); D_(v)
                    v.tensor_tensor(scr[...], ohb[:, :, 8:16], Pf[:, :, 8:16], ALU.mult); D_(v)
                    v.tensor_reduce(locP[:, :, 1:2], scr[...], AX.X, ALU.add); D_(v)
                    v.wait_ge(EAr.h, 1)
                    v.tensor_scalar(sm1[...], sm2[...], 1.0, None, op0=ALU.add); D_(v)
                    v.reciprocal(sm1[...], sm1[...]); D_(v)
                    v.tensor_scalar(g01[:, :, 0:1], sm1[...], 0.5, None, op0=ALU.mult); D_(v)
                    v.tensor_tensor(tmp1[...], sm2[...], sm1[...], ALU.mult); D_(v)
                    v.tensor_scalar(g01[:, :, 1:2], tmp1[...], 0.5, None, op0=ALU.mult); D_(v)
                    v.tensor_tensor(totbc[:, :], r2(Incl[:, :, NT - 1:NT]),
                                    r2(Tot2[:, :, NT - 1:NT]), ALU.add)
                    v.drain()
                    v.tensor_tensor(scr[...], ohb[:, :, 8:16],
                                    totbc[:, 0:8].rearrange("p (a e) -> p a e", a=1).to_broadcast([128, NT, 8]),
                                    ALU.mult); D_(v)
                    v.tensor_reduce(tmp1[...], scr[...], AX.X, ALU.add); D_(v)
                    v.tensor_tensor(locP[:, :, 1:2], locP[:, :, 1:2], tmp1[...], ALU.add); D_(v)
                    v.tensor_scalar(sm1[...], locP[:, :, 0:1], -1.0, None, op0=ALU.add); D_(v)
                    v.tensor_scalar(sm2[...], locP[:, :, 1:2], -1.0, None, op0=ALU.add); D_(v)
                    v.tensor_scalar(scr[:, :, 0:1], sm1[...], float(SHARD - 8), None, op0=ALU.min); D_(v)
                    v.tensor_scalar(scr[:, :, 1:2], sm2[...], float(SHARD - 8), None, op0=ALU.min); D_(v)
                    v.scalar_tensor_tensor(scr[:, :, 2:3], si1[...], float(SHARD),
                                           scr[:, :, 0:1], op0=ALU.mult, op1=ALU.add); D_(v)
                    v.scalar_tensor_tensor(scr[:, :, 3:4], si2[...], float(SHARD),
                                           scr[:, :, 1:2], op0=ALU.mult, op1=ALU.add); D_(v)
                    v.tensor_copy(sidx[:, 0:32], r2(scr[:, :, 2:3])); D_(v)
                    v.tensor_copy(sidx[:, 32:64], r2(scr[:, :, 3:4])); D_(v)
                    v.tensor_scalar(scr[:, :, 0:1], sm1[...], float(CAPL - 1), None, op0=ALU.min); D_(v)
                    v.tensor_scalar(scr[:, :, 1:2], sm2[...], float(CAPL - 1), None, op0=ALU.min); D_(v)
                    v.scalar_tensor_tensor(scr[:, :, 2:3], si1[...], float(CAPL),
                                           scr[:, :, 0:1], op0=ALU.mult, op1=ALU.add); D_(v)
                    v.scalar_tensor_tensor(scr[:, :, 3:4], si2[...], float(CAPL),
                                           scr[:, :, 1:2], op0=ALU.mult, op1=ALU.add); D_(v)
                    v.tensor_copy(cidx[:, 0:32], r2(scr[:, :, 2:3])); D_(v)
                    if _DEBUG:
                        v.tensor_copy(dbg_oh_sb[:, :, :], ohb[:, :, :]); D_(v)
                        v.tensor_copy(dbg_pp_sb[:, :, :], Pp[:, :, :]); D_(v)
                    EVr.inc(v.tensor_copy(cidx[:, 32:64], r2(scr[:, :, 3:4])))

                @blk.scalar
                def _(sc):
                    sc.wait_ge(EVr.h, 2)
                    EAr.inc(sc.activation(sm2[...], sm2[...], AF.Exp))

                @blk.tensor
                def _(t_):
                    t_.wait_ge(EVr.h, 1)
                    for t in range(NT):
                        if t > 0:
                            t_.wait_ge(EVr.h, 2 + 2 * (t - 1) + 1)
                        t_.matmul(psP[:, 0:8], C["utri"][:, :], ohb[:, t, 0:8],
                                  start=True, stop=True)
                        ins = t_.matmul(psP[:, 8:16], C["utri"][:, :], ohb[:, t, 8:16],
                                        start=True, stop=True)
                        MMr.inc(ins)
                        if t > 0:
                            t_.wait_ge(EVr.h, 2 + 2 * (t - 1) + 2)
                        t_.matmul(psT[:, 0:8], C["onesm"][:, :], ohb[:, t, 0:8],
                                  start=True, stop=True)
                        ins = t_.matmul(psT[:, 8:16], C["onesm"][:, :], ohb[:, t, 8:16],
                                        start=True, stop=True)
                        MMr.inc(ins)

                @blk.sync
                def _(s):
                    s.wait_ge(EVr.h, 2 + 2 * NT + 1)
                    STr.inc(s.dma_start(out=ap["totd"], in_=tot[:, :]), 16)

            EVr_done = EVr.n
            with ExitStack() as c3:
                lbuf = [sbuf(c3, tag + "lb%d" % i, [128, D], BF16) for i in range(2)]
                with nc.Block() as blk:
                    @blk.gpsimd
                    def _(g):
                        g.wait_ge(ld0.h, P0_LD)
                        g.wait_ge(ldg.h, 80)
                        g.wait_ge(lds.h, 16)
                        g.wait_ge(ldz.h, 144)
                        g.wait_ge(STr.h, 16)
                        cc1.inc(g.collective_compute(
                            "AllGather", ALU.bypass,
                            replica_groups=[list(range(NCORE))],
                            ins=[ap["totd"]], outs=[ap["cnt_all"]]))
                        g.wait_ge(EVr.h, EVr_done)
                        rb = g.to_reg(E * SHARD - 1)
                        for t in range(NT):
                            b = t % 2
                            g.wait_ge(LDD[b].h, 16 * (t // 2 + 1))
                            GID[b].inc(g.indirect_dma_start(
                                out=ap["ldisp"],
                                out_offset=bass.IndirectOffsetOnAxis(sidx[:, t:t + 1], 0),
                                in_=lbuf[b][:, :], in_offset=None,
                                bounds_check=rb, oob_is_err=False), 16)
                            GID[b].inc(g.indirect_dma_start(
                                out=ap["ldisp"],
                                out_offset=bass.IndirectOffsetOnAxis(sidx[:, 32 + t:33 + t], 0),
                                in_=lbuf[b][:, :], in_offset=None,
                                bounds_check=rb, oob_is_err=False), 16)

                    @blk.sync
                    def _(s):
                        for t in range(NT):
                            b = t % 2
                            if t >= 2:
                                s.wait_ge(GID[b].h, 32 * (t // 2))
                            LDD[b].inc(s.dma_start(out=lbuf[b][:, :],
                                                   in_=ap["lnx_my"][128 * t:128 * t + 128, :]), 16)

                with nc.Block() as blk:
                    @blk.gpsimd
                    def _(g):
                        g.wait_ge(GID[0].h, 32 * 16)
                        g.wait_ge(GID[1].h, 32 * 16)
                        with allow_ici_all_to_all():
                            cc2.inc(g.collective_compute(
                                "AllToAll", ALU.bypass,
                                replica_groups=[list(range(NCORE))],
                                ins=[ap["ldisp"]], outs=[ap["ddisp"]]))

                    @blk.sync
                    def _(s):
                        s.wait_ge(cc1.h, 1)
                        LD4.inc(s.dma_start(out=cnt_sb[:, :], in_=ap["cnt_all"]), 16)

            with ExitStack() as c5:
                psq = pst(c5, tag + "psq", [1, 16], F32)
                psq2 = pst(c5, tag + "psq2", [1, 16], F32)
                MM5 = Sem(nc, c5, tag + "mm5")
                EV5 = Sem(nc, c5, tag + "ev5")
                ST5 = Sem(nc, c5, tag + "st5")
                LD5 = Sem(nc, c5, tag + "ld5")
                with nc.Block() as blk:
                    @blk.tensor
                    def _(t_):
                        t_.wait_ge(LD4.h, 16)
                        t_.matmul(psq[:, :], C["prefv"][:, :], cnt_sb[:, :],
                                  start=True, stop=True)
                        MM5.inc(t_.matmul(psq2[:, :], C["ones8"][:, :], cnt_sb[:, :],
                                          start=True, stop=True))

                    @blk.vector
                    def _(v):
                        v.wait_ge(MM5.h, 1)
                        v.tensor_copy(pq[0:1, 0:16], psq[:, :]); D_(v)
                        EV5.inc(v.tensor_copy(pq[0:1, 16:32], psq2[:, :]))
                        v.wait_ge(LD5.h, 32)
                        v.tensor_tensor(scr[...], ohb[:, :, 0:8],
                                        prefbc[:, 0:8].rearrange("p (a e) -> p a e", a=1).to_broadcast([128, NT, 8]),
                                        ALU.mult); D_(v)
                        v.tensor_reduce(tmp1[...], scr[...], AX.X, ALU.add); D_(v)
                        v.tensor_tensor(tmp1[...], tmp1[...], locP[:, :, 0:1], ALU.add); D_(v)
                        v.tensor_scalar(tmp1[...], tmp1[...], -1.0, None, op0=ALU.add); D_(v)
                        v.tensor_scalar(msk[:, :, 0:1], tmp1[...], 8192.0, None, op0=ALU.is_lt); D_(v)
                        v.tensor_copy(tmp1[...], msk[:, :, 0:1]); D_(v)
                        v.tensor_tensor(geff[:, :, 0:1], g01[:, :, 0:1], tmp1[...], ALU.mult); D_(v)
                        v.tensor_tensor(scr[...], ohb[:, :, 8:16],
                                        prefbc[:, 8:16].rearrange("p (a e) -> p a e", a=1).to_broadcast([128, NT, 8]),
                                        ALU.mult); D_(v)
                        v.tensor_reduce(tmp1[...], scr[...], AX.X, ALU.add); D_(v)
                        v.tensor_tensor(scr[...], ohb[:, :, 8:16],
                                        totA[:, 0:8].rearrange("p (a e) -> p a e", a=1).to_broadcast([128, NT, 8]),
                                        ALU.mult); D_(v)
                        v.tensor_reduce(sm1[...], scr[...], AX.X, ALU.add); D_(v)
                        v.tensor_tensor(tmp1[...], tmp1[...], sm1[...], ALU.add); D_(v)
                        v.tensor_tensor(scr[...], ohb[:, :, 8:16],
                                        totbc[:, 0:8].rearrange("p (a e) -> p a e", a=1).to_broadcast([128, NT, 8]),
                                        ALU.mult); D_(v)
                        v.tensor_reduce(sm1[...], scr[...], AX.X, ALU.add); D_(v)
                        v.tensor_tensor(tmp1[...], tmp1[...], sm1[...], ALU.subtract); D_(v)
                        v.tensor_tensor(tmp1[...], tmp1[...], locP[:, :, 1:2], ALU.add); D_(v)
                        v.tensor_scalar(tmp1[...], tmp1[...], -1.0, None, op0=ALU.add); D_(v)
                        v.tensor_scalar(msk[:, :, 0:1], tmp1[...], 8192.0, None, op0=ALU.is_lt); D_(v)
                        v.tensor_copy(tmp1[...], msk[:, :, 0:1]); D_(v)
                        v.tensor_tensor(geff[:, :, 1:2], g01[:, :, 1:2], tmp1[...], ALU.mult)
                        v.drain()
                        EV5.inc(v.tensor_copy(geff[:, 0:1, 0:1], geff[:, 0:1, 0:1]))
                        grdy.inc(v.tensor_copy(sm1[:, 0:1, 0:1], sm1[:, 0:1, 0:1]))

                    @blk.sync
                    def _(s):
                        s.wait_ge(EV5.h, 1)
                        ST5.inc(s.dma_start(out=ap["pq_d"], in_=pq[:, :]), 16)

                    @blk.scalar
                    def _(sc):
                        sc.wait_ge(ST5.h, 16)
                        b0 = bass.AP(_th["pq_d"].ap().tensor, 0, [[0, 128], [1, 16]])
                        b1 = bass.AP(_th["pq_d"].ap().tensor, 16, [[0, 128], [1, 16]])
                        LD5.inc(sc.dma_start(out=prefbc[:, :], in_=b0), 16)
                        LD5.inc(sc.dma_start(out=totA[:, :], in_=b1), 16)

        if _DEBUG and tag == "f1":
            dbs = Sem(nc, fctx, tag + "dbs")
            with nc.Block() as blk:
                @blk.sync
                def _(s):
                    dbs.inc(s.dma_start(out=ap["dbg_rt"][:, 0:32, 0:1].rearrange("p a b -> p (a b)"), in_=r2(geff[:, :, 0:1]), allow_slow_non_contiguous=True), 16)
                    dbs.inc(s.dma_start(out=ap["dbg_rt"][:, 0:32, 1:2].rearrange("p a b -> p (a b)"), in_=r2(geff[:, :, 1:2]), allow_slow_non_contiguous=True), 16)
                    dbs.inc(s.dma_start(out=ap["dbg_rt"][:, 0:64, 2:3].rearrange("p a b -> p (a b)"), in_=sidx[:, :].bitcast(F32), allow_slow_non_contiguous=True), 16)
                    dbs.inc(s.dma_start(out=ap["dbg_rt"][:, 0:64, 3:4].rearrange("p a b -> p (a b)"), in_=cidx[:, :].bitcast(F32), allow_slow_non_contiguous=True), 16)
                    dbs.inc(s.dma_start(out=ap["dbg_rt"][:, 0:32, 4:5].rearrange("p a b -> p (a b)"), in_=r2(logits[:, :, 0:1]), allow_slow_non_contiguous=True), 16)
                    dbs.inc(s.dma_start(out=ap["dbg_lg"], in_=logits[:, :, :]), 16)
                    dbs.inc(s.dma_start(out=ap["dbg_oh"], in_=dbg_oh_sb[:, :, :]), 16)
                    dbs.inc(s.dma_start(out=ap["dbg_pp"], in_=dbg_pp_sb[:, :, :]), 16)
                    s.wait_ge(dbs.h, dbs.n)
        emit_expert(tag, wa, wb, cc2)
        with nc.Block() as blk:
            @blk.gpsimd
            def _(g):
                with allow_ici_all_to_all():
                    cc3.inc(g.collective_compute(
                        "AllToAll", ALU.bypass,
                        replica_groups=[list(range(NCORE))],
                        ins=[ap["eo_my"]], outs=[ap["eo_ret"]]))
        emit_combine(tag, src_x, dst_x, cidx, geff, cc3, grdy, 1, done)
        fctx.close()

    def emit_expert(tag, wa, wb, ccdep):
        with ExitStack() as ce:
            dbuf = [[sbuf(ce, tag + "db%d_%d" % (i, j), [128, D], BF16)
                     for j in range(4)] for i in range(2)]
            xT = sbuf(ce, tag + "xT", [128, 8, 512], BF16)
            hT = sbuf(ce, tag + "hT", [128, 32, 512], BF16)
            obuf = sbuf(ce, tag + "obuf", [128, 4, D], BF16)
            ptT = [pst(ce, tag + "ptT%d" % i, [128, 1024], BF16) for i in range(2)]
            ph = [pst(ce, tag + "ph%d" % i, [128, 512], F32) for i in range(2)]
            po = [pst(ce, tag + "po%d" % i, [128, 512], F32) for i in range(2)]
            LDE = [[Sem(nc, ce, tag + "lde%d_%d" % (i, j)) for j in range(4)]
                   for i in range(2)]
            STE = [Sem(nc, ce, tag + "ste%d" % r) for r in range(4)]
            MME = Sem(nc, ce, tag + "mme")
            EVE = Sem(nc, ce, tag + "eve")
            EAE = Sem(nc, ce, tag + "eae")

            def dd_row(t):
                return (t // 10) * SHARD + (t % 10) * 128

            with nc.Block() as blk:
                @blk.sync
                def _(s):
                    s.wait_ge(ccdep.h, 1)
                    for st in range(NST):
                        i = st % 2
                        for j in range(2):
                            t = 4 * st + j
                            if st >= 2:
                                s.wait_ge(MME.h, 44 * (st - 2) + j + 1)
                            r = dd_row(t)
                            LDE[i][j].inc(s.dma_start(out=dbuf[i][j][:, :],
                                                      in_=ap["ddisp"][r:r + 128, :]), 16)

                @blk.tensor
                def _(t_):
                    for st in range(NST):
                        i = st % 2
                        for j in range(4):
                            t_.wait_ge(LDE[i][j].h, 16 * (st // 2 + 1))
                            if 4 * st + j >= 2:
                                prev = 4 * st + j - 2
                                t_.wait_ge(EVE.h, 12 * (prev // 4) + (prev % 4) + 1)
                            pt = ptT[j % 2]
                            for k in range(8):
                                ins = t_.transpose(pt[:, 128 * k:128 * k + 128],
                                                   dbuf[i][j][:, 128 * k:128 * k + 128],
                                                   C["ident"][:, :])
                            MME.inc(ins)
                        t_.wait_ge(EVE.h, 12 * st + 4)
                        for hg in range(32):
                            gi = 32 * st + hg
                            if gi >= 2:
                                t_.wait_ge(EAE.h, gi - 1)
                            p = ph[hg % 2]
                            for k in range(8):
                                ins = t_.matmul(p[:, :], wa[:, k, 128 * hg:128 * hg + 128],
                                                xT[:, k, :], start=(k == 0), stop=(k == 7))
                            MME.inc(ins)
                        t_.wait_ge(EAE.h, 32 * (st + 1))
                        for rs in range(4):
                            for dh in range(2):
                                m = 8 * st + rs * 2 + dh
                                if m >= 2:
                                    pm = m - 2
                                    t_.wait_ge(EVE.h, 12 * (pm // 8) + 4 + (pm % 8) + 1)
                                p = po[dh]
                                for hg in range(32):
                                    ins = t_.matmul(p[:, :], hT[:, hg, 128 * rs:128 * rs + 128],
                                                    wb[:, hg, 512 * dh:512 * dh + 512],
                                                    start=(hg == 0), stop=(hg == 31))
                                MME.inc(ins)

                @blk.vector
                def _(v):
                    for st in range(NST):
                        for j in range(4):
                            v.wait_ge(MME.h, 44 * st + j + 1)
                            pt = ptT[j % 2]
                            EVE.inc(v.tensor_copy(xT[:, :, 128 * j:128 * j + 128],
                                                  pt[:, :].rearrange("p (k q) -> p k q", q=128)))
                        for rs in range(4):
                            for dh in range(2):
                                m = rs * 2 + dh
                                v.wait_ge(MME.h, 44 * st + 36 + m + 1)
                                if st >= 1:
                                    v.wait_ge(STE[rs].h, 16 * st)
                                EVE.inc(v.tensor_tensor(obuf[:, rs, 512 * dh:512 * dh + 512],
                                                        po[dh][:, :],
                                                        C["b2bc"][:, 512 * dh:512 * dh + 512],
                                                        ALU.add))

                @blk.scalar
                def _(sc):
                    sc.wait_ge(ccdep.h, 1)
                    for st in range(NST):
                        i = st % 2
                        for j in (2, 3):
                            t = 4 * st + j
                            if st >= 2:
                                sc.wait_ge(MME.h, 44 * (st - 2) + j + 1)
                            r = dd_row(t)
                            LDE[i][j].inc(sc.dma_start(out=dbuf[i][j][:, :],
                                                       in_=ap["ddisp"][r:r + 128, :]), 16)
                        for hg in range(32):
                            sc.wait_ge(MME.h, 44 * st + 4 + hg + 1)
                            EAE.inc(sc.activation(hT[:, hg, :], ph[hg % 2][:, :], AF.Relu,
                                                  bias=C["b1t"][:, hg:hg + 1]))

                @blk.gpsimd
                def _(g):
                    for st in range(NST):
                        for rs in range(4):
                            g.wait_ge(EVE.h, 12 * st + 4 + rs * 2 + 2)
                            STE[rs].inc(g.dma_start(
                                out=ap["eo_my"][512 * st + 128 * rs:512 * st + 128 * rs + 128, :],
                                in_=obuf[:, rs, :]), 16)

            with nc.Block() as blk:
                @blk.gpsimd
                def _(g):
                    for rs in range(4):
                        g.wait_ge(STE[rs].h, 16 * NST)

    def emit_combine(tag, src_x, dst_x, cidx, geff, ccdep, gsem, gval, done):
        with ExitStack() as cc_:
            ga = [[sbuf(cc_, tag + "ga%d_%d" % (i, j), [128, D], BF16)
                   for j in range(2)] for i in range(2)]
            xc = [sbuf(cc_, tag + "xc%d" % i, [128, D], F32) for i in range(2)]
            xo = [sbuf(cc_, tag + "xo%d" % i, [128, D], F32) for i in range(2)]
            tmpf = [sbuf(cc_, tag + "ctmp%d" % i, [128, D], F32) for i in range(2)]
            mk1 = sbuf(cc_, tag + "cmk", [1, 2], F32)
            GIC = [Sem(nc, cc_, tag + "gic%d" % i) for i in range(2)]
            LDC = [Sem(nc, cc_, tag + "ldc%d" % i) for i in range(2)]
            EVC = Sem(nc, cc_, tag + "evc")
            STC = [Sem(nc, cc_, tag + "stc%d" % i) for i in range(2)]
            with nc.Block() as blk:
                @blk.gpsimd
                def _(g):
                    g.wait_ge(ccdep.h, 1)
                    rb = g.to_reg(E * CAPL - 1)
                    for t in range(NT):
                        b = t % 2
                        if t >= 2:
                            g.wait_ge(EVC.h, 3 * (t - 2) + 2)
                        GIC[b].inc(g.indirect_dma_start(
                            out=ga[b][0][:, :], out_offset=None,
                            in_=ap["eo_ret"],
                            in_offset=bass.IndirectOffsetOnAxis(cidx[:, t:t + 1], 0),
                            bounds_check=rb, oob_is_err=False), 16)
                        GIC[b].inc(g.indirect_dma_start(
                            out=ga[b][1][:, :], out_offset=None,
                            in_=ap["eo_ret"],
                            in_offset=bass.IndirectOffsetOnAxis(cidx[:, 32 + t:33 + t], 0),
                            bounds_check=rb, oob_is_err=False), 16)

                @blk.sync
                def _(s):
                    for t in range(NT):
                        b = t % 2
                        if t >= 2:
                            s.wait_ge(EVC.h, 3 * (t - 2) + 3)
                        LDC[b].inc(s.dma_start(out=xc[b][:, :],
                                               in_=src_x[128 * t:128 * t + 128, :]), 16)

                @blk.vector
                def _(v):
                    v.wait_ge(gsem.h, gval)
                    for t in range(NT):
                        b = t % 2
                        v.wait_ge(GIC[b].h, 32 * (t // 2 + 1))
                        v.wait_ge(LDC[b].h, 16 * (t // 2 + 1))
                        if t >= 2:
                            v.wait_ge(STC[b].h, 16 * ((t - 2) // 2 + 1))
                        v.tensor_scalar(tmpf[b][:, :], ga[b][0][:, :],
                                        r2(geff[:, t:t + 1, 0:1]), None, op0=ALU.mult)
                        v.drain()
                        EVC.inc(v.scalar_tensor_tensor(tmpf[b][:, :], ga[b][1][:, :],
                                                       r2(geff[:, t:t + 1, 1:2]), tmpf[b][:, :],
                                                       op0=ALU.mult, op1=ALU.add))
                        v.drain()
                        EVC.inc(v.tensor_copy(mk1[0:1, 0:1], xc[b][0:1, 0:1]))
                        EVC.inc(v.tensor_tensor(xo[b][:, :], xc[b][:, :], tmpf[b][:, :], ALU.add))

                @blk.scalar
                def _(sc):
                    for t in range(NT):
                        b = t % 2
                        sc.wait_ge(EVC.h, 3 * t + 3)
                        STC[b].inc(sc.dma_start(out=dst_x[128 * t:128 * t + 128, :],
                                                in_=xo[b][:, :]), 16)
                    sc.wait_ge(STC[0].h, 16 * (NT // 2))
                    sc.wait_ge(STC[1].h, 16 * (NT // 2))
                    done.inc(sc.activation(tmpf[0][0:1, 0:1], tmpf[0][0:1, 0:1], AF.Copy))

    def emit_ln2(tag, src_x, dst_ln, dep, depval, done):
        with ExitStack() as cl:
            xin = [sbuf(cl, tag + "xi%d" % i, [128, D], F32) for i in range(2)]
            lnb = [sbuf(cl, tag + "lo%d" % i, [128, D], BF16) for i in range(2)]
            sqs = sbuf(cl, tag + "sq", [128, D], F32)
            st1 = sbuf(cl, tag + "st", [128, 8], F32)
            LD = [Sem(nc, cl, tag + "ld%d" % i) for i in range(2)]
            ST = [Sem(nc, cl, tag + "st%d" % i) for i in range(2)]
            EV = Sem(nc, cl, tag + "ev")
            EA = Sem(nc, cl, tag + "ea")
            with nc.Block() as blk:
                @blk.sync
                def _(s):
                    s.wait_ge(dep.h, depval)
                    for t in range(NT):
                        b = t % 2
                        if t >= 2:
                            s.wait_ge(EV.h, 2 * (t - 2) + 2)
                        LD[b].inc(s.dma_start(out=xin[b][:, :],
                                              in_=src_x[128 * t:128 * t + 128, :]), 16)
                        if t >= 1:
                            s.wait_ge(EV.h, 2 * (t - 1) + 2)
                            ST[(t - 1) % 2].inc(s.dma_start(
                                out=dst_ln[128 * (t - 1):128 * t, :],
                                in_=lnb[(t - 1) % 2][:, :]), 16)
                    s.wait_ge(EV.h, 2 * (NT - 1) + 2)
                    ST[(NT - 1) % 2].inc(s.dma_start(
                        out=dst_ln[128 * (NT - 1):128 * NT, :],
                        in_=lnb[(NT - 1) % 2][:, :]), 16)

                @blk.vector
                def _(v):
                    for t in range(NT):
                        b = t % 2
                        v.wait_ge(LD[b].h, 16 * (t // 2 + 1))
                        v.tensor_reduce(st1[:, 4 * b:4 * b + 1], xin[b][:, :], AX.X, ALU.add)
                        v.drain()
                        v.wait_ge(EA.h, 2 * t + 1)
                        v.tensor_scalar(st1[:, 4 * b:4 * b + 1], st1[:, 4 * b:4 * b + 1],
                                        1.0 / D, None, op0=ALU.mult)
                        v.drain()
                        v.tensor_tensor(st1[:, 4 * b + 2:4 * b + 3], st1[:, 4 * b:4 * b + 1],
                                        st1[:, 4 * b:4 * b + 1], ALU.mult)
                        v.drain()
                        EV.inc(v.scalar_tensor_tensor(st1[:, 4 * b + 1:4 * b + 2],
                                                      st1[:, 4 * b + 1:4 * b + 2], 1.0 / D,
                                                      st1[:, 4 * b + 2:4 * b + 3],
                                                      op0=ALU.mult, op1=ALU.subtract))
                        v.wait_ge(EA.h, 2 * t + 2)
                        if t >= 2:
                            v.wait_ge(ST[b].h, 16 * ((t - 2) // 2 + 1))
                        EV.inc(v.tensor_scalar(lnb[b][:, :], xin[b][:, :],
                                               st1[:, 4 * b:4 * b + 1],
                                               st1[:, 4 * b + 1:4 * b + 2],
                                               op0=ALU.subtract, op1=ALU.mult))

                @blk.scalar
                def _(sc):
                    for t in range(NT):
                        b = t % 2
                        sc.wait_ge(LD[b].h, 16 * (t // 2 + 1))
                        if t >= 2:
                            sc.wait_ge(EV.h, 2 * (t - 2) + 2)
                        EA.inc(sc.activation(sqs[:, :], xin[b][:, :], AF.Square,
                                             accum_out=st1[:, 4 * b + 1:4 * b + 2]))
                        sc.wait_ge(EV.h, 2 * t + 1)
                        sc.activation(st1[:, 4 * b + 1:4 * b + 2],
                                      st1[:, 4 * b + 1:4 * b + 2], AF.Ln,
                                      bias=C["fconst"][:, 0:1])
                        sc.drain()
                        EA.inc(sc.activation(st1[:, 4 * b + 1:4 * b + 2],
                                             st1[:, 4 * b + 1:4 * b + 2], AF.Exp,
                                             scale=C["fconst"][:, 1:2]))
                    sc.wait_ge(ST[0].h, 16 * (NT // 2))
                    sc.wait_ge(ST[1].h, 16 * (NT // 2))
                    done.inc(sc.activation(st1[0:1, 0:1], st1[0:1, 0:1], AF.Copy))

    def emit_mamba(mdone):
        PAIRS = [[0, 1], [2, 3], [4, 5], [6, 7]]
        mctx = ExitStack()
        woutb = sbuf(mctx, "woutb", [128, 8, D], BF16)
        wxpb = sbuf(mctx, "wxpb", [128, 4, 96], BF16)
        wdtcb = sbuf(mctx, "wdtcb", [64, 512], BF16)
        dtBC = sbuf(mctx, "dtBC", [128, SEQ], F32)
        ldm = Sem(nc, mctx, "ldm")
        lndone = Sem(nc, mctx, "lndone")
        ccM2 = Sem(nc, mctx, "ccM2")
        ccM3 = Sem(nc, mctx, "ccM3")
        ST13 = Sem(nc, mctx, "st13")

        with nc.Block() as blk:
            @blk.gpsimd
            def _(g):
                g.wait_ge(f1done.h, 1)
                ldm.inc(g.dma_start(out=wxpb[:, :, :],
                        in_=ap["wxp"].rearrange("(k p) c -> p k c", p=128)), 16)
                ldm.inc(g.dma_start(out=wdtcb[:, :], in_=ap["wdtc"]), 16)
                ldm.inc(g.dma_start(out=woutb[:, :, :],
                        in_=ap["woutr"].rearrange("(k p) d -> p k d", p=128)), 16)

        emit_ln2("m9", ap["x1_d"], ap["lnx_my"], f1done, 1, lndone)

        with ExitStack() as c11:
            wincb = sbuf(c11, "wincb", [128, 8, D], BF16)
            rT = [sbuf(c11, "m11r%d" % i, [128, 8, 512], BF16) for i in range(2)]
            xzs = [sbuf(c11, "m11x%d" % i, [128, 8, 512], BF16) for i in range(2)]
            pz = [pst(c11, "m11p%d" % i, [128, 512], F32) for i in range(2)]
            LTI = [Sem(nc, c11, "m11lt%d" % i) for i in range(2)]
            STI = [Sem(nc, c11, "m11st%d" % i) for i in range(2)]
            EVI = Sem(nc, c11, "m11ev")
            MMI = Sem(nc, c11, "m11mm")
            ccM1 = Sem(nc, c11, "ccM1")
            with nc.Block() as blk:
                @blk.gpsimd
                def _(g):
                    g.wait_ge(f1done.h, 1)
                    ldm.inc(g.dma_start(out=wincb[:, :, :],
                            in_=ap["winc"].rearrange("(k p) i -> p k i", p=128)), 16)
                    g.wait_ge(lndone.h, 1)
                    ccM1.inc(g.collective_compute(
                        "AllGather", ALU.bypass, replica_groups=PAIRS,
                        ins=[ap["lnx_my"]], outs=[ap["ln2pair"]]))

            with nc.Block() as blk:
                @blk.scalar
                def _(sc):
                    sc.wait_ge(ccM1.h, 1)
                    for tc in range(NTC):
                        b = tc % 2
                        if tc >= 2:
                            sc.wait_ge(MMI.h, 8 * (tc - 1))
                        for k in range(8):
                            LTI[b].inc(sc.dma_start(
                                out=rT[b][:, k, :],
                                in_=ap["ln2pair"][512 * tc:512 * tc + 512,
                                                  128 * k:128 * k + 128],
                                transpose=True), 16)

                @blk.tensor
                def _(t_):
                    t_.wait_ge(ldm.h, 64)
                    for tc in range(NTC):
                        b = tc % 2
                        t_.wait_ge(LTI[b].h, 128 * (tc // 2 + 1))
                        for m in range(8):
                            gidx = 8 * tc + m
                            if gidx >= 2:
                                t_.wait_ge(EVI.h, gidx - 1)
                            for k in range(8):
                                ins = t_.matmul(pz[m % 2][:, :],
                                                wincb[:, k, 128 * m:128 * m + 128],
                                                rT[b][:, k, :],
                                                start=(k == 0), stop=(k == 7))
                            MMI.inc(ins)

                @blk.vector
                def _(v):
                    for tc in range(NTC):
                        b2 = tc % 2
                        for m in range(8):
                            gidx = 8 * tc + m
                            v.wait_ge(MMI.h, gidx + 1)
                            if tc >= 2 and m == 0:
                                v.wait_ge(STI[b2].h, 16 * ((tc - 2) // 2 + 1))
                            EVI.inc(v.tensor_copy(xzs[b2][:, m, :], pz[m % 2][:, :]))

                @blk.sync
                def _(s):
                    for tc in range(NTC):
                        b2 = tc % 2
                        s.wait_ge(EVI.h, 8 * (tc + 1))
                        STI[b2].inc(s.dma_start(
                            out=ap["xz_d"].rearrange("(m p) t -> p m t", p=128)[
                                :, :, 512 * tc:512 * tc + 512],
                            in_=xzs[b2][:, :, :]), 16)
                    s.wait_ge(STI[0].h, 16 * 8)
                    s.wait_ge(STI[1].h, 16 * 8)

        with ExitStack() as c12:
            czin = [sbuf(c12, "m12c%d" % i, [128, 8, 516], BF16) for i in range(2)]
            cf = sbuf(c12, "m12f", [128, 8, 512], F32)
            xhzs = [sbuf(c12, "m12h%d" % i, [128, 8, 512], BF16) for i in range(2)]
            stg = [sbuf(c12, "m12s%d" % i, [128, 512], F32) for i in range(2)]
            px = pst(c12, "m12p", [128, 512], F32)
            LCZ = [Sem(nc, c12, "m12l%d" % i) for i in range(2)]
            STC2 = [Sem(nc, c12, "m12t%d" % i) for i in range(2)]
            STX = [Sem(nc, c12, "m12x%d" % i) for i in range(2)]
            EVC = Sem(nc, c12, "m12ev")
            EAC = Sem(nc, c12, "m12ea")
            EAX = Sem(nc, c12, "m12ax")
            MMX = Sem(nc, c12, "m12mm")
            with nc.Block() as blk:
                @blk.sync
                def _(s):
                    for tc in range(NTC):
                        b = tc % 2
                        if tc >= 2:
                            s.wait_ge(EVC.h, 32 * (tc - 1))
                        xzv = ap["xz_d"].rearrange("(g p) t -> p g t", p=128)
                        if tc == 0:
                            LCZ[b].inc(s.dma_start(out=czin[b][:, :, 1:516],
                                                   in_=xzv[:, :, 0:515]), 16)
                        elif tc == NTC - 1:
                            LCZ[b].inc(s.dma_start(out=czin[b][:, :, 0:513],
                                                   in_=xzv[:, :, 512 * tc - 1:SEQ]), 16)
                        else:
                            LCZ[b].inc(s.dma_start(
                                out=czin[b][:, :, 0:515],
                                in_=xzv[:, :, 512 * tc - 1:512 * tc + 514]), 16)
                        if tc >= 1:
                            bp = (tc - 1) % 2
                            s.wait_ge(EAC.h, 8 * (tc - 1) + 8)
                            STC2[bp].inc(s.dma_start(
                                out=ap["xhc_d"].rearrange("(g p) t -> p g t", p=128)[
                                    :, :, 512 * (tc - 1):512 * tc],
                                in_=xhzs[bp][:, 0:4, :]), 16)
                            STC2[bp].inc(s.dma_start(
                                out=ap["zc_d"].rearrange("(g p) t -> p g t", p=128)[
                                    :, :, 512 * (tc - 1):512 * tc],
                                in_=xhzs[bp][:, 4:8, :]), 16)
                            s.wait_ge(EAX.h, tc)
                            STX[bp].inc(s.dma_start(
                                out=ap["xdbl_my"][0:96, 512 * (tc - 1):512 * tc],
                                in_=stg[bp][0:96, :]), 16)
                    bp = (NTC - 1) % 2
                    s.wait_ge(EAC.h, 8 * NTC)
                    STC2[bp].inc(s.dma_start(
                        out=ap["xhc_d"].rearrange("(g p) t -> p g t", p=128)[
                            :, :, 512 * (NTC - 1):SEQ],
                        in_=xhzs[bp][:, 0:4, :]), 16)
                    STC2[bp].inc(s.dma_start(
                        out=ap["zc_d"].rearrange("(g p) t -> p g t", p=128)[
                            :, :, 512 * (NTC - 1):SEQ],
                        in_=xhzs[bp][:, 4:8, :]), 16)
                    s.wait_ge(EAX.h, NTC)
                    STX[bp].inc(s.dma_start(
                        out=ap["xdbl_my"][0:96, 512 * (NTC - 1):SEQ],
                        in_=stg[bp][0:96, :]), 16)
                    s.wait_ge(STC2[0].h, 16 * 16)
                    s.wait_ge(STC2[1].h, 16 * 16)
                    s.wait_ge(STX[0].h, 16 * 8)
                    s.wait_ge(STX[1].h, 16 * 8)

                @blk.vector
                def _(v):
                    v.memset(czin[0][:, :, 0:1], 0.0)
                    v.memset(czin[1][:, :, 0:1], 0.0)
                    v.memset(stg[0][0:1, 0:2], 0.0)
                    for tc in range(NTC):
                        b = tc % 2
                        v.wait_ge(LCZ[b].h, 16 * (tc // 2 + 1))
                        if tc == NTC - 1:
                            v.memset(czin[b][:, :, 513:515], 0.0)
                        for j in range(4):
                            for gch in range(8):
                                if j == 0:
                                    if tc >= 1:
                                        v.wait_ge(EAC.h, 8 * (tc - 1) + gch + 1)
                                    EVC.inc(v.tensor_scalar(
                                        cf[:, gch, :], czin[b][:, gch, 0:512],
                                        C["ck"][:, gch, 0:1], None, op0=ALU.mult))
                                else:
                                    EVC.inc(v.scalar_tensor_tensor(
                                        cf[:, gch, :], czin[b][:, gch, j:j + 512],
                                        C["ck"][:, gch, j:j + 1], cf[:, gch, :],
                                        op0=ALU.mult, op1=ALU.add))

                @blk.scalar
                def _(sc):
                    for tc in range(NTC):
                        b = tc % 2
                        for gch in range(8):
                            sc.wait_ge(EVC.h, 32 * tc + 25 + gch)
                            if gch == 0 and tc >= 2:
                                sc.wait_ge(STC2[b].h, 32 * ((tc - 2) // 2 + 1))
                                sc.wait_ge(MMX.h, tc - 1)
                            EAC.inc(sc.activation(xhzs[b][:, gch, :], cf[:, gch, :],
                                                  AF.Silu))
                        sc.wait_ge(MMX.h, tc + 1)
                        if tc >= 2:
                            sc.wait_ge(STX[b].h, 16 * ((tc - 2) // 2 + 1))
                        EAX.inc(sc.activation(stg[b][0:96, :], px[0:96, :], AF.Copy))

                @blk.tensor
                def _(t_):
                    t_.wait_ge(ldm.h, 64)
                    for tc in range(NTC):
                        b = tc % 2
                        t_.wait_ge(EAC.h, 8 * tc + 4)
                        if tc >= 1:
                            t_.wait_ge(EAX.h, tc)
                        for k in range(4):
                            ins = t_.matmul(px[0:96, :], wxpb[:, k, :],
                                            xhzs[b][:, k, :],
                                            start=(k == 0), stop=(k == 3))
                        MMX.inc(ins)

            with nc.Block() as blk:
                @blk.gpsimd
                def _(g):
                    g.wait_ge(STX[0].h, 16 * 8)
                    g.wait_ge(STX[1].h, 16 * 8)
                    ccM2.inc(g.collective_compute(
                        "AllGather", ALU.bypass, replica_groups=PAIRS,
                        ins=[ap["xdbl_my"]], outs=[ap["xdbl_pair"]]))

        with ExitStack() as c13:
            L0 = sbuf(c13, "m13a", [128, SEQ], F32)
            L1 = sbuf(c13, "m13b", [128, SEQ], F32)
            LD13 = Sem(nc, c13, "m13l")
            EV13 = Sem(nc, c13, "m13e")
            with nc.Block() as blk:
                @blk.sync
                def _(s):
                    s.wait_ge(ccM2.h, 1)
                    LD13.inc(s.dma_start(out=L0[0:96, :], in_=ap["xdbl_pair"][0:96, :]), 16)
                    LD13.inc(s.dma_start(out=L1[0:96, :], in_=ap["xdbl_pair"][96:192, :]), 16)

                @blk.vector
                def _(v):
                    v.wait_ge(LD13.h, 32)
                    EV13.inc(v.tensor_tensor(dtBC[0:96, :], L0[0:96, :], L1[0:96, :],
                                             ALU.add))

                @blk.gpsimd
                def _(g):
                    g.wait_ge(EV13.h, 1)
                    ST13.inc(g.dma_start(out=ap["bd_d"], in_=dtBC[64:80, :]), 16)
                    ST13.inc(g.dma_start(out=ap["cd_d"], in_=dtBC[80:96, :]), 16)

        VPT, APT, PPT = 117, 80, 80
        ev_dtrb = lambda tc: VPT * tc + 1
        ev_deltaT = lambda tc, m: VPT * tc + 2 + m
        ev_du = lambda tc: VPT * tc + 9
        scan_done = lambda tc, n: VPT * tc + (13 if n == 0 else 6 * n + 11)
        hc_done = lambda tc, n: VPT * tc + (104 if n == 15 else 6 * n + 18)
        ev_ybf = lambda tc: VPT * tc + 109
        ev_evac = lambda tc, m: VPT * tc + 110 + m
        ea_abs = lambda tc, m: APT * tc + 2 * m + 2
        ea_ln = lambda tc, m: APT * tc + 13 + m
        ea_da = lambda tc, n, g: APT * tc + 17 + 4 * n + g
        mm_dt = lambda tc, m: PPT * tc + 1 + m
        mm_yi = lambda tc, g: PPT * tc + 5 + g
        mm_ya = lambda tc, n, g: PPT * tc + 9 + 4 * n + g
        mm_op = lambda tc, m: PPT * tc + 73 + m
        scf_ = lambda fn: scan_done(fn // 16, fn % 16)
        hcf_ = lambda fn: hc_done(fn // 16, fn % 16)
        with ExitStack() as c14:
            xhcb = [sbuf(c14, "m14u%d" % i, [128, 4, 512], BF16) for i in range(2)]
            zcb = [sbuf(c14, "m14z%d" % i, [128, 4, 512], BF16) for i in range(2)]
            dtrb = [sbuf(c14, "m14d%d" % i, [64, 512], BF16) for i in range(2)]
            rbuf = sbuf(c14, "m14r", [128, 4, 512], BF16)
            abuf = sbuf(c14, "m14a", [128, 4, 512], BF16)
            deltaT = sbuf(c14, "m14t", [128, 4, 512], BF16)
            dub = sbuf(c14, "m14w", [128, 4, 512], BF16)
            wb = [sbuf(c14, "m14v%d" % i, [128, 4, 512], F32) for i in range(2)]
            dab = [sbuf(c14, "m14e%d" % i, [128, 4, 512], F32) for i in range(2)]
            hcb = [sbuf(c14, "m14h%d" % i, [128, 4, 512], BF16) for i in range(2)]
            hng = [sbuf(c14, "m14g%d" % i, [128, 4, 512], F32) for i in range(4)]
            Bbc = [sbuf(c14, "m14B%d" % i, [128, 512], BF16) for i in range(2)]
            Cbc = [sbuf(c14, "m14C%d" % i, [128, 512], BF16) for i in range(2)]
            ybfb = [sbuf(c14, "m14q%d" % i, [128, 4, 512], BF16) for i in range(2)]
            ost = [sbuf(c14, "m14o%d" % i, [128, 8, 512], BF16) for i in range(2)]
            carry = sbuf(c14, "m14k", [128, 64], F32)
            psd2 = [pst(c14, "m14pd%d" % i, [128, 512], F32) for i in range(2)]
            pso = [pst(c14, "m14po%d" % i, [128, 512], F32) for i in range(2)]
            psy = [pst(c14, "m14py%d" % i, [128, 512], F32) for i in range(4)]
            LXH = [Sem(nc, c14, "m14lx%d" % i) for i in range(2)]
            LZC = [Sem(nc, c14, "m14lz%d" % i) for i in range(2)]
            LBB = [Sem(nc, c14, "m14lb%d" % i) for i in range(2)]
            LCC = [Sem(nc, c14, "m14lc%d" % i) for i in range(2)]
            STO = [Sem(nc, c14, "m14so%d" % i) for i in range(2)]
            EVS = Sem(nc, c14, "m14ev")
            EAS = Sem(nc, c14, "m14ea")
            MMP = Sem(nc, c14, "m14mm")
            GPW = Sem(nc, c14, "m14gw")
            with nc.Block() as blk:
                @blk.sync
                def _(s):
                    s.wait_ge(ST13.h, 32)
                    for tc in range(NTC):
                        b = tc % 2
                        if tc >= 2:
                            s.wait_ge(EVS.h, ev_du(tc - 2))
                            s.wait_ge(MMP.h, mm_yi(tc - 2, 3))
                        LXH[b].inc(s.dma_start(
                            out=xhcb[b][:, :, :],
                            in_=ap["xhc_d"].rearrange("(g p) t -> p g t", p=128)[
                                :, :, 512 * tc:512 * tc + 512]), 16)
                        if tc >= 2:
                            s.wait_ge(MMP.h, PPT * (tc - 1))
                        LZC[b].inc(s.dma_start(
                            out=zcb[b][:, :, :],
                            in_=ap["zc_d"].rearrange("(g p) t -> p g t", p=128)[
                                :, :, 512 * tc:512 * tc + 512]), 16)
                        for n in range(16):
                            fn = 16 * tc + n
                            if fn >= 2:
                                s.wait_ge(GPW.h, fn - 1)
                            src = bass.AP(_th["bd_d"].ap().tensor,
                                          n * SEQ + 512 * tc, [[0, 128], [1, 512]])
                            LBB[n % 2].inc(s.dma_start(out=Bbc[n % 2][:, :], in_=src), 16)

                @blk.gpsimd
                def _(g):
                    for tc in range(NTC):
                        if tc == 9:
                            g.wait_ge(STO[0].h, 16 * 4)
                            g.wait_ge(STO[1].h, 16 * 4)
                            ccM3.inc(g.collective_compute(
                                "AllGather", ALU.bypass, replica_groups=PAIRS,
                                ins=[ap["op_send"][0:1024, :]],
                                outs=[ap["op_pag"][0:2048, :]]))
                        for n in range(16):
                            fn = 16 * tc + n
                            if n == 0:
                                g.wait_ge(EVS.h, ev_du(tc))
                            g.wait_ge(LBB[n % 2].h, 16 * (8 * tc + n // 2 + 1))
                            if fn >= 2:
                                g.wait_ge(EVS.h, scf_(fn - 2))
                            GPW.inc(g.tensor_tensor(
                                wb[n % 2][:, :, :], dub[:, :, :],
                                Bbc[n % 2][:, :].rearrange(
                                    "p (a t) -> p a t", a=1).to_broadcast([128, 4, 512]),
                                ALU.mult))

                @blk.scalar
                def _(sc):
                    sc.wait_ge(ST13.h, 32)
                    for tc in range(NTC):
                        b = tc % 2
                        if tc >= 1:
                            sc.wait_ge(EVS.h, ev_evac(tc - 1, 7))
                            STO[(tc - 1) % 2].inc(sc.dma_start(
                                out=ap["op_send"].rearrange("(m p) t -> p m t", p=128)[
                                    :, 8 * ((tc - 1) // 8):8 * ((tc - 1) // 8) + 8,
                                    (512 * (tc - 1)) % 4096:(512 * (tc - 1)) % 4096 + 512],
                                in_=ost[(tc - 1) % 2][:, :, :]), 16)
                        for n in range(2):
                            fn = 16 * tc + n
                            if fn >= 2:
                                sc.wait_ge(EVS.h, hcf_(fn - 2))
                            src = bass.AP(_th["cd_d"].ap().tensor,
                                          n * SEQ + 512 * tc, [[0, 128], [1, 512]])
                            LCC[n % 2].inc(sc.dma_start(out=Cbc[n % 2][:, :], in_=src), 16)
                        for m in range(4):
                            sc.wait_ge(MMP.h, mm_dt(tc, m))
                            if m == 0 and tc >= 1:
                                sc.wait_ge(EVS.h, ev_deltaT(tc - 1, 3))
                            EAS.inc(sc.activation(rbuf[:, m, :], psd2[m % 2][:, :],
                                                  AF.Relu, bias=C["bdt2"][:, m:m + 1]))
                            EAS.inc(sc.activation(abuf[:, m, :], psd2[m % 2][:, :],
                                                  AF.Abs, bias=C["bdt2"][:, m:m + 1]))
                        for m in range(4):
                            EAS.inc(sc.activation(abuf[:, m, :], abuf[:, m, :],
                                                  AF.Exp, scale=C["fconst"][:, 2:3]))
                        for m in range(4):
                            EAS.inc(sc.activation(abuf[:, m, :], abuf[:, m, :],
                                                  AF.Ln, bias=C["fconst"][:, 3:4]))
                        for n in range(16):
                            fn = 16 * tc + n
                            for g in range(4):
                                if n == 0:
                                    sc.wait_ge(EVS.h, ev_deltaT(tc, g))
                                if g == 0 and fn >= 2:
                                    sc.wait_ge(EVS.h, scf_(fn - 2))
                                EAS.inc(sc.activation(dab[n % 2][:, g, :],
                                                      deltaT[:, g, :], AF.Exp,
                                                      scale=C["A"][:, g, n:n + 1]))
                            if 1 <= n < 15:
                                sc.wait_ge(EVS.h, hcf_(16 * tc + n - 1))
                                src = bass.AP(_th["cd_d"].ap().tensor,
                                              (n + 1) * SEQ + 512 * tc,
                                              [[0, 128], [1, 512]])
                                LCC[(n + 1) % 2].inc(sc.dma_start(out=Cbc[(n + 1) % 2][:, :],
                                                                  in_=src), 16)
                    sc.wait_ge(EVS.h, ev_evac(NTC - 1, 7))
                    STO[(NTC - 1) % 2].inc(sc.dma_start(
                        out=ap["op_send"].rearrange("(m p) t -> p m t", p=128)[
                            :, 8 * ((NTC - 1) // 8):8 * ((NTC - 1) // 8) + 8,
                            (512 * (NTC - 1)) % 4096:(512 * (NTC - 1)) % 4096 + 512],
                        in_=ost[(NTC - 1) % 2][:, :, :]), 16)
                    sc.wait_ge(STO[0].h, 16 * 8)
                    sc.wait_ge(STO[1].h, 16 * 8)

                @blk.tensor
                def _(t_):
                    t_.wait_ge(ldm.h, 64)
                    for tc in range(NTC):
                        b = tc % 2
                        for m in range(4):
                            q = 4 * tc + m
                            if m == 0:
                                t_.wait_ge(EVS.h, ev_dtrb(tc))
                            if q >= 2:
                                t_.wait_ge(EAS.h, ea_abs((q - 2) // 4, (q - 2) % 4))
                            MMP.inc(t_.matmul(psd2[m % 2][:, :],
                                              wdtcb[0:64, 128 * m:128 * m + 128],
                                              dtrb[b][0:64, :],
                                              start=True, stop=True))
                        for g in range(4):
                            if g == 0:
                                t_.wait_ge(LXH[b].h, 16 * (tc // 2 + 1))
                                if tc >= 1:
                                    t_.wait_ge(EVS.h, ev_ybf(tc - 1))
                            MMP.inc(t_.matmul(psy[g][:, :], C["diagd"][:, g, :],
                                              xhcb[b][:, g, :],
                                              start=True, stop=False))
                        for n in range(16):
                            for g in range(4):
                                if g == 0:
                                    t_.wait_ge(EVS.h, hc_done(tc, n))
                                MMP.inc(t_.matmul(psy[g][:, :], C["ident"][:, :],
                                                  hcb[n % 2][:, g, :],
                                                  start=False, stop=(n == 15)))
                        for m8 in range(8):
                            r = 8 * tc + m8
                            if m8 == 0:
                                t_.wait_ge(EVS.h, ev_ybf(tc))
                                t_.wait_ge(LZC[b].h, 16 * (tc // 2 + 1))
                            if r >= 2:
                                t_.wait_ge(EVS.h, ev_evac((r - 2) // 8, (r - 2) % 8))
                            for k in range(8):
                                rhs = ybfb[b][:, k, :] if k < 4 else zcb[b][:, k - 4, :]
                                ins = t_.matmul(pso[m8 % 2][:, :],
                                                woutb[:, k, 128 * m8:128 * m8 + 128],
                                                rhs, start=(k == 0), stop=(k == 7))
                            MMP.inc(ins)

                @blk.vector
                def _(v):
                    for tc in range(NTC):
                        b = tc % 2
                        if tc >= 2:
                            v.wait_ge(MMP.h, mm_dt(tc - 2, 3))
                        EVS.inc(v.tensor_copy(dtrb[b][0:64, :],
                                              dtBC[0:64, 512 * tc:512 * tc + 512]))
                        for m in range(4):
                            v.wait_ge(EAS.h, ea_ln(tc, m))
                            EVS.inc(v.tensor_tensor(deltaT[:, m, :], rbuf[:, m, :],
                                                    abuf[:, m, :], ALU.add))
                        for g in range(4):
                            if g == 0:
                                v.wait_ge(LXH[b].h, 16 * (tc // 2 + 1))
                            EVS.inc(v.tensor_tensor(dub[:, g, :], deltaT[:, g, :],
                                                    xhcb[b][:, g, :], ALU.mult))
                        for n in range(16):
                            fn = 16 * tc + n
                            for g in range(4):
                                if g == 0:
                                    v.wait_ge(GPW.h, fn + 1)
                                    v.wait_ge(EAS.h, ea_da(tc, n, 3))
                                EVS.inc(v.tensor_tensor_scan(
                                    hng[n % 4][:, g, :], dab[n % 2][:, g, :],
                                    wb[n % 2][:, g, :],
                                    0.0 if tc == 0 else carry[:, 4 * n + g:4 * n + g + 1],
                                    ALU.mult, ALU.add))
                            if n >= 1:
                                j = n - 1
                                fj = 16 * tc + j
                                v.wait_ge(LCC[j % 2].h, 16 * (8 * tc + j // 2 + 1))
                                if fj >= 2:
                                    v.wait_ge(MMP.h,
                                              mm_ya((fj - 2) // 16, (fj - 2) % 16, 3))
                                EVS.inc(v.tensor_tensor(
                                    hcb[j % 2][:, :, :], hng[j % 4][:, :, :],
                                    Cbc[j % 2][:, :].rearrange(
                                        "p (a t) -> p a t", a=1).to_broadcast([128, 4, 512]),
                                    ALU.mult))
                                EVS.inc(v.tensor_copy(
                                    carry[:, 4 * j:4 * j + 4],
                                    r2(hng[j % 4][:, :, 511:512])))
                        v.wait_ge(LCC[15 % 2].h, 16 * (8 * tc + 8))
                        v.wait_ge(MMP.h, mm_ya(tc, 13, 3))
                        EVS.inc(v.tensor_tensor(
                            hcb[15 % 2][:, :, :], hng[15 % 4][:, :, :],
                            Cbc[15 % 2][:, :].rearrange(
                                "p (a t) -> p a t", a=1).to_broadcast([128, 4, 512]),
                            ALU.mult))
                        EVS.inc(v.tensor_copy(carry[:, 60:64],
                                              r2(hng[15 % 4][:, :, 511:512])))
                        for g in range(4):
                            if g == 0:
                                v.wait_ge(MMP.h, mm_ya(tc, 15, 3))
                            EVS.inc(v.tensor_copy(ybfb[b][:, g, :], psy[g][:, :]))
                        for m8 in range(8):
                            v.wait_ge(MMP.h, mm_op(tc, m8))
                            if m8 == 0 and tc >= 2:
                                v.wait_ge(STO[b].h, 16 * ((tc - 2) // 2 + 1))
                            EVS.inc(v.tensor_copy(ost[b][:, m8, :], pso[m8 % 2][:, :]))


            with nc.Block() as blk:
                @blk.gpsimd
                def _(g):
                    g.wait_ge(STO[0].h, 16 * 8)
                    g.wait_ge(STO[1].h, 16 * 8)
                    ccM3.inc(g.collective_compute(
                        "AllGather", ALU.bypass, replica_groups=PAIRS,
                        ins=[ap["op_send"][1024:2048, :]],
                        outs=[ap["op_pag"][2048:4096, :]]))

        with ExitStack() as c15:
            la = [sbuf(c15, "m15a%d" % i, [128, TCOR], BF16) for i in range(2)]
            lb = [sbuf(c15, "m15b%d" % i, [128, TCOR], BF16) for i in range(2)]
            mos = [sbuf(c15, "m15m%d" % i, [128, TCOR], BF16) for i in range(2)]
            GI15 = [Sem(nc, c15, "m15g%d" % i) for i in range(2)]
            ST15 = [Sem(nc, c15, "m15s%d" % i) for i in range(2)]
            EV15 = Sem(nc, c15, "m15e")
            with nc.Block() as blk:
                @blk.gpsimd
                def _(g):
                    g.wait_ge(ccM3.h, 2)
                    rb = g.to_reg(4 * D - 1)
                    for m in range(8):
                        b = m % 2
                        if m >= 2:
                            g.wait_ge(EV15.h, m - 1)
                        GI15[b].inc(g.indirect_dma_start(
                            out=la[b][:, :], out_offset=None, in_=ap["op_pag"],
                            in_offset=bass.IndirectOffsetOnAxis(
                                C["mo_rows"][:, 2 * m:2 * m + 1], 0),
                            bounds_check=rb, oob_is_err=False), 16)
                        GI15[b].inc(g.indirect_dma_start(
                            out=lb[b][:, :], out_offset=None, in_=ap["op_pag"],
                            in_offset=bass.IndirectOffsetOnAxis(
                                C["mo_rows"][:, 2 * m + 1:2 * m + 2], 0),
                            bounds_check=rb, oob_is_err=False), 16)

                @blk.vector
                def _(v):
                    for m in range(8):
                        b = m % 2
                        v.wait_ge(GI15[b].h, 32 * (m // 2 + 1))
                        if m >= 2:
                            v.wait_ge(ST15[b].h, 16 * (m // 2))
                        EV15.inc(v.scalar_tensor_tensor(
                            mos[b][:, :], la[b][:, :], C["bout"][:, m:m + 1],
                            lb[b][:, :], op0=ALU.add, op1=ALU.add))

                @blk.sync
                def _(s):
                    for m in range(8):
                        s.wait_ge(EV15.h, m + 1)
                        ST15[m % 2].inc(s.dma_start(
                            out=ap["mo_d"][128 * m:128 * m + 128, :],
                            in_=mos[m % 2][:, :]), 16)
                    s.wait_ge(ST15[0].h, 16 * 4)
                    s.wait_ge(ST15[1].h, 16 * 4)

        with ExitStack() as c16:
            mt = [sbuf(c16, "m16m%d" % i, [128, D], BF16) for i in range(2)]
            x1b = [sbuf(c16, "m16a%d" % i, [128, D], F32) for i in range(2)]
            x2b = [sbuf(c16, "m16b%d" % i, [128, D], F32) for i in range(2)]
            LT16 = [Sem(nc, c16, "m16l%d" % i) for i in range(2)]
            LX16 = [Sem(nc, c16, "m16x%d" % i) for i in range(2)]
            ST16 = [Sem(nc, c16, "m16s%d" % i) for i in range(2)]
            EV16 = Sem(nc, c16, "m16e")
            with nc.Block() as blk:
                @blk.sync
                def _(s):
                    for t in range(NT):
                        b = t % 2
                        if t >= 2:
                            s.wait_ge(EV16.h, t - 1)
                        LT16[b].inc(s.dma_start(
                            out=mt[b][:, :],
                            in_=ap["mo_d"][0:D, 128 * t:128 * t + 128],
                            transpose=True), 16)
                        LX16[b].inc(s.dma_start(
                            out=x1b[b][:, :],
                            in_=ap["x1_d"][128 * t:128 * t + 128, :]), 16)

                @blk.vector
                def _(v):
                    for t in range(NT):
                        b = t % 2
                        v.wait_ge(LT16[b].h, 16 * (t // 2 + 1))
                        v.wait_ge(LX16[b].h, 16 * (t // 2 + 1))
                        if t >= 2:
                            v.wait_ge(ST16[b].h, 16 * ((t - 2) // 2 + 1))
                        EV16.inc(v.tensor_tensor(x2b[b][:, :], x1b[b][:, :],
                                                 mt[b][:, :], ALU.add))

                @blk.scalar
                def _(sc):
                    for t in range(NT):
                        sc.wait_ge(EV16.h, t + 1)
                        ST16[t % 2].inc(sc.dma_start(
                            out=ap["x2_d"][128 * t:128 * t + 128, :],
                            in_=x2b[t % 2][:, :]), 16)
                    sc.wait_ge(ST16[0].h, 16 * (NT // 2))
                    sc.wait_ge(ST16[1].h, 16 * (NT // 2))
                    mdone.inc(sc.activation(x2b[0][0:1, 0:1], x2b[0][0:1, 0:1],
                                            AF.Copy))
        mctx.close()

    f1done = Sem(nc, pctx, "f1done")
    emit_ffn("f1", ap["x_my"], ap["x1_d"], w1s, w2s, f1done)
    f1ctx.close()

    if _UPTO >= 9:
        x2done = Sem(nc, pctx, "x2done")
        emit_mamba(x2done)
        f2ctx = ExitStack()
        w1s2 = sbuf(f2ctx, "w1s2", [128, 8, H], BF16)
        w2s2 = sbuf(f2ctx, "w2s2", [128, 32, D], BF16)
        ldw = Sem(nc, f2ctx, "ldw2")
        with nc.Block() as blk:
            @blk.gpsimd
            def _(g):
                g.wait_ge(x2done.h, 1)
                ldw.inc(g.dma_start(out=w1s2[:, :, :], in_=ap["wbf1_d"]), 16)
                ldw.inc(g.dma_start(out=w2s2[:, :, :], in_=ap["wbf2_d"]), 16)
                g.wait_ge(ldw.h, 32)
        f2done = Sem(nc, pctx, "f2done")
        emit_ffn("f2", ap["x2_d"], ap["y_out"], w1s2, w2s2, f2done, xdep=(x2done, 1))
        f2ctx.close()
    if _DEBUG:
        dbf = Sem(nc, pctx, "dbf")
        with nc.Block() as blk:
            @blk.sync
            def _(s):
                s.wait_ge(f1done.h, 1)
                for src, dst in [("lnx_my", "dbg_lnx"), ("totd", "dbg_tot"),
                                 ("cnt_all", "dbg_cnt"), ("pq_d", "dbg_pq"),
                                 ("ldisp", "dbg_ldisp"), ("ddisp", "dbg_ddisp"),
                                 ("eo_my", "dbg_eo"), ("eo_ret", "dbg_eor")]:
                    dbf.inc(s.dma_start(out=ap[dst], in_=ap[src]), 16)
                s.wait_ge(dbf.h, dbf.n)
    if _UPTO >= 9:
        pass
    else:
        with ExitStack() as cd:
            tb = sbuf(cd, "devcp", [128, D], F32)
            LDz = Sem(nc, cd, "ldz")
            STz = Sem(nc, cd, "stz")
            with nc.Block() as blk:
                @blk.sync
                def _(s):
                    s.wait_ge(f1done.h, 1)
                    for t in range(NT):
                        if t >= 1:
                            s.wait_ge(STz.h, 16 * t)
                        LDz.inc(s.dma_start(out=tb[:, :],
                                            in_=ap["x1_d"][128 * t:128 * t + 128, :]), 16)
                        s.wait_ge(LDz.h, 16 * (t + 1))
                        STz.inc(s.dma_start(out=ap["y_out"][128 * t:128 * t + 128, :],
                                            in_=tb[:, :]), 16)
                    s.wait_ge(STz.h, 16 * NT)

    pctx.close()
    return nc



def _prep_core(c, I):
    b, k = c // 2, c % 2
    xh = np.arange(512 * k, 512 * k + 512)
    f32 = np.float32
    bf = ml_dtypes.bfloat16
    d = {}
    X = np.asarray(I["L0_attn_x"], f32).reshape(32768, D)
    d["x_my"] = np.ascontiguousarray(X[4096 * c:4096 * (c + 1)])
    d["fc1wT"] = np.ascontiguousarray(np.asarray(I["L0_ffn_fc1_w"], f32)[c].T)
    d["fc2w"] = np.ascontiguousarray(np.asarray(I["L0_ffn_fc2_w"], f32)[c])
    d["b1t"] = np.ascontiguousarray(np.asarray(I["L0_ffn_fc1_b"], f32)[c, 0].reshape(32, 128).T)
    d["b2"] = np.ascontiguousarray(np.asarray(I["L0_ffn_fc2_b"], f32)[c, 0].reshape(1, D))
    d["gwT"] = np.ascontiguousarray(np.asarray(I["L0_ffn_gate_w"], f32).T)
    cols = np.concatenate([xh, 1024 + xh])
    d["winc"] = np.ascontiguousarray(np.asarray(I["L0_attn_W_in"], f32)[:, cols])
    d["woutr"] = np.ascontiguousarray(np.asarray(I["L0_attn_W_out"], f32)[cols, :])
    d["wxp"] = np.ascontiguousarray(np.asarray(I["L0_attn_W_xproj"], f32)[xh, :])
    d["wdtc"] = np.ascontiguousarray(np.asarray(I["L0_attn_W_dt"], f32)[:, xh])
    d["bdt"] = np.ascontiguousarray(np.asarray(I["L0_attn_b_dt"], f32)[xh].reshape(4, 128).T)
    d["alog"] = np.ascontiguousarray(
        np.asarray(I["L0_attn_A_log"], f32)[xh].reshape(4, 128, 16).transpose(1, 0, 2))
    d["dvec"] = np.ascontiguousarray(np.asarray(I["L0_attn_D"], f32)[xh].reshape(4, 128).T)
    ckx = np.asarray(I["L0_attn_conv_x_k"], f32)[:, 0, :][:, xh]
    ckz = np.asarray(I["L0_attn_conv_z_k"], f32)[:, 0, :][:, xh]
    ck = np.zeros((128, 8, 4), f32)
    ck[:, 0:4, :] = ckx.T.reshape(4, 128, 4).transpose(1, 0, 2)
    ck[:, 4:8, :] = ckz.T.reshape(4, 128, 4).transpose(1, 0, 2)
    d["ck"] = ck
    d["bout"] = np.ascontiguousarray(np.asarray(I["L0_attn_b_out"], f32).reshape(8, 128).T)
    d["ident"] = np.eye(128, dtype=bf)
    d["utri"] = np.triu(np.ones((128, 128), bf))
    d["ltri"] = np.tril(np.ones((128, 128), bf), -1)
    d["onesm"] = np.ones((128, 128), bf)
    d["iotaE"] = np.broadcast_to(np.arange(E, dtype=f32), (128, NT, E)).copy()
    d["prefv"] = (np.arange(8) < c).astype(f32).reshape(8, 1)
    d["ones8"] = np.ones((8, 1), f32)
    p = np.arange(128)
    d["mam_rows"] = (b * 8192 + (np.arange(64) * 128)[None, :] + p[:, None]).astype(np.int32)
    xr = np.zeros((128, 4), np.int32)
    xr[:, 0] = (2 * b) * 256 + p
    xr[:, 1] = (2 * b + 1) * 256 + p
    xr[:, 2] = (2 * b) * 256 + 128 + p
    xr[:, 3] = (2 * b + 1) * 256 + 128 + p
    d["xdb_rows"] = xr
    mo = np.zeros((128, 16), np.int32)
    for m in range(8):
        for src in range(2):
            mo[:, 2 * m + src] = 2048 * k + 1024 * src + 128 * m + p
    d["mo_rows"] = mo
    dg = np.zeros((128, 4, 128), bf)
    for g_ in range(4):
        dg[np.arange(128), g_, np.arange(128)] = d["dvec"][:, g_].astype(bf)
    d["diagd"] = dg
    m01 = np.zeros((128, 2), f32)
    m01[:, 0] = 1.0 if k == 0 else 0.0
    m01[:, 1] = 1.0 if k == 1 else 0.0
    d["m01"] = m01
    fc = np.zeros((128, 8), f32)
    fc[:, 0] = 1e-5
    fc[:, 1] = -0.5
    fc[:, 2] = -1.0
    fc[:, 3] = 1.0
    fc[:, 4] = 0.5
    fc[:, 6] = 8192.0
    d["fconst"] = fc
    return d


_last_results = None


def kernel(**inputs):
    global _last_results
    nc = build_nc()
    in_maps = [_prep_core(c, inputs) for c in range(NCORE)]
    res = run_bass_kernel_spmd(nc, in_maps, core_ids=list(range(NCORE)))
    _last_results = res.results
    y = np.concatenate([np.asarray(res.results[c]["y_out"]) for c in range(NCORE)], axis=0)
    return y.reshape(4, 8192, 1024).astype(np.float32)
